# version 04
# baseline (speedup 1.0000x reference)
import numpy as np
import ml_dtypes
from contextlib import ExitStack

import concourse.bass as bass
import concourse.mybir as mybir
from concourse.bass_utils import run_bass_kernel_spmd

F32 = mybir.dt.float32
BF16 = mybir.dt.bfloat16
I32 = mybir.dt.int32
AF = mybir.ActivationFunctionType
ALU = mybir.AluOpType
AX = mybir.AxisListType

D = 4096
KV = 1024
NH = 32
NG = 8
TOK = 1024
SEQ = 2048
IC = 86
DC = 32
INV4096 = 1.0 / 4096.0
INV_SQRT_HD = float(1.0 / np.sqrt(128.0))
EPS = 1e-5

DEBUG_DUMPS = False


def build_nc(debug=False):
    nc = bass.Bass("TRN2", target_bir_lowering=False)

    xo_t = nc.dram_tensor("xo_t", [DC, 128, TOK], F32, kind="ExternalInput")
    xr_t = nc.dram_tensor("xr_t", [DC, 128, TOK], F32, kind="ExternalInput")
    wq_t = nc.dram_tensor("wq_t", [NH, 128, DC * 128], F32, kind="ExternalInput")
    wk_t = nc.dram_tensor("wk_t", [NG, 128, DC * 128], F32, kind="ExternalInput")
    wv_t = nc.dram_tensor("wv_t", [DC, 128, KV], F32, kind="ExternalInput")
    wo_t = nc.dram_tensor("wo_t", [DC, 128, DC * 128], F32, kind="ExternalInput")
    wg_t = nc.dram_tensor("wg_t", [IC, 128, DC * 128], I32, kind="ExternalInput")
    wu_t = nc.dram_tensor("wu_t", [IC, 128, DC * 128], I32, kind="ExternalInput")
    wd_t = nc.dram_tensor("wd_t", [8, IC, 128, 512], I32, kind="ExternalInput")
    qsc_r = nc.dram_tensor("qsc_r", [1, D], F32, kind="ExternalInput")
    ksc_r = nc.dram_tensor("ksc_r", [1, KV], F32, kind="ExternalInput")
    vsc_r = nc.dram_tensor("vsc_r", [1, KV], F32, kind="ExternalInput")
    osc_r = nc.dram_tensor("osc_r", [1, D], F32, kind="ExternalInput")
    gs_d = nc.dram_tensor("gs_d", [1, 1], F32, kind="ExternalInput")
    us_d = nc.dram_tensor("us_d", [1, 1], F32, kind="ExternalInput")
    ds_d = nc.dram_tensor("ds_d", [1, 1], F32, kind="ExternalInput")
    out = nc.dram_tensor("out", [DC, 128, TOK], F32, kind="ExternalOutput")

    skind = "ExternalOutput" if debug else "Internal"
    k_s = nc.dram_tensor("k_s", [NG, 128, SEQ], BF16, kind=skind)
    v_s = nc.dram_tensor("v_s", [16, 128, KV], BF16, kind=skind)
    if debug:
        dbg_q = nc.dram_tensor("dbg_q", [128, 33 * TOK], BF16, kind="ExternalOutput")
        dbg_x1 = nc.dram_tensor("dbg_x1", [128, DC * TOK], BF16, kind="ExternalOutput")

    c_ones_col = nc.inline_tensor(np.ones((128, 1), dtype=ml_dtypes.bfloat16), name="c_ones_col")
    c_negones = nc.inline_tensor(-np.ones((1, 128), dtype=ml_dtypes.bfloat16), name="c_negones")
    c_ones_rowf = nc.inline_tensor(np.ones((1, 128), dtype=np.float32), name="c_ones_rowf")
    c_onef = nc.inline_tensor(np.ones((1, 1), dtype=np.float32), name="c_onef")

    with ExitStack() as outer:
        E = outer.enter_context
        ones_col = E(nc.sbuf_tensor("ones_col", [128, 1], BF16))
        negones = E(nc.sbuf_tensor("negones", [1, 128], BF16))
        ones_rowf = E(nc.sbuf_tensor("ones_rowf", [1, 128], F32))
        onef = E(nc.sbuf_tensor("onef", [1, 1], F32))
        gs_sb = E(nc.sbuf_tensor("gs_sb", [1, 1], F32))
        us_sb = E(nc.sbuf_tensor("us_sb", [1, 1], F32))
        ds_sb = E(nc.sbuf_tensor("ds_sb", [1, 1], F32))
        usds_row = E(nc.sbuf_tensor("usds_row", [1, 1], F32))
        qsc_vec = E(nc.sbuf_tensor("qsc_vec", [128, NH], F32))
        ksc_vec = E(nc.sbuf_tensor("ksc_vec", [128, NG], F32))
        osc_vec = E(nc.sbuf_tensor("osc_vec", [128, DC], F32))
        gs_vec = E(nc.sbuf_tensor("gs_vec", [128, 1], F32))
        usds_vec = E(nc.sbuf_tensor("usds_vec", [128, 1], F32))
        srow_sb = E(nc.sbuf_tensor("srow_sb", [1, 2 * 128], F32))
        scol_tmp = E(nc.sbuf_tensor("scol_tmp", [128, 2], F32))
        kvec_sb = E(nc.sbuf_tensor("kvec_sb", [128, 2], F32))
        qvec_sb = E(nc.sbuf_tensor("qvec_sb", [128, 2], F32))
        ovec_sb = E(nc.sbuf_tensor("ovec_sb", [128, 2], F32))
        Mk2_sb = E(nc.sbuf_tensor("Mk2_sb", [1, NG], F32))
        nqx_sb = E(nc.sbuf_tensor("nqx_sb", [1, 2], F32))
        Mk2t_sb = E(nc.sbuf_tensor("Mk2t_sb", [1, 4], F32))
        cb_c = E(nc.sbuf_tensor("cb_c", [1, 2], F32))
        cb_n = E(nc.sbuf_tensor("cb_n", [1, 2], F32))
        cbias = E(nc.sbuf_tensor("cbias", [128, NH], F32))
        scr_a = E(nc.sbuf_tensor("scr_a", [128, 2048], BF16))
        scr_b = E(nc.sbuf_tensor("scr_b", [128, 1024], F32))

        x_sb = E(nc.sbuf_tensor("x_sb", [128, DC * TOK], BF16, side="right"))

        def S(name):
            return E(nc.semaphore(name))

        def S2(name):
            return [E(nc.semaphore(name + "0")), E(nc.semaphore(name + "1"))]

        s0_c = S("s0_c"); s0_x = S2("s0_x"); s0_pe = S("s0_pe"); s0_ae = S("s0_ae"); s0_dv = S("s0_dv")
        k_w = S2("k_w"); k_abs = S("k_abs"); k_spe = S("k_spe"); k_tpe = S("k_tpe")
        k_ev1 = S("k_ev1"); k_ev2 = S("k_ev2"); k_dv = S("k_dv"); k_pe = S("k_pe")
        k_ev = S("k_ev"); k_st = S2("k_st"); k_sq = S("k_sq"); k_nkpe = S("k_nkpe"); k_dvnk = S("k_dvnk"); k_dvnk2 = S("k_dvnk2")
        v_w = S2("v_w"); v_sgn = S("v_sgn"); v_svpe = S("v_svpe"); v_sve = S("v_sve")
        v_dvr = S("v_dvr"); v_bpe = S("v_bpe"); v_bev = S("v_bev"); v_pe = S("v_pe")
        v_ev = S("v_ev"); v_st = S2("v_st")
        q_w = S2("q_w"); q_abs = S("q_abs"); q_spe = S("q_spe"); q_tpe = S("q_tpe")
        q_ev1 = S("q_ev1"); q_ev2 = S("q_ev2"); q_dv = S("q_dv"); q_pe = S("q_pe")
        q_ev = S("q_ev"); q_sq = S("q_sq"); q_nqpe = S("q_nqpe"); q_cev = S("q_cev")
        a_k = S2("a_k"); a_v = S2("a_v"); a_pe = S("a_pe"); a_pt = S("a_pt"); a_pu = S("a_pu")
        a_pr = S("a_pr"); a_rr = S("a_rr"); a_pb = S("a_pb"); a_bce = S("a_bce"); a_at = S("a_at")
        o_w = S2("o_w"); o_abs = S("o_abs"); o_spe = S("o_spe"); o_tpe = S("o_tpe")
        o_ev1 = S("o_ev1"); o_ev2 = S("o_ev2"); o_dv = S("o_dv"); o_pe = S("o_pe"); o_ev = S("o_ev")
        l1_sq = S("l1_sq"); l1_pe = S("l1_pe"); l1_ev = S("l1_ev"); l1_dv = S("l1_dv")
        l1_bpe = S("l1_bpe"); l1_bev = S("l1_bev"); l1_n = S("l1_n")
        f_w = S2("f_w"); f_mm = S("f_mm"); f_g = S("f_g"); f_h = S("f_h")
        d_w = [S("d_w0"), S("d_w1"), S("d_w2"), S("d_w3")]; d_mm = S("d_mm"); d_ev = S("d_ev")
        l2_sq = S("l2_sq"); l2_pe = S("l2_pe"); l2_ev = S("l2_ev"); l2_dv = S("l2_dv")
        l2_bpe = S("l2_bpe"); l2_bev = S("l2_bev"); l2_n = S("l2_n"); s_out = S2("s_out")
        q_dvc = S("q_dvc"); q_cpe = S("q_cpe")
        dbg_s = S("dbg_s")

        with ExitStack() as ph:
            psum_t = [ph.enter_context(nc.psum_tensor(f"p0t{i}", [128, 1], F32)) for i in range(2)]
            qsc_row = ph.enter_context(nc.sbuf_tensor("qsc_row", [1, D], F32))
            ksc_row = ph.enter_context(nc.sbuf_tensor("ksc_row", [1, KV], F32))
            osc_row = ph.enter_context(nc.sbuf_tensor("osc_row", [1, D], F32))

            jobs = ([("q", j) for j in range(NH)] + [("k", j) for j in range(NG)]
                    + [("o", j) for j in range(DC)] + [("gs", 0), ("usds", 0)])

            with nc.Block("P0", no_gpsimd_drain=True) as block:
                @block.gpsimd
                def _(g):
                    g.dma_start(
                        x_sb[:].rearrange("p (c t) -> p c t", c=DC),
                        xo_t.rearrange("c p t -> p c t"),
                    ).then_inc(s0_x[0], 16)

                @block.sync
                def _(s):
                    for dst, src in [
                        (ones_col, c_ones_col), (negones, c_negones),
                        (ones_rowf, c_ones_rowf), (onef, c_onef),
                        (qsc_row, qsc_r), (ksc_row, ksc_r),
                        (osc_row, osc_r), (gs_sb, gs_d), (us_sb, us_d), (ds_sb, ds_d),
                    ]:
                        s.dma_start(dst[:], src[:]).then_inc(s0_c, 16)

                @block.vector
                def _(v):
                    v.wait_ge(s0_c, 160)
                    v.tensor_tensor(usds_row[:], us_sb[:], ds_sb[:], ALU.mult).then_inc(s0_dv, 1)

                @block.tensor
                def _(t):
                    t.wait_ge(s0_c, 160)
                    for cnt, (kind, j) in enumerate(jobs):
                        if cnt >= 2:
                            t.wait_ge(s0_ae, cnt - 1)
                        if kind == "q":
                            lhs = qsc_row[0:1, j * 128:(j + 1) * 128]
                            rhs = onef[:]
                        elif kind == "k":
                            lhs = ksc_row[0:1, j * 128:(j + 1) * 128]
                            rhs = onef[:]
                        elif kind == "o":
                            lhs = osc_row[0:1, j * 128:(j + 1) * 128]
                            rhs = onef[:]
                        elif kind == "gs":
                            lhs = ones_rowf[:]
                            rhs = gs_sb[:]
                        else:
                            t.wait_ge(s0_dv, 1)
                            lhs = ones_rowf[:]
                            rhs = usds_row[:]
                        t.matmul(psum_t[cnt % 2][:], lhs, rhs, start=True, stop=True).then_inc(s0_pe, 1)

                @block.scalar
                def _(sc):
                    for cnt, (kind, j) in enumerate(jobs):
                        sc.wait_ge(s0_pe, cnt + 1)
                        if kind == "q":
                            dst = qsc_vec[:, j:j + 1]
                        elif kind == "k":
                            dst = ksc_vec[:, j:j + 1]
                        elif kind == "o":
                            dst = osc_vec[:, j:j + 1]
                        elif kind == "gs":
                            dst = gs_vec[:]
                        else:
                            dst = usds_vec[:]
                        sc.activation(dst, psum_t[cnt % 2][:], AF.Copy).then_inc(s0_ae, 1)

        with ExitStack() as sc01:
            xr_sb = sc01.enter_context(nc.sbuf_tensor("xr_sb", [128, DC * TOK], BF16, side="right"))
            vsc_row = sc01.enter_context(nc.sbuf_tensor("vsc_row", [1, KV], F32))
            vrow_sb = sc01.enter_context(nc.sbuf_tensor("vrow_sb", [1, KV], F32))

            def xall(dc, sp):
                if sp < 2:
                    return x_sb[:, dc * TOK + sp * 512: dc * TOK + (sp + 1) * 512]
                return xr_sb[:, dc * TOK + (sp - 2) * 512: dc * TOK + (sp - 1) * 512]

            def xtile_all(dc, tc):
                if tc < 8:
                    return x_sb[:, dc * TOK + tc * 128: dc * TOK + (tc + 1) * 128]
                return xr_sb[:, dc * TOK + (tc - 8) * 128: dc * TOK + (tc - 7) * 128]

            with nc.Block("P0X", no_gpsimd_drain=True) as block:
                @block.gpsimd
                def _(g):
                    g.dma_start(
                        xr_sb[:].rearrange("p (c t) -> p c t", c=DC),
                        xr_t.rearrange("c p t -> p c t"),
                    ).then_inc(s0_x[1], 16)

                @block.sync
                def _(s):
                    s.dma_start(vsc_row[:], vsc_r[:]).then_inc(s0_c, 16)

            with ExitStack() as ph:
                PH = ph.enter_context
                wkf = PH(nc.sbuf_tensor("wkf", [128, 2 * DC * 128], F32, side="right"))
                wkb = PH(nc.sbuf_tensor("wkb", [128, 2 * DC * 128], BF16, side="right"))
                absb = PH(nc.sbuf_tensor("absb", [128, 2 * 512], BF16, side="right"))
                kstage = PH(nc.sbuf_tensor("kstage", [128, 2 * SEQ], BF16, side="right"))
                psum_k = [PH(nc.psum_tensor(f"pk{i}", [128, 512], F32)) for i in range(4)]
                psum_sk = PH(nc.psum_tensor("psk", [1, 128], F32))
                psum_tk = PH(nc.psum_tensor("ptk", [128, 1], F32))
                psum_nk = PH(nc.psum_tensor("pnk", [1, 512], F32))

                def kst(g):
                    return kstage[:, (g % 2) * SEQ:((g % 2) + 1) * SEQ]

                with nc.Block("P1K", no_gpsimd_drain=True) as block:
                    @block.sync
                    def _(s):
                        for g in range(NG):
                            if g >= 2:
                                s.wait_ge(k_abs, 9 * (g - 1))
                            s.dma_start(wkf[:, (g % 2) * D:((g % 2) + 1) * D],
                                        wk_t[g]).then_inc(k_w[g % 2], 16)
                            if g >= 1:
                                s.wait_ge(k_ev, 4 * g)
                                s.dma_start(k_s[g - 1], kst(g - 1)).then_inc(k_st[(g - 1) % 2], 16)
                        s.wait_ge(k_ev, 4 * NG)
                        s.dma_start(k_s[NG - 1], kst(NG - 1)).then_inc(k_st[(NG - 1) % 2], 16)
                        s.wait_ge(k_st[0], 16 * (NG // 2))
                        s.wait_ge(k_st[1], 16 * (NG // 2))

                    @block.scalar
                    def _(sc):
                        for g in range(NG):
                            p = g % 2
                            sc.wait_ge(k_w[g % 2], 16 * (g // 2 + 1))
                            if g >= 2:
                                sc.wait_ge(k_pe, 4 * (g - 1))
                            sc.activation(wkb[:, p * D:(p + 1) * D], wkf[:, p * D:(p + 1) * D],
                                          AF.Sign).then_inc(k_abs, 1)
                            for c in range(8):
                                if 8 * g + c >= 2:
                                    sc.wait_ge(k_spe, 8 * g + c - 1)
                                sc.activation(absb[:, (c % 2) * 512:((c % 2) + 1) * 512],
                                              wkf[:, p * D + c * 512: p * D + (c + 1) * 512],
                                              AF.Abs).then_inc(k_abs, 1)
                            sc.wait_ge(k_spe, 8 * (g + 1))
                            if g >= 2:
                                sc.wait_ge(k_tpe, g - 1)
                            sc.activation(srow_sb[0:1, p * 128:(p + 1) * 128], psum_sk[:], AF.Copy).then_inc(k_ev1, 1)
                            sc.wait_ge(k_tpe, g + 1)
                            if g >= 2:
                                sc.wait_ge(k_dv, 2 * (g - 1))
                            sc.activation(scol_tmp[:, p:p + 1], psum_tk[:], AF.Copy).then_inc(k_ev2, 1)
                            for j in range(4):
                                sc.wait_ge(k_dv, 2 * (g + 1))
                                sc.wait_ge(k_pe, 4 * g + j + 1)
                                if g >= 2 and j == 0:
                                    sc.wait_ge(k_st[g % 2], 16 * (g // 2))
                                sc.activation(kstage[:, p * SEQ + j * 512: p * SEQ + (j + 1) * 512],
                                              psum_k[(4 * g + j) % 4][:], AF.Copy,
                                              scale=kvec_sb[:, p:p + 1]).then_inc(k_ev, 1)
                            if g >= 1:
                                sc.wait_ge(k_nkpe, 4 * g)
                            sc.wait_ge(k_ev, 4 * (g + 1))
                            for j in range(4):
                                sc.activation(scr_a[:, j * 512:(j + 1) * 512],
                                              kstage[:, p * SEQ + j * 512: p * SEQ + (j + 1) * 512],
                                              AF.Square).then_inc(k_sq, 1)

                    @block.tensor
                    def _(t):
                        for g in range(NG):
                            p = g % 2
                            if g >= 1:
                                t.wait_ge(k_ev1, g)
                            for c in range(8):
                                t.wait_ge(k_abs, 9 * g + c + 2)
                                for jj in range(4):
                                    dc = 4 * c + jj
                                    ins = t.matmul(psum_sk[:], ones_col[:],
                                                   absb[:, (c % 2) * 512 + jj * 128:(c % 2) * 512 + (jj + 1) * 128],
                                                   start=(dc == 0), stop=(dc == DC - 1))
                                ins.then_inc(k_spe, 1)
                            t.wait_ge(k_ev1, g + 1)
                            if g >= 1:
                                t.wait_ge(k_ev2, g)
                            t.matmul(psum_tk[:], srow_sb[0:1, p * 128:(p + 1) * 128], onef[:],
                                     start=True, stop=True).then_inc(k_tpe, 1)
                            for j in range(4):
                                t.wait_ge(s0_x[0 if j < 2 else 1], 16)
                                if 4 * g + j >= 4:
                                    t.wait_ge(k_ev, 4 * g + j - 3)
                                for dc in range(DC):
                                    ins = t.matmul(psum_k[(4 * g + j) % 4][:],
                                                   wkb[:, p * D + dc * 128: p * D + (dc + 1) * 128],
                                                   xall(dc, j),
                                                   start=(dc == 0), stop=(dc == DC - 1))
                                ins.then_inc(k_pe, 1)
                            for j in range(4):
                                jj4 = 4 * g + j
                                if jj4 >= 1:
                                    t.wait_ge(k_dvnk, jj4)
                                t.wait_ge(k_sq, jj4 + 1)
                                t.matmul(psum_nk[:], ones_col[:],
                                         scr_a[:, j * 512:(j + 1) * 512],
                                         start=True, stop=True).then_inc(k_nkpe, 1)

                    @block.vector
                    def _(v):
                        for g in range(NG):
                            p = g % 2
                            v.wait_ge(k_ev2, g + 1)
                            v.tensor_scalar(scol_tmp[:, p:p + 1], scol_tmp[:, p:p + 1], INV4096, 1e-8,
                                            ALU.mult, ALU.max).then_inc(k_dv, 1)
                            v.wait_ge(k_dv, 2 * g + 1)
                            if g >= 2:
                                v.wait_ge(k_ev, 4 * (g - 1))
                            v.tensor_tensor(kvec_sb[:, p:p + 1], scol_tmp[:, p:p + 1],
                                            ksc_vec[:, g:g + 1], ALU.mult).then_inc(k_dv, 1)
                            for j in range(4):
                                jj4 = 4 * g + j
                                v.wait_ge(k_nkpe, jj4 + 1)
                                if j == 0 and g >= 1:
                                    v.wait_ge(k_dvnk2, g)
                                v.reduce_max(Mk2t_sb[0:1, j:j + 1],
                                             psum_nk[:], axis=AX.X).then_inc(k_dvnk, 1)
                            v.wait_ge(k_dvnk, 4 * (g + 1))
                            v.tensor_reduce(Mk2_sb[0:1, g:g + 1], Mk2t_sb[0:1, 0:4], axis=AX.X,
                                            op=ALU.max).then_inc(k_dvnk2, 1)

            with ExitStack() as ph:
                PH = ph.enter_context
                wvf = PH(nc.sbuf_tensor("wvf", [128, 2 * KV], F32, side="right"))
                wvh = PH(nc.sbuf_tensor("wvh", [128, DC * 512], BF16, side="right"))
                vabs = PH(nc.sbuf_tensor("vabs", [128, 2 * KV], BF16, side="right"))
                vstage = PH(nc.sbuf_tensor("vstage", [128, 2 * 512], BF16, side="right"))
                psum_v = [PH(nc.psum_tensor(f"pv{i}", [128, 512], F32)) for i in range(2)]
                psum_sv = PH(nc.psum_tensor("psv", [1, KV], F32))
                psum_vb = PH(nc.psum_tensor("pvb", [128, 1024], F32))

                def vsgn_count(i):
                    return 2 * (i + 1) if i < DC else 2 * DC + (i - DC) + 1

                with nc.Block("P1V", no_gpsimd_drain=True) as block:
                    @block.sync
                    def _(s):
                        for half in range(2):
                            for dc in range(DC):
                                i = half * DC + dc
                                if i >= 2:
                                    s.wait_ge(v_sgn, vsgn_count(i - 2))
                                s.dma_start(wvf[:, (i % 2) * KV:((i % 2) + 1) * KV],
                                            wv_t[dc]).then_inc(v_w[i % 2], 16)

                    @block.gpsimd
                    def _(g):
                        for it in range(32):
                            g.wait_ge(v_ev, it + 1)
                            half, tc = it // 16, it % 16
                            g.dma_start(v_s[tc, :, half * 512:(half + 1) * 512],
                                        vstage[:, (it % 2) * 512:((it % 2) + 1) * 512]).then_inc(v_st[it % 2], 16)
                        g.wait_ge(v_st[0], 16 * 16)
                        g.wait_ge(v_st[1], 16 * 16)

                    @block.scalar
                    def _(sc):
                        for half in range(2):
                            if half == 1:
                                sc.wait_ge(v_pe, 16)
                            for dc in range(DC):
                                i = half * DC + dc
                                sc.wait_ge(v_w[i % 2], 16 * (i // 2 + 1))
                                sc.activation(wvh[:, dc * 512:(dc + 1) * 512],
                                              wvf[:, (i % 2) * KV + half * 512:(i % 2) * KV + (half + 1) * 512],
                                              AF.Sign).then_inc(v_sgn, 1)
                                if half == 0:
                                    if dc >= 2:
                                        sc.wait_ge(v_svpe, 2 * (dc - 1))
                                    sc.activation(vabs[:, (dc % 2) * KV:((dc % 2) + 1) * KV],
                                                  wvf[:, (i % 2) * KV:((i % 2) + 1) * KV],
                                                  AF.Abs).then_inc(v_sgn, 1)
                            if half == 0:
                                sc.wait_ge(v_svpe, 2 * DC)
                                sc.activation(vrow_sb[:], psum_sv[:], AF.Copy).then_inc(v_sve, 1)
                                sc.wait_ge(v_bpe, 2)
                                sc.activation(scr_b[:, 0:512], psum_vb[:, 0:512], AF.Copy).then_inc(v_bev, 1)
                                sc.activation(scr_b[:, 512:1024], psum_vb[:, 512:1024], AF.Copy).then_inc(v_bev, 1)

                    @block.tensor
                    def _(t):
                        t.wait_ge(s0_x[0], 16)
                        t.wait_ge(s0_x[1], 16)
                        for half in range(2):
                            if half == 0:
                                for dc in range(DC):
                                    t.wait_ge(v_sgn, 2 * (dc + 1))
                                    for j in range(2):
                                        t.matmul(psum_sv[0:1, j * 512:(j + 1) * 512], ones_col[:],
                                                 vabs[:, (dc % 2) * KV + j * 512:(dc % 2) * KV + (j + 1) * 512],
                                                 start=(dc == 0), stop=(dc == DC - 1)).then_inc(v_svpe, 1)
                                t.wait_ge(v_dvr, 2)
                                for j in range(2):
                                    t.matmul(psum_vb[:, j * 512:(j + 1) * 512], ones_rowf[:],
                                             vrow_sb[0:1, j * 512:(j + 1) * 512],
                                             start=True, stop=True).then_inc(v_bpe, 1)
                            for tc in range(16):
                                it = half * 16 + tc
                                if it >= 2:
                                    t.wait_ge(v_ev, it - 1)
                                for dc in range(DC):
                                    if tc == 0:
                                        t.wait_ge(v_sgn, (2 * dc + 1) if half == 0 else (2 * DC + dc + 1))
                                    ins = t.matmul(psum_v[it % 2][:], xtile_all(dc, tc),
                                                   wvh[:, dc * 512:(dc + 1) * 512],
                                                   start=(dc == 0), stop=(dc == DC - 1))
                                ins.then_inc(v_pe, 1)

                    @block.vector
                    def _(v):
                        v.wait_ge(v_sve, 1)
                        v.wait_ge(s0_c, 176)
                        v.tensor_scalar(vrow_sb[:], vrow_sb[:], INV4096, 1e-8, ALU.mult, ALU.max).then_inc(v_dvr, 1)
                        v.wait_ge(v_dvr, 1)
                        v.tensor_tensor(vrow_sb[:], vrow_sb[:], vsc_row[:], ALU.mult).then_inc(v_dvr, 1)
                        for it in range(32):
                            half = it // 16
                            v.wait_ge(v_pe, it + 1)
                            if it == 0:
                                v.wait_ge(v_bev, 2)
                            if it >= 2:
                                v.wait_ge(v_st[it % 2], 16 * (it // 2))
                            v.tensor_tensor(vstage[:, (it % 2) * 512:((it % 2) + 1) * 512],
                                            psum_v[it % 2][:],
                                            scr_b[:, half * 512:(half + 1) * 512],
                                            ALU.mult).then_inc(v_ev, 1)

        with ExitStack() as scA:
            A = scA.enter_context(nc.sbuf_tensor("A", [128, 33 * TOK], BF16))
            Kg = scA.enter_context(nc.sbuf_tensor("Kg", [128, 2 * SEQ], BF16, side="right"))
            Vg = scA.enter_context(nc.sbuf_tensor("Vg", [128, 2 * SEQ], BF16, side="right"))

            def q_slot(h):
                return A[:, h * TOK:(h + 1) * TOK]

            def att_slot(h):
                s = 32 if h == 0 else h - 1
                return A[:, s * TOK:(s + 1) * TOK]

            with ExitStack() as ph:
                PH = ph.enter_context
                wqf = PH(nc.sbuf_tensor("wqf", [128, 2 * DC * 128], F32, side="right"))
                wqb = PH(nc.sbuf_tensor("wqb", [128, 2 * DC * 128], BF16, side="right"))
                qabsb = PH(nc.sbuf_tensor("qabsb", [128, 2 * 512], BF16, side="right"))
                psum_q = [PH(nc.psum_tensor(f"pq{i}", [128, 512], F32)) for i in range(4)]
                psum_sq = PH(nc.psum_tensor("psq", [1, 128], F32))
                psum_tq = PH(nc.psum_tensor("ptq", [128, 1], F32))
                psum_nq = PH(nc.psum_tensor("pnq", [1, TOK], F32))

                with nc.Block("P2", no_gpsimd_drain=True) as block:
                    @block.sync
                    def _(s):
                        for h in range(NH):
                            if h >= 2:
                                s.wait_ge(q_abs, 9 * (h - 1))
                            s.dma_start(wqf[:, (h % 2) * D:((h % 2) + 1) * D],
                                        wq_t[h]).then_inc(q_w[h % 2], 16)
                        for g in range(2):
                            s.dma_start(Kg[:, (g % 2) * SEQ:((g % 2) + 1) * SEQ],
                                        k_s[g]).then_inc(a_k[g % 2], 16)
                            s.dma_start(
                                Vg[:, (g % 2) * SEQ:((g % 2) + 1) * SEQ].rearrange("p (k c) -> p k c", k=16),
                                v_s[:, :, g * 128:(g + 1) * 128].rearrange("k p c -> p k c"),
                            ).then_inc(a_v[g % 2], 16)

                    @block.scalar
                    def _(sc):
                        for h in range(NH):
                            p = h % 2
                            sc.wait_ge(q_w[h % 2], 16 * (h // 2 + 1))
                            if h >= 2:
                                sc.wait_ge(q_pe, 2 * (h - 1))
                            sc.activation(wqb[:, p * D:(p + 1) * D], wqf[:, p * D:(p + 1) * D],
                                          AF.Sign).then_inc(q_abs, 1)
                            for c in range(8):
                                if 8 * h + c >= 2:
                                    sc.wait_ge(q_spe, 8 * h + c - 1)
                                sc.activation(qabsb[:, (c % 2) * 512:((c % 2) + 1) * 512],
                                              wqf[:, p * D + c * 512: p * D + (c + 1) * 512],
                                              AF.Abs).then_inc(q_abs, 1)
                            sc.wait_ge(q_spe, 8 * (h + 1))
                            if h >= 2:
                                sc.wait_ge(q_tpe, h - 1)
                            sc.activation(srow_sb[0:1, p * 128:(p + 1) * 128], psum_sq[:], AF.Copy).then_inc(q_ev1, 1)
                            sc.wait_ge(q_tpe, h + 1)
                            if h >= 2:
                                sc.wait_ge(q_dv, 2 * (h - 1))
                            sc.activation(scol_tmp[:, p:p + 1], psum_tq[:], AF.Copy).then_inc(q_ev2, 1)
                            for j in range(2):
                                sc.wait_ge(q_dv, 2 * (h + 1))
                                sc.wait_ge(q_pe, 2 * h + j + 1)
                                sc.activation(q_slot(h)[:, j * 512:(j + 1) * 512],
                                              psum_q[(2 * h + j) % 4][:], AF.Copy,
                                              scale=qvec_sb[:, p:p + 1]).then_inc(q_ev, 1)
                            if h >= 1:
                                sc.wait_ge(q_nqpe, h)
                            sc.wait_ge(q_ev, 2 * (h + 1))
                            for j in range(2):
                                sc.activation(scr_a[:, j * 512:(j + 1) * 512],
                                              q_slot(h)[:, j * 512:(j + 1) * 512],
                                              AF.Square).then_inc(q_sq, 1)
                            gg = h // 4
                            sc.wait_ge(q_dvc, 2 * h + 1)
                            if h >= 2:
                                sc.wait_ge(q_dvc, 2 * (h - 1))
                            sc.activation(cb_c[0:1, h % 2:h % 2 + 1],
                                          nqx_sb[0:1, h % 2:h % 2 + 1], AF.Sqrt,
                                          scale=Mk2_sb[0:1, gg:gg + 1]).then_inc(q_cev, 1)
                            sc.wait_ge(q_cpe, h + 1)
                            sc.activation(cbias[:, h:h + 1], psum_tq[:], AF.Copy).then_inc(q_cev, 1)

                    @block.tensor
                    def _(t):
                        for h in range(NH):
                            p = h % 2
                            if h >= 1:
                                t.wait_ge(q_ev1, h)
                            for c in range(8):
                                t.wait_ge(q_abs, 9 * h + c + 2)
                                for jj in range(4):
                                    dc = 4 * c + jj
                                    ins = t.matmul(psum_sq[:], ones_col[:],
                                                   qabsb[:, (c % 2) * 512 + jj * 128:(c % 2) * 512 + (jj + 1) * 128],
                                                   start=(dc == 0), stop=(dc == DC - 1))
                                ins.then_inc(q_spe, 1)
                            t.wait_ge(q_ev1, h + 1)
                            if h >= 1:
                                t.wait_ge(q_ev2, h)
                                t.wait_ge(q_cev, 2 * h)
                            t.matmul(psum_tq[:], srow_sb[0:1, p * 128:(p + 1) * 128], onef[:],
                                     start=True, stop=True).then_inc(q_tpe, 1)
                            for j in range(2):
                                n = 2 * h + j
                                if n >= 4:
                                    t.wait_ge(q_ev, n - 3)
                                for dc in range(DC):
                                    ins = t.matmul(psum_q[n % 4][:],
                                                   wqb[:, p * D + dc * 128: p * D + (dc + 1) * 128],
                                                   x_sb[:, dc * TOK + j * 512: dc * TOK + (j + 1) * 512],
                                                   start=(dc == 0), stop=(dc == DC - 1))
                                ins.then_inc(q_pe, 1)
                            if h >= 1:
                                t.wait_ge(q_dvc, 2 * h - 1)
                            for j in range(2):
                                t.wait_ge(q_sq, 2 * h + j + 1)
                                ins = t.matmul(psum_nq[0:1, j * 512:(j + 1) * 512], ones_col[:],
                                               scr_a[:, j * 512:(j + 1) * 512],
                                               start=True, stop=True)
                                if j == 1:
                                    ins.then_inc(q_nqpe, 1)
                            t.wait_ge(q_dvc, 2 * h + 2)
                            t.wait_ge(q_ev2, h + 1)
                            t.matmul(psum_tq[:], ones_rowf[:], cb_n[0:1, h % 2:h % 2 + 1],
                                     start=True, stop=True).then_inc(q_cpe, 1)

                    @block.vector
                    def _(v):
                        for h in range(NH):
                            p = h % 2
                            v.wait_ge(q_ev2, h + 1)
                            v.tensor_scalar(scol_tmp[:, p:p + 1], scol_tmp[:, p:p + 1], INV4096, 1e-8,
                                            ALU.mult, ALU.max).then_inc(q_dv, 1)
                            v.wait_ge(q_dv, 2 * h + 1)
                            if h >= 2:
                                v.wait_ge(q_ev, 2 * (h - 1))
                            v.scalar_tensor_tensor(qvec_sb[:, p:p + 1], scol_tmp[:, p:p + 1],
                                                   INV_SQRT_HD, qsc_vec[:, h:h + 1],
                                                   ALU.mult, ALU.mult).then_inc(q_dv, 1)
                            v.wait_ge(q_nqpe, h + 1)
                            if h >= 2:
                                v.wait_ge(q_cev, 2 * h - 3)
                            v.reduce_max(nqx_sb[0:1, h % 2:h % 2 + 1], psum_nq[:],
                                         axis=AX.X).then_inc(q_dvc, 1)
                            v.wait_ge(q_cev, 2 * h + 1)
                            if h >= 2:
                                v.wait_ge(q_cpe, h - 1)
                            v.tensor_scalar_mul(cb_n[0:1, h % 2:h % 2 + 1],
                                                cb_c[0:1, h % 2:h % 2 + 1], -1.0).then_inc(q_dvc, 1)

            if debug:
                with nc.Block("DBGQ", no_gpsimd_drain=True) as block:
                    @block.sync
                    def _(s):
                        s.dma_start(dbg_q[:], A[:]).then_inc(dbg_s, 16)
                        s.wait_ge(dbg_s, 16)

            wof = scA.enter_context(nc.sbuf_tensor("wof", [128, 2 * DC * 128], F32, side="right"))

            with ExitStack() as ph:
                PH = ph.enter_context
                rr_sb = PH(nc.sbuf_tensor("rr_sb", [1, 2 * 512], F32, side="right"))
                ptbuf = PH(nc.sbuf_tensor("ptbuf", [128, 2 * 512], BF16, side="right"))
                psum_st = [PH(nc.psum_tensor(f"pst{i}", [128, 512], F32)) for i in range(2)]
                psum_av = PH(nc.psum_tensor("pav", [128, 512], F32))
                psum_r = PH(nc.psum_tensor("pr", [1, 512], F32))
                psum_bc = PH(nc.psum_tensor("pbc", [128, 512], F32))

                with nc.Block("P3", no_gpsimd_drain=True) as block:
                    @block.sync
                    def _(s):
                        for g in range(2, NG):
                            s.wait_ge(a_pe, 128 * (g - 1))
                            s.dma_start(Kg[:, (g % 2) * SEQ:((g % 2) + 1) * SEQ], k_s[g]).then_inc(a_k[g % 2], 16)
                            s.wait_ge(a_pu, 128 * (g - 1))
                            s.dma_start(
                                Vg[:, (g % 2) * SEQ:((g % 2) + 1) * SEQ].rearrange("p (k c) -> p k c", k=16),
                                v_s[:, :, g * 128:(g + 1) * 128].rearrange("k p c -> p k c"),
                            ).then_inc(a_v[g % 2], 16)
                        for oc in range(2):
                            s.wait_ge(a_pe, 128 * (NG - 1))
                            s.dma_start(wof[:, (oc % 2) * D:((oc % 2) + 1) * D],
                                        wo_t[oc]).then_inc(o_w[oc % 2], 16)

                    @block.tensor
                    def _(t):
                        for g in range(NG):
                            gp = g % 2
                            for hh in range(4):
                                h = g * 4 + hh
                                for qs in range(2):
                                    it = h * 2 + qs
                                    for kc in range(17):
                                        if kc < 16:
                                            n = it * 16 + kc
                                            if kc == 0:
                                                t.wait_ge(a_k[g % 2], 16 * (g // 2 + 1))
                                                t.wait_ge(a_v[g % 2], 16 * (g // 2 + 1))
                                                if it >= 1:
                                                    t.wait_ge(a_rr, it)
                                                    t.wait_ge(a_at, it)
                                            if n >= 2:
                                                t.wait_ge(a_pt, n - 1)
                                            t.matmul(psum_st[kc % 2][:],
                                                     Kg[:, gp * SEQ + kc * 128: gp * SEQ + (kc + 1) * 128],
                                                     q_slot(h)[:, qs * 512:(qs + 1) * 512],
                                                     start=True, stop=True).then_inc(a_pe, 1)
                                        if kc >= 1:
                                            km = kc - 1
                                            nm = it * 16 + km
                                            t.wait_ge(a_pt, nm + 1)
                                            ins_r = t.matmul(psum_r[:], ones_col[:],
                                                             ptbuf[:, (km % 2) * 512:((km % 2) + 1) * 512],
                                                             start=(km == 0), stop=(km == 15))
                                            if km == 15:
                                                ins_r.then_inc(a_pr, 1)
                                            t.matmul(psum_av[:],
                                                     Vg[:, gp * SEQ + km * 128: gp * SEQ + (km + 1) * 128],
                                                     ptbuf[:, (km % 2) * 512:((km % 2) + 1) * 512],
                                                     start=(km == 0), stop=(km == 15)).then_inc(a_pu, 1)
                                    t.wait_ge(a_rr, it + 1)
                                    if it >= 1:
                                        t.wait_ge(a_bce, it)
                                    t.matmul(psum_bc[:], ones_rowf[:],
                                             rr_sb[0:1, (it % 2) * 512:((it % 2) + 1) * 512],
                                             start=True, stop=True).then_inc(a_pb, 1)

                    @block.scalar
                    def _(sc):
                        for it in range(64):
                            for kc in range(16):
                                n = it * 16 + kc
                                sc.wait_ge(a_pe, n + 1)
                                if n >= 2:
                                    sc.wait_ge(a_pu, n - 1)
                                sc.activation(ptbuf[:, (kc % 2) * 512:((kc % 2) + 1) * 512],
                                              psum_st[kc % 2][:], AF.Exp,
                                              bias=cbias[:, (it // 2):(it // 2) + 1]).then_inc(a_pt, 1)
                            sc.wait_ge(a_pb, it + 1)
                            if it >= 2:
                                sc.wait_ge(a_at, it - 1)
                            sc.activation(scr_b[:, (it % 2) * 512:((it % 2) + 1) * 512],
                                          psum_bc[:], AF.Copy).then_inc(a_bce, 1)

                    @block.vector
                    def _(v):
                        for it in range(64):
                            h = it // 2
                            qs = it % 2
                            v.wait_ge(a_pr, it + 1)
                            if it >= 2:
                                v.wait_ge(a_pb, it - 1)
                            v.reciprocal(rr_sb[0:1, (it % 2) * 512:((it % 2) + 1) * 512],
                                         psum_r[:]).then_inc(a_rr, 1)
                            v.wait_ge(a_bce, it + 1)
                            v.wait_ge(a_pu, 16 * (it + 1))
                            v.tensor_tensor(att_slot(h)[:, qs * 512:(qs + 1) * 512],
                                            psum_av[:],
                                            scr_b[:, (it % 2) * 512:((it % 2) + 1) * 512],
                                            ALU.mult).then_inc(a_at, 1)

            with ExitStack() as ph:
                PH = ph.enter_context
                wob = PH(nc.sbuf_tensor("wob", [128, 2 * DC * 128], BF16, side="right"))
                oabsb = PH(nc.sbuf_tensor("oabsb", [128, 2 * 512], BF16, side="right"))
                psum_o = [PH(nc.psum_tensor(f"po{i}", [128, 512], F32)) for i in range(4)]
                psum_so = PH(nc.psum_tensor("pso", [1, 128], F32))
                psum_to = PH(nc.psum_tensor("pto", [128, 1], F32))

                with nc.Block("P4a", no_gpsimd_drain=True) as block:
                    @block.sync
                    def _(s):
                        for oc in range(2, DC):
                            s.wait_ge(o_abs, 9 * (oc - 1))
                            s.dma_start(wof[:, (oc % 2) * D:((oc % 2) + 1) * D],
                                        wo_t[oc]).then_inc(o_w[oc % 2], 16)

                    @block.scalar
                    def _(sc):
                        for oc in range(DC):
                            p = oc % 2
                            sc.wait_ge(o_w[oc % 2], 16 * (oc // 2 + 1))
                            if oc >= 2:
                                sc.wait_ge(o_pe, 2 * (oc - 1))
                            sc.activation(wob[:, p * D:(p + 1) * D], wof[:, p * D:(p + 1) * D],
                                          AF.Sign).then_inc(o_abs, 1)
                            for c in range(8):
                                if 8 * oc + c >= 2:
                                    sc.wait_ge(o_spe, 8 * oc + c - 1)
                                sc.activation(oabsb[:, (c % 2) * 512:((c % 2) + 1) * 512],
                                              wof[:, p * D + c * 512: p * D + (c + 1) * 512],
                                              AF.Abs).then_inc(o_abs, 1)
                            sc.wait_ge(o_spe, 8 * (oc + 1))
                            if oc >= 2:
                                sc.wait_ge(o_tpe, oc - 1)
                            sc.activation(srow_sb[0:1, p * 128:(p + 1) * 128], psum_so[:], AF.Copy).then_inc(o_ev1, 1)
                            sc.wait_ge(o_tpe, oc + 1)
                            if oc >= 2:
                                sc.wait_ge(o_dv, 2 * (oc - 1))
                            sc.activation(scol_tmp[:, p:p + 1], psum_to[:], AF.Copy).then_inc(o_ev2, 1)

                    @block.tensor
                    def _(t):
                        for oc in range(DC):
                            p = oc % 2
                            if oc >= 1:
                                t.wait_ge(o_ev1, oc)
                            for c in range(8):
                                t.wait_ge(o_abs, 9 * oc + c + 2)
                                for jj in range(4):
                                    hc = 4 * c + jj
                                    ins = t.matmul(psum_so[:], ones_col[:],
                                                   oabsb[:, (c % 2) * 512 + jj * 128:(c % 2) * 512 + (jj + 1) * 128],
                                                   start=(hc == 0), stop=(hc == DC - 1))
                                ins.then_inc(o_spe, 1)
                            t.wait_ge(o_ev1, oc + 1)
                            if oc >= 1:
                                t.wait_ge(o_ev2, oc)
                            t.matmul(psum_to[:], srow_sb[0:1, p * 128:(p + 1) * 128], onef[:],
                                     start=True, stop=True).then_inc(o_tpe, 1)
                            for qs in range(2):
                                n = 2 * oc + qs
                                if n >= 4:
                                    t.wait_ge(o_ev, n - 3)
                                for hc in range(DC):
                                    ins = t.matmul(psum_o[n % 4][:],
                                                   wob[:, p * D + hc * 128: p * D + (hc + 1) * 128],
                                                   att_slot(hc)[:, qs * 512:(qs + 1) * 512],
                                                   start=(hc == 0), stop=(hc == DC - 1))
                                ins.then_inc(o_pe, 1)

                    @block.vector
                    def _(v):
                        for oc in range(DC):
                            p = oc % 2
                            v.wait_ge(o_ev2, oc + 1)
                            v.tensor_scalar(scol_tmp[:, p:p + 1], scol_tmp[:, p:p + 1], INV4096, 1e-8,
                                            ALU.mult, ALU.max).then_inc(o_dv, 1)
                            v.wait_ge(o_dv, 2 * oc + 1)
                            if oc >= 2:
                                v.wait_ge(o_ev, 2 * (oc - 1))
                            v.tensor_tensor(ovec_sb[:, p:p + 1], scol_tmp[:, p:p + 1],
                                            osc_vec[:, oc:oc + 1], ALU.mult).then_inc(o_dv, 1)
                            v.wait_ge(o_dv, 2 * (oc + 1))
                            for qs in range(2):
                                n = 2 * oc + qs
                                v.wait_ge(o_pe, n + 1)
                                tgt = x_sb[:, oc * TOK + qs * 512: oc * TOK + (qs + 1) * 512]
                                v.scalar_tensor_tensor(tgt, psum_o[n % 4][:], ovec_sb[:, p:p + 1], tgt,
                                                       ALU.mult, ALU.add).then_inc(o_ev, 1)

        def emit_ln(name, span_lo, span_w, to_out, sq, pe, ev, dv, bpe, bev, nrm, base,
                    store_sem=None, tbi=0, prefetch=None):
            nspan = span_w // 512
            with ExitStack() as ph:
                PH = ph.enter_context
                psum_s1 = PH(nc.psum_tensor(f"ps1{name}", [1, span_w], F32))
                psum_s2 = PH(nc.psum_tensor(f"ps2{name}", [1, span_w], F32))
                psum_ab = PH(nc.psum_tensor(f"pab{name}", [128, 2 * span_w], F32))
                ln_r1 = PH(nc.sbuf_tensor(f"lr1{name}", [1, span_w], F32))
                ln_r2 = PH(nc.sbuf_tensor(f"lr2{name}", [1, span_w], F32))
                ln_mu = ln_r1
                ln_a = PH(nc.sbuf_tensor(f"la{name}", [1, span_w], F32))
                ln_b = PH(nc.sbuf_tensor(f"lb{name}", [1, span_w], F32))
                ln_tmp = PH(nc.sbuf_tensor(f"lt{name}", [1, span_w], F32))
                ln_sd = ln_tmp
                a_bc = PH(nc.sbuf_tensor(f"abc{name}", [128, span_w], F32))
                b_bc = PH(nc.sbuf_tensor(f"bbc{name}", [128, span_w], F32))

                NDV = 7 + DC

                with nc.Block(name, no_gpsimd_drain=True) as block:
                    if prefetch is not None:
                        @block.gpsimd
                        def _(g):
                            prefetch(g)

                    @block.scalar
                    def _(sc):
                        B = base * NDV
                        if base >= 1:
                            sc.wait_ge(dv, B)
                        for dc in range(DC):
                            if dc >= 2:
                                sc.wait_ge(pe, base * DC + dc - 1)
                            sc.activation(scr_a[:, (dc % 2) * span_w:((dc % 2) + 1) * span_w],
                                          x_sb[:, dc * TOK + span_lo: dc * TOK + span_lo + span_w],
                                          AF.Square).then_inc(sq, 1)
                        sc.wait_ge(pe, base * DC + DC)
                        sc.activation(ln_r1[0:1, 0:span_w], psum_s1[:], AF.Copy).then_inc(ev, 1)
                        sc.activation(ln_r2[0:1, 0:span_w], psum_s2[:], AF.Copy).then_inc(ev, 1)
                        sc.wait_ge(dv, B + 5)
                        sc.activation(ln_sd[0:1, 0:span_w], ln_r2[0:1, 0:span_w], AF.Sqrt).then_inc(ev, 1)
                        sc.wait_ge(bpe, base * 2 * nspan + 2 * nspan)
                        for j in range(nspan):
                            sc.activation(a_bc[:, j * 512:(j + 1) * 512],
                                          psum_ab[:, j * 512:(j + 1) * 512], AF.Copy).then_inc(bev, 1)
                            sc.activation(b_bc[:, j * 512:(j + 1) * 512],
                                          psum_ab[:, span_w + j * 512: span_w + (j + 1) * 512],
                                          AF.Copy).then_inc(bev, 1)

                    @block.tensor
                    def _(t):
                        B = base * NDV
                        for dc in range(DC):
                            t.wait_ge(sq, base * DC + dc + 1)
                            for j in range(nspan):
                                t.matmul(psum_s1[0:1, j * 512:(j + 1) * 512], ones_col[:],
                                         x_sb[:, dc * TOK + span_lo + j * 512:
                                              dc * TOK + span_lo + (j + 1) * 512],
                                         start=(dc == 0), stop=(dc == DC - 1))
                                ins = t.matmul(psum_s2[0:1, j * 512:(j + 1) * 512], ones_col[:],
                                               scr_a[:, (dc % 2) * span_w + j * 512:
                                                     (dc % 2) * span_w + (j + 1) * 512],
                                               start=(dc == 0), stop=(dc == DC - 1))
                            ins.then_inc(pe, 1)
                        t.wait_ge(dv, B + 7)
                        for j in range(nspan):
                            t.matmul(psum_ab[:, j * 512:(j + 1) * 512], ones_rowf[:],
                                     ln_a[0:1, j * 512:(j + 1) * 512], start=True, stop=True).then_inc(bpe, 1)
                            t.matmul(psum_ab[:, span_w + j * 512: span_w + (j + 1) * 512], ones_rowf[:],
                                     ln_b[0:1, j * 512:(j + 1) * 512], start=True, stop=True).then_inc(bpe, 1)

                    @block.vector
                    def _(v):
                        B = base * NDV
                        v.wait_ge(ev, base * 3 + 2)
                        v.tensor_scalar_mul(ln_mu[0:1, 0:span_w], ln_r1[0:1, 0:span_w], INV4096).then_inc(dv, 1)
                        v.tensor_scalar_mul(ln_r2[0:1, 0:span_w], ln_r2[0:1, 0:span_w], INV4096).then_inc(dv, 1)
                        v.wait_ge(dv, B + 1)
                        v.tensor_tensor(ln_tmp[0:1, 0:span_w], ln_mu[0:1, 0:span_w],
                                        ln_mu[0:1, 0:span_w], ALU.mult).then_inc(dv, 1)
                        v.wait_ge(dv, B + 3)
                        v.tensor_tensor(ln_r2[0:1, 0:span_w], ln_r2[0:1, 0:span_w],
                                        ln_tmp[0:1, 0:span_w], ALU.subtract).then_inc(dv, 1)
                        v.wait_ge(dv, B + 4)
                        v.tensor_scalar_add(ln_r2[0:1, 0:span_w], ln_r2[0:1, 0:span_w], EPS).then_inc(dv, 1)
                        v.wait_ge(ev, base * 3 + 3)
                        v.reciprocal(ln_a[0:1, 0:span_w], ln_sd[0:1, 0:span_w]).then_inc(dv, 1)
                        v.wait_ge(dv, B + 6)
                        v.scalar_tensor_tensor(ln_b[0:1, 0:span_w], ln_mu[0:1, 0:span_w], -1.0,
                                               ln_a[0:1, 0:span_w], ALU.mult, ALU.mult).then_inc(dv, 1)
                        v.wait_ge(bev, base * 2 * nspan + 2 * nspan)
                        for dc in range(DC):
                            src = x_sb[:, dc * TOK + span_lo: dc * TOK + span_lo + span_w]
                            tt = scr_a[:, (dc % 2) * span_w:((dc % 2) + 1) * span_w]
                            if dc >= 2:
                                v.wait_ge(nrm, base * DC + dc - 1)
                            v.tensor_tensor(tt, src, a_bc[:, 0:span_w], ALU.mult).then_inc(dv, 1)
                            v.wait_ge(dv, B + 7 + dc + 1)
                            if to_out:
                                stg = scr_b[:, (dc % 2) * 512:((dc % 2) + 1) * 512]
                                gdc = tbi * DC + dc
                                if gdc >= 2:
                                    v.wait_ge(store_sem[gdc % 2], 16 * (gdc // 2))
                                v.tensor_tensor(stg, tt, b_bc[:, 0:span_w], ALU.add).then_inc(nrm, 1)
                            else:
                                v.tensor_tensor(src, tt, b_bc[:, 0:span_w], ALU.add).then_inc(nrm, 1)

                    if to_out:
                        @block.sync
                        def _(s):
                            for dc in range(DC):
                                s.wait_ge(nrm, base * DC + dc + 1)
                                s.dma_start(out[dc, :, span_lo:span_lo + 512],
                                            scr_b[:, (dc % 2) * 512:((dc % 2) + 1) * 512]
                                            ).then_inc(store_sem[dc % 2], 16)
                            if tbi == 1:
                                s.wait_ge(store_sem[0], 16 * DC)
                                s.wait_ge(store_sem[1], 16 * DC)

        if debug:
            with nc.Block("DBGX1", no_gpsimd_drain=True) as block:
                @block.sync
                def _(s):
                    s.dma_start(dbg_x1[:], x_sb[:]).then_inc(dbg_s, 16)
                    s.wait_ge(dbg_s, 32)

        with ExitStack() as sc5:
            wgb = sc5.enter_context(nc.sbuf_tensor("wgb", [128, 2 * DC * 128], BF16, side="right"))
            wub = sc5.enter_context(nc.sbuf_tensor("wub", [128, 2 * DC * 128], BF16, side="right"))
            wdb = sc5.enter_context(nc.sbuf_tensor("wdb", [128, 4 * 1024], BF16, side="right"))

            def ffn_prefetch(tb):
                def pf(g):
                    for ic in range(2):
                        g.dma_start(wgb[:, (ic % 2) * D:((ic % 2) + 1) * D],
                                    wg_t[ic]).then_inc(f_w[ic % 2], 16)
                        g.dma_start(wub[:, (ic % 2) * D:((ic % 2) + 1) * D],
                                    wu_t[ic]).then_inc(f_w[ic % 2], 16)
                return pf

            emit_ln("P4b", 0, TOK, False, l1_sq, l1_pe, l1_ev, l1_dv, l1_bpe, l1_bev, l1_n, 0,
                    prefetch=ffn_prefetch(0))

            for tb in range(2):
                lo = tb * 512
                schb = ExitStack()
                hbuf = schb.enter_context(nc.sbuf_tensor(f"hbuf{tb}", [128, IC * 512], BF16))
                with ExitStack() as ph:
                    PH = ph.enter_context
                    gstage = PH(nc.sbuf_tensor(f"gstage{tb}", [128, 2 * 512], F32, side="right"))
                    psum_g = [PH(nc.psum_tensor(f"pg{tb}{i}", [128, 512], F32)) for i in range(2)]
                    psum_u = [PH(nc.psum_tensor(f"pu{tb}{i}", [128, 512], F32)) for i in range(2)]

                    with nc.Block(f"P5a{tb}", no_gpsimd_drain=True) as block:
                        @block.gpsimd
                        def _(g):
                            for ic in range(2, IC):
                                g.wait_ge(f_mm, 2 * (tb * IC + ic - 1))
                                g.dma_start(wgb[:, (ic % 2) * D:((ic % 2) + 1) * D],
                                            wg_t[ic]).then_inc(f_w[ic % 2], 16)
                                g.dma_start(wub[:, (ic % 2) * D:((ic % 2) + 1) * D],
                                            wu_t[ic]).then_inc(f_w[ic % 2], 16)
                            for npf in range(4):
                                n0 = tb * 8 * 43 + npf
                                og0, icp0 = npf // 43, npf % 43
                                g.dma_start(
                                    wdb[:, (n0 % 4) * 1024:((n0 % 4) + 1) * 1024].rearrange(
                                        "p (i c) -> p i c", i=2),
                                    wd_t[og0, 2 * icp0:2 * icp0 + 2].rearrange("i p c -> p i c"),
                                ).then_inc(d_w[n0 % 4], 16)

                        @block.tensor
                        def _(t):
                            for ic in range(IC):
                                n = tb * IC + ic
                                t.wait_ge(f_w[ic % 2], 32 * (tb * 43 + ic // 2 + 1))
                                if ic >= 2:
                                    t.wait_ge(f_h, tb * IC + ic - 1)
                                p = ic % 2
                                for dc in range(DC):
                                    ins = t.matmul(psum_g[p][:],
                                                   wgb[:, p * D + dc * 128: p * D + (dc + 1) * 128],
                                                   x_sb[:, dc * TOK + lo: dc * TOK + lo + 512],
                                                   start=(dc == 0), stop=(dc == DC - 1))
                                ins.then_inc(f_mm, 1)
                                for dc in range(DC):
                                    ins = t.matmul(psum_u[p][:],
                                                   wub[:, p * D + dc * 128: p * D + (dc + 1) * 128],
                                                   x_sb[:, dc * TOK + lo: dc * TOK + lo + 512],
                                                   start=(dc == 0), stop=(dc == DC - 1))
                                ins.then_inc(f_mm, 1)

                        @block.scalar
                        def _(sc):
                            for ic in range(IC):
                                n = tb * IC + ic
                                sc.wait_ge(f_mm, 2 * n + 1)
                                sc.activation(gstage[:, (ic % 2) * 512:((ic % 2) + 1) * 512],
                                              psum_g[ic % 2][:], AF.Silu,
                                              scale=gs_vec[:]).then_inc(f_g, 1)

                        @block.vector
                        def _(v):
                            for ic in range(IC):
                                n = tb * IC + ic
                                v.wait_ge(f_g, n + 1)
                                v.wait_ge(f_mm, 2 * n + 2)
                                v.scalar_tensor_tensor(hbuf[:, ic * 512:(ic + 1) * 512],
                                                       psum_u[ic % 2][:], usds_vec[:],
                                                       gstage[:, (ic % 2) * 512:((ic % 2) + 1) * 512],
                                                       ALU.mult, ALU.mult).then_inc(f_h, 1)

                with ExitStack() as ph:
                    PH = ph.enter_context
                    psum_d = [PH(nc.psum_tensor(f"pd{tb}{i}", [128, 512], F32)) for i in range(8)]

                    with nc.Block(f"P5b{tb}", no_gpsimd_drain=True) as block:
                        @block.gpsimd
                        def _(g):
                            for og in range(8):
                                for icp in range(43):
                                    n = (tb * 8 + og) * 43 + icp
                                    if n - tb * 8 * 43 < 4:
                                        continue
                                    g.wait_ge(d_mm, n - 3)
                                    g.dma_start(
                                        wdb[:, (n % 4) * 1024:((n % 4) + 1) * 1024].rearrange(
                                            "p (i c) -> p i c", i=2),
                                        wd_t[og, 2 * icp:2 * icp + 2].rearrange("i p c -> p i c"),
                                    ).then_inc(d_w[n % 4], 16)

                        @block.tensor
                        def _(t):
                            for og in range(8):
                                gp = og % 2
                                if og >= 2:
                                    t.wait_ge(d_ev, (tb * 8 + og - 1) * 4)
                                for icp in range(43):
                                    n = (tb * 8 + og) * 43 + icp
                                    t.wait_ge(d_w[n % 4], 16 * (n // 4 + 1))
                                    for sub in range(2):
                                        ic = 2 * icp + sub
                                        for oc in range(4):
                                            ins = t.matmul(psum_d[gp * 4 + oc][:],
                                                           wdb[:, (n % 4) * 1024 + sub * 512 + oc * 128:
                                                               (n % 4) * 1024 + sub * 512 + (oc + 1) * 128],
                                                           hbuf[:, ic * 512:(ic + 1) * 512],
                                                           start=(ic == 0), stop=(ic == IC - 1))
                                    ins.then_inc(d_mm, 1)

                        @block.vector
                        def _(v):
                            for og in range(8):
                                gp = og % 2
                                v.wait_ge(d_mm, (tb * 8 + og + 1) * 43)
                                for oc in range(4):
                                    ocf = og * 4 + oc
                                    tgt = x_sb[:, ocf * TOK + lo: ocf * TOK + lo + 512]
                                    v.tensor_tensor(tgt, psum_d[gp * 4 + oc][:], tgt, ALU.add).then_inc(d_ev, 1)

                schb.close()
                emit_ln(f"P5c{tb}", lo, 512, True, l2_sq, l2_pe, l2_ev, l2_dv, l2_bpe, l2_bev,
                        l2_n, tb, store_sem=s_out, tbi=tb,
                        prefetch=(ffn_prefetch(1) if tb == 0 else None))

    return nc


def _prep_weights(inputs):
    f32 = np.float32
    wq = np.asarray(inputs["L0_attn_q_weight"], f32)
    wk = np.asarray(inputs["L0_attn_k_weight"], f32)
    wv = np.asarray(inputs["L0_attn_v_weight"], f32)
    wo = np.asarray(inputs["L0_attn_o_weight"], f32)
    wg = np.asarray(inputs["L0_ffn_gate_w"], np.int32)
    wu = np.asarray(inputs["L0_ffn_up_w"], np.int32)
    wd = np.asarray(inputs["L0_ffn_down_w"], np.int32)

    d = {}
    d["wq_t"] = np.ascontiguousarray(
        wq.reshape(NH, 128, DC, 128).transpose(0, 3, 2, 1)).reshape(NH, 128, DC * 128)
    d["wk_t"] = np.ascontiguousarray(
        wk.reshape(NG, 128, DC, 128).transpose(0, 3, 2, 1)).reshape(NG, 128, DC * 128)
    d["wv_t"] = np.ascontiguousarray(wv.T).reshape(DC, 128, KV)
    d["wo_t"] = np.ascontiguousarray(
        wo.reshape(DC, 128, DC, 128).transpose(0, 3, 2, 1)).reshape(DC, 128, DC * 128)
    d["wg_t"] = np.ascontiguousarray(
        wg.reshape(IC, 128, DC, 128).transpose(0, 3, 2, 1)).reshape(IC, 128, DC * 128)
    d["wu_t"] = np.ascontiguousarray(
        wu.reshape(IC, 128, DC, 128).transpose(0, 3, 2, 1)).reshape(IC, 128, DC * 128)
    d["wd_t"] = np.ascontiguousarray(
        wd.reshape(8, 4, 128, IC, 128).transpose(0, 3, 4, 1, 2)).reshape(8, IC, 128, 512)
    d["qsc_r"] = np.ascontiguousarray(np.asarray(inputs["L0_attn_q_scale"], f32).reshape(1, D))
    d["ksc_r"] = np.ascontiguousarray(np.asarray(inputs["L0_attn_k_scale"], f32).reshape(1, KV))
    d["vsc_r"] = np.ascontiguousarray(np.asarray(inputs["L0_attn_v_scale"], f32).reshape(1, KV))
    d["osc_r"] = np.ascontiguousarray(np.asarray(inputs["L0_attn_o_scale"], f32).reshape(1, D))
    d["gs_d"] = np.asarray(inputs["L0_ffn_gate_s"], f32).reshape(1, 1)
    d["us_d"] = np.asarray(inputs["L0_ffn_up_s"], f32).reshape(1, 1)
    d["ds_d"] = np.asarray(inputs["L0_ffn_down_s"], f32).reshape(1, 1)
    return d


def kernel(**inputs):
    x = np.asarray(inputs["L0_attn_hidden_states"], np.float32)
    shared = _prep_weights(inputs)

    nc = build_nc(debug=DEBUG_DUMPS)

    in_maps = []
    for c in range(8):
        b, hh = c // 2, c % 2
        x_own = x[b, hh * TOK:(hh + 1) * TOK, :]
        x_oth = x[b, (1 - hh) * TOK:(2 - hh) * TOK, :]
        m = dict(shared)
        m["xo_t"] = np.ascontiguousarray(x_own.T).reshape(DC, 128, TOK)
        m["xr_t"] = np.ascontiguousarray(x_oth.T).reshape(DC, 128, TOK)
        in_maps.append(m)

    res = run_bass_kernel_spmd(nc, in_maps, core_ids=list(range(8)))

    y = np.empty((4, SEQ, D), np.float32)
    for c in range(8):
        b, hh = c // 2, c % 2
        o = np.asarray(res.results[c]["out"])
        y[b, hh * TOK:(hh + 1) * TOK, :] = o.reshape(D, TOK).T
    return y


if __name__ == "__main__":
    import importlib.util
    spec = importlib.util.spec_from_file_location("_r", "/root/problem/reference.py")
    _r = importlib.util.module_from_spec(spec)
    spec.loader.exec_module(_r)
    ins = _r.setup_inputs()
    got = kernel(**{k: np.asarray(v) for k, v in ins.items()})
    want = np.asarray(_r.reference(**ins))
    err = got - want
    rv = float(np.mean(err ** 2) / np.mean(want ** 2))
    mx = float(np.abs(got.astype(ml_dtypes.bfloat16).astype(np.float32)
                      - want.astype(ml_dtypes.bfloat16).astype(np.float32)).max())
    print("residual variance ratio:", rv, " max_abs_bf16:", mx)


# version 05
# speedup vs baseline: 1.0396x; 1.0396x over previous
import numpy as np
import ml_dtypes
from contextlib import ExitStack

import concourse.bass as bass
import concourse.mybir as mybir
from concourse.bass_utils import run_bass_kernel_spmd

F32 = mybir.dt.float32
BF16 = mybir.dt.bfloat16
I32 = mybir.dt.int32
AF = mybir.ActivationFunctionType
ALU = mybir.AluOpType
AX = mybir.AxisListType

D = 4096
KV = 1024
NH = 32
NG = 8
TOK = 1024
SEQ = 2048
IC = 86
DC = 32
INV4096 = 1.0 / 4096.0
INV_SQRT_HD = float(1.0 / np.sqrt(128.0))
EPS = 1e-5

DEBUG_DUMPS = False


def build_nc(debug=False):
    nc = bass.Bass("TRN2", target_bir_lowering=False)

    xo_t = nc.dram_tensor("xo_t", [DC, 128, TOK], F32, kind="ExternalInput")
    xr_t = nc.dram_tensor("xr_t", [DC, 128, TOK], F32, kind="ExternalInput")
    wq_t = nc.dram_tensor("wq_t", [NH, 128, DC * 128], F32, kind="ExternalInput")
    wk_t = nc.dram_tensor("wk_t", [NG, 128, DC * 128], F32, kind="ExternalInput")
    wv_t = nc.dram_tensor("wv_t", [DC, 128, KV], F32, kind="ExternalInput")
    wo_t = nc.dram_tensor("wo_t", [DC, 128, DC * 128], F32, kind="ExternalInput")
    wg_t = nc.dram_tensor("wg_t", [IC, 128, DC * 128], I32, kind="ExternalInput")
    wu_t = nc.dram_tensor("wu_t", [IC, 128, DC * 128], I32, kind="ExternalInput")
    wd_t = nc.dram_tensor("wd_t", [8, IC, 128, 512], I32, kind="ExternalInput")
    qsc_r = nc.dram_tensor("qsc_r", [1, D], F32, kind="ExternalInput")
    ksc_r = nc.dram_tensor("ksc_r", [1, KV], F32, kind="ExternalInput")
    vsc_r = nc.dram_tensor("vsc_r", [1, KV], F32, kind="ExternalInput")
    osc_r = nc.dram_tensor("osc_r", [1, D], F32, kind="ExternalInput")
    gs_d = nc.dram_tensor("gs_d", [1, 1], F32, kind="ExternalInput")
    us_d = nc.dram_tensor("us_d", [1, 1], F32, kind="ExternalInput")
    ds_d = nc.dram_tensor("ds_d", [1, 1], F32, kind="ExternalInput")
    out = nc.dram_tensor("out", [DC, 128, TOK], F32, kind="ExternalOutput")

    skind = "ExternalOutput" if debug else "Internal"
    k_s = nc.dram_tensor("k_s", [NG, 128, SEQ], BF16, kind=skind)
    v_s = nc.dram_tensor("v_s", [16, 128, KV], BF16, kind=skind)
    if debug:
        dbg_q = nc.dram_tensor("dbg_q", [128, 33 * TOK], BF16, kind="ExternalOutput")
        dbg_x1 = nc.dram_tensor("dbg_x1", [128, DC * TOK], BF16, kind="ExternalOutput")

    c_ones_col = nc.inline_tensor(np.ones((128, 1), dtype=ml_dtypes.bfloat16), name="c_ones_col")
    c_negones = nc.inline_tensor(-np.ones((1, 128), dtype=ml_dtypes.bfloat16), name="c_negones")
    c_ones_rowf = nc.inline_tensor(np.ones((1, 128), dtype=np.float32), name="c_ones_rowf")
    c_onef = nc.inline_tensor(np.ones((1, 1), dtype=np.float32), name="c_onef")

    with ExitStack() as outer:
        E = outer.enter_context
        ones_col = E(nc.sbuf_tensor("ones_col", [128, 1], BF16))
        negones = E(nc.sbuf_tensor("negones", [1, 128], BF16))
        ones_rowf = E(nc.sbuf_tensor("ones_rowf", [1, 128], F32))
        onef = E(nc.sbuf_tensor("onef", [1, 1], F32))
        gs_sb = E(nc.sbuf_tensor("gs_sb", [1, 1], F32))
        us_sb = E(nc.sbuf_tensor("us_sb", [1, 1], F32))
        ds_sb = E(nc.sbuf_tensor("ds_sb", [1, 1], F32))
        usds_row = E(nc.sbuf_tensor("usds_row", [1, 1], F32))
        qsc_vec = E(nc.sbuf_tensor("qsc_vec", [128, NH], F32))
        ksc_vec = E(nc.sbuf_tensor("ksc_vec", [128, NG], F32))
        osc_vec = E(nc.sbuf_tensor("osc_vec", [128, DC], F32))
        gs_vec = E(nc.sbuf_tensor("gs_vec", [128, 1], F32))
        usds_vec = E(nc.sbuf_tensor("usds_vec", [128, 1], F32))
        srow_sb = E(nc.sbuf_tensor("srow_sb", [1, 2 * 128], F32))
        scol_tmp = E(nc.sbuf_tensor("scol_tmp", [128, 2], F32))
        kvec_sb = E(nc.sbuf_tensor("kvec_sb", [128, 2], F32))
        qvec_sb = E(nc.sbuf_tensor("qvec_sb", [128, 2], F32))
        ovec_sb = E(nc.sbuf_tensor("ovec_sb", [128, 2], F32))
        Mk2_sb = E(nc.sbuf_tensor("Mk2_sb", [1, NG], F32))
        nqx_sb = E(nc.sbuf_tensor("nqx_sb", [1, 2], F32))
        Mk2t_sb = E(nc.sbuf_tensor("Mk2t_sb", [1, 4], F32))
        cb_c = E(nc.sbuf_tensor("cb_c", [1, 2], F32))
        cb_n = E(nc.sbuf_tensor("cb_n", [1, 2], F32))
        cbias = E(nc.sbuf_tensor("cbias", [128, NH], F32))
        scr_a = E(nc.sbuf_tensor("scr_a", [128, 2048], BF16))
        scr_b = E(nc.sbuf_tensor("scr_b", [128, 1024], F32))

        x_sb = E(nc.sbuf_tensor("x_sb", [128, DC * TOK], BF16, side="right"))

        def S(name):
            return E(nc.semaphore(name))

        def S2(name):
            return [E(nc.semaphore(name + "0")), E(nc.semaphore(name + "1"))]

        s0_c = S("s0_c"); s0_x = S2("s0_x"); s0_pe = S("s0_pe"); s0_ae = S("s0_ae"); s0_dv = S("s0_dv")
        k_w = S2("k_w"); k_abs = S("k_abs"); k_spe = S("k_spe"); k_tpe = S("k_tpe")
        k_ev1 = S("k_ev1"); k_ev2 = S("k_ev2"); k_dv = S("k_dv"); k_pe = S("k_pe")
        k_ev = S("k_ev"); k_st = S2("k_st"); k_sq = S("k_sq"); k_nkpe = S("k_nkpe"); k_dvnk = S("k_dvnk"); k_dvnk2 = S("k_dvnk2")
        v_w = S2("v_w"); v_sgn = S("v_sgn"); v_svpe = S("v_svpe"); v_sve = S("v_sve")
        v_dvr = S("v_dvr"); v_bpe = S("v_bpe"); v_bev = S("v_bev"); v_pe = S("v_pe")
        v_ev = S("v_ev"); v_st = S2("v_st")
        q_w = S2("q_w"); q_abs = S("q_abs"); q_spe = S("q_spe"); q_tpe = S("q_tpe")
        q_ev1 = S("q_ev1"); q_ev2 = S("q_ev2"); q_dv = S("q_dv"); q_pe = S("q_pe")
        q_ev = S("q_ev"); q_sq = S("q_sq"); q_nqpe = S("q_nqpe"); q_cev = S("q_cev")
        a_k = S2("a_k"); a_v = S2("a_v"); a_pe = S("a_pe"); a_pt = S("a_pt"); a_pu = S("a_pu")
        a_pr = S("a_pr"); a_rr = S("a_rr"); a_pb = S("a_pb"); a_bce = S("a_bce"); a_at = S("a_at")
        o_w = S2("o_w"); o_abs = S("o_abs"); o_spe = S("o_spe"); o_tpe = S("o_tpe")
        o_ev1 = S("o_ev1"); o_ev2 = S("o_ev2"); o_dv = S("o_dv"); o_pe = S("o_pe"); o_ev = S("o_ev")
        l1_sq = S("l1_sq"); l1_pe = S("l1_pe"); l1_ev = S("l1_ev"); l1_dv = S("l1_dv")
        l1_bpe = S("l1_bpe"); l1_bev = S("l1_bev"); l1_n = S("l1_n")
        f_w = S2("f_w"); f_mm = S("f_mm"); f_g = S("f_g"); f_h = S("f_h")
        d_w = [S("d_w0"), S("d_w1"), S("d_w2"), S("d_w3")]; d_mm = S("d_mm"); d_ev = S("d_ev")
        l2_sq = S("l2_sq"); l2_pe = S("l2_pe"); l2_ev = S("l2_ev"); l2_dv = S("l2_dv")
        l2_bpe = S("l2_bpe"); l2_bev = S("l2_bev"); l2_n = S("l2_n"); s_out = S2("s_out")
        q_dvc = S("q_dvc"); q_cpe = S("q_cpe")
        dbg_s = S("dbg_s")

        with ExitStack() as ph:
            psum_t = [ph.enter_context(nc.psum_tensor(f"p0t{i}", [128, 1], F32)) for i in range(2)]
            qsc_row = ph.enter_context(nc.sbuf_tensor("qsc_row", [1, D], F32))
            ksc_row = ph.enter_context(nc.sbuf_tensor("ksc_row", [1, KV], F32))
            osc_row = ph.enter_context(nc.sbuf_tensor("osc_row", [1, D], F32))

            jobs = ([("q", j) for j in range(NH)] + [("k", j) for j in range(NG)]
                    + [("o", j) for j in range(DC)] + [("gs", 0), ("usds", 0)])

            with nc.Block("P0", no_gpsimd_drain=True) as block:
                @block.gpsimd
                def _(g):
                    g.dma_start(
                        x_sb[:].rearrange("p (c t) -> p c t", c=DC),
                        xo_t.rearrange("c p t -> p c t"),
                    ).then_inc(s0_x[0], 16)

                @block.sync
                def _(s):
                    for dst, src in [
                        (ones_col, c_ones_col), (negones, c_negones),
                        (ones_rowf, c_ones_rowf), (onef, c_onef),
                        (qsc_row, qsc_r), (ksc_row, ksc_r),
                        (osc_row, osc_r), (gs_sb, gs_d), (us_sb, us_d), (ds_sb, ds_d),
                    ]:
                        s.dma_start(dst[:], src[:]).then_inc(s0_c, 16)

                @block.vector
                def _(v):
                    v.wait_ge(s0_c, 160)
                    v.tensor_tensor(usds_row[:], us_sb[:], ds_sb[:], ALU.mult).then_inc(s0_dv, 1)

                @block.tensor
                def _(t):
                    t.wait_ge(s0_c, 160)
                    for cnt, (kind, j) in enumerate(jobs):
                        if cnt >= 2:
                            t.wait_ge(s0_ae, cnt - 1)
                        if kind == "q":
                            lhs = qsc_row[0:1, j * 128:(j + 1) * 128]
                            rhs = onef[:]
                        elif kind == "k":
                            lhs = ksc_row[0:1, j * 128:(j + 1) * 128]
                            rhs = onef[:]
                        elif kind == "o":
                            lhs = osc_row[0:1, j * 128:(j + 1) * 128]
                            rhs = onef[:]
                        elif kind == "gs":
                            lhs = ones_rowf[:]
                            rhs = gs_sb[:]
                        else:
                            t.wait_ge(s0_dv, 1)
                            lhs = ones_rowf[:]
                            rhs = usds_row[:]
                        t.matmul(psum_t[cnt % 2][:], lhs, rhs, start=True, stop=True).then_inc(s0_pe, 1)

                @block.scalar
                def _(sc):
                    for cnt, (kind, j) in enumerate(jobs):
                        sc.wait_ge(s0_pe, cnt + 1)
                        if kind == "q":
                            dst = qsc_vec[:, j:j + 1]
                        elif kind == "k":
                            dst = ksc_vec[:, j:j + 1]
                        elif kind == "o":
                            dst = osc_vec[:, j:j + 1]
                        elif kind == "gs":
                            dst = gs_vec[:]
                        else:
                            dst = usds_vec[:]
                        sc.activation(dst, psum_t[cnt % 2][:], AF.Copy).then_inc(s0_ae, 1)

        with ExitStack() as sc01:
            xr_sb = sc01.enter_context(nc.sbuf_tensor("xr_sb", [128, DC * TOK], BF16, side="right"))
            vsc_row = sc01.enter_context(nc.sbuf_tensor("vsc_row", [1, KV], F32))
            vrow_sb = sc01.enter_context(nc.sbuf_tensor("vrow_sb", [1, KV], F32))

            def xall(dc, sp):
                if sp < 2:
                    return x_sb[:, dc * TOK + sp * 512: dc * TOK + (sp + 1) * 512]
                return xr_sb[:, dc * TOK + (sp - 2) * 512: dc * TOK + (sp - 1) * 512]

            def xtile_all(dc, tc):
                if tc < 8:
                    return x_sb[:, dc * TOK + tc * 128: dc * TOK + (tc + 1) * 128]
                return xr_sb[:, dc * TOK + (tc - 8) * 128: dc * TOK + (tc - 7) * 128]

            with nc.Block("P0X", no_gpsimd_drain=True) as block:
                @block.gpsimd
                def _(g):
                    g.dma_start(
                        xr_sb[:].rearrange("p (c t) -> p c t", c=DC),
                        xr_t.rearrange("c p t -> p c t"),
                    ).then_inc(s0_x[1], 16)

                @block.sync
                def _(s):
                    s.dma_start(vsc_row[:], vsc_r[:]).then_inc(s0_c, 16)

            with ExitStack() as ph:
                PH = ph.enter_context
                wkf = PH(nc.sbuf_tensor("wkf", [128, 2 * DC * 128], F32, side="right"))
                wkb = PH(nc.sbuf_tensor("wkb", [128, 2 * DC * 128], BF16, side="right"))
                absb = PH(nc.sbuf_tensor("absb", [128, 2 * 512], BF16, side="right"))
                kstage = PH(nc.sbuf_tensor("kstage", [128, 2 * SEQ], BF16, side="right"))
                psum_k = [PH(nc.psum_tensor(f"pk{i}", [128, 512], F32)) for i in range(4)]
                psum_sk = PH(nc.psum_tensor("psk", [1, 128], F32))
                psum_tk = PH(nc.psum_tensor("ptk", [128, 1], F32))
                psum_nk = [PH(nc.psum_tensor(f"pnk{i}", [1, 512], F32)) for i in range(2)]

                def kst(g):
                    return kstage[:, (g % 2) * SEQ:((g % 2) + 1) * SEQ]

                with nc.Block("P1K", no_gpsimd_drain=True) as block:
                    @block.sync
                    def _(s):
                        for g in range(NG):
                            if g >= 2:
                                s.wait_ge(k_abs, 9 * (g - 1))
                            s.dma_start(wkf[:, (g % 2) * D:((g % 2) + 1) * D],
                                        wk_t[g]).then_inc(k_w[g % 2], 16)
                            if g >= 1:
                                s.wait_ge(k_ev, 4 * g)
                                s.dma_start(k_s[g - 1], kst(g - 1)).then_inc(k_st[(g - 1) % 2], 16)
                        s.wait_ge(k_ev, 4 * NG)
                        s.dma_start(k_s[NG - 1], kst(NG - 1)).then_inc(k_st[(NG - 1) % 2], 16)
                        s.wait_ge(k_st[0], 16 * (NG // 2))
                        s.wait_ge(k_st[1], 16 * (NG // 2))

                    @block.scalar
                    def _(sc):
                        for g in range(NG):
                            p = g % 2
                            sc.wait_ge(k_w[g % 2], 16 * (g // 2 + 1))
                            if g >= 2:
                                sc.wait_ge(k_pe, 4 * (g - 1))
                            sc.activation(wkb[:, p * D:(p + 1) * D], wkf[:, p * D:(p + 1) * D],
                                          AF.Sign).then_inc(k_abs, 1)
                            for c in range(8):
                                if 8 * g + c >= 2:
                                    sc.wait_ge(k_spe, 8 * g + c - 1)
                                sc.activation(absb[:, (c % 2) * 512:((c % 2) + 1) * 512],
                                              wkf[:, p * D + c * 512: p * D + (c + 1) * 512],
                                              AF.Abs).then_inc(k_abs, 1)
                            sc.wait_ge(k_spe, 8 * (g + 1))
                            if g >= 2:
                                sc.wait_ge(k_tpe, g - 1)
                            sc.activation(srow_sb[0:1, p * 128:(p + 1) * 128], psum_sk[:], AF.Copy).then_inc(k_ev1, 1)
                            sc.wait_ge(k_tpe, g + 1)
                            if g >= 2:
                                sc.wait_ge(k_dv, 2 * (g - 1))
                            sc.activation(scol_tmp[:, p:p + 1], psum_tk[:], AF.Copy).then_inc(k_ev2, 1)
                            for j in range(4):
                                sc.wait_ge(k_dv, 2 * (g + 1))
                                sc.wait_ge(k_pe, 4 * g + j + 1)
                                if g >= 2 and j == 0:
                                    sc.wait_ge(k_st[g % 2], 16 * (g // 2))
                                sc.activation(kstage[:, p * SEQ + j * 512: p * SEQ + (j + 1) * 512],
                                              psum_k[(4 * g + j) % 4][:], AF.Copy,
                                              scale=kvec_sb[:, p:p + 1]).then_inc(k_ev, 1)
                            if g >= 1:
                                sc.wait_ge(k_nkpe, 4 * g)
                            sc.wait_ge(k_ev, 4 * (g + 1))
                            for j in range(4):
                                sc.activation(scr_a[:, j * 512:(j + 1) * 512],
                                              kstage[:, p * SEQ + j * 512: p * SEQ + (j + 1) * 512],
                                              AF.Square).then_inc(k_sq, 1)

                    @block.tensor
                    def _(t):
                        def emit_nk(gg):
                            for j in range(4):
                                jj4 = 4 * gg + j
                                if jj4 >= 2:
                                    t.wait_ge(k_dvnk, jj4 - 1)
                                t.wait_ge(k_sq, jj4 + 1)
                                t.matmul(psum_nk[jj4 % 2][:], ones_col[:],
                                         scr_a[:, j * 512:(j + 1) * 512],
                                         start=True, stop=True).then_inc(k_nkpe, 1)

                        for g in range(NG):
                            p = g % 2
                            t.wait_ge(k_abs, 9 * g + 1)
                            for j in range(4):
                                t.wait_ge(s0_x[0 if j < 2 else 1], 16)
                                if 4 * g + j >= 4:
                                    t.wait_ge(k_ev, 4 * g + j - 3)
                                for dc in range(DC):
                                    ins = t.matmul(psum_k[(4 * g + j) % 4][:],
                                                   wkb[:, p * D + dc * 128: p * D + (dc + 1) * 128],
                                                   xall(dc, j),
                                                   start=(dc == 0), stop=(dc == DC - 1))
                                ins.then_inc(k_pe, 1)
                            if g >= 1:
                                t.wait_ge(k_ev1, g)
                            for c in range(8):
                                t.wait_ge(k_abs, 9 * g + c + 2)
                                for jj in range(4):
                                    dc = 4 * c + jj
                                    ins = t.matmul(psum_sk[:], ones_col[:],
                                                   absb[:, (c % 2) * 512 + jj * 128:(c % 2) * 512 + (jj + 1) * 128],
                                                   start=(dc == 0), stop=(dc == DC - 1))
                                ins.then_inc(k_spe, 1)
                            if g >= 1:
                                emit_nk(g - 1)
                            t.wait_ge(k_ev1, g + 1)
                            if g >= 1:
                                t.wait_ge(k_ev2, g)
                            t.matmul(psum_tk[:], srow_sb[0:1, p * 128:(p + 1) * 128], onef[:],
                                     start=True, stop=True).then_inc(k_tpe, 1)
                        emit_nk(NG - 1)

                    @block.vector
                    def _(v):
                        for g in range(NG):
                            p = g % 2
                            v.wait_ge(k_ev2, g + 1)
                            v.tensor_scalar(scol_tmp[:, p:p + 1], scol_tmp[:, p:p + 1], INV4096, 1e-8,
                                            ALU.mult, ALU.max).then_inc(k_dv, 1)
                            v.wait_ge(k_dv, 2 * g + 1)
                            if g >= 2:
                                v.wait_ge(k_ev, 4 * (g - 1))
                            v.tensor_tensor(kvec_sb[:, p:p + 1], scol_tmp[:, p:p + 1],
                                            ksc_vec[:, g:g + 1], ALU.mult).then_inc(k_dv, 1)
                            for j in range(4):
                                jj4 = 4 * g + j
                                v.wait_ge(k_nkpe, jj4 + 1)
                                if j == 0 and g >= 1:
                                    v.wait_ge(k_dvnk2, g)
                                v.reduce_max(Mk2t_sb[0:1, j:j + 1],
                                             psum_nk[jj4 % 2][:], axis=AX.X).then_inc(k_dvnk, 1)
                            v.wait_ge(k_dvnk, 4 * (g + 1))
                            v.tensor_reduce(Mk2_sb[0:1, g:g + 1], Mk2t_sb[0:1, 0:4], axis=AX.X,
                                            op=ALU.max).then_inc(k_dvnk2, 1)

            with ExitStack() as ph:
                PH = ph.enter_context
                wvf = PH(nc.sbuf_tensor("wvf", [128, 2 * KV], F32, side="right"))
                wvh = PH(nc.sbuf_tensor("wvh", [128, DC * 512], BF16, side="right"))
                vabs = PH(nc.sbuf_tensor("vabs", [128, 2 * KV], BF16, side="right"))
                vstage = PH(nc.sbuf_tensor("vstage", [128, 2 * 512], BF16, side="right"))
                psum_v = [PH(nc.psum_tensor(f"pv{i}", [128, 512], F32)) for i in range(2)]
                psum_sv = PH(nc.psum_tensor("psv", [1, KV], F32))
                psum_vb = PH(nc.psum_tensor("pvb", [128, 1024], F32))

                def vsgn_count(i):
                    return 2 * (i + 1) if i < DC else 2 * DC + (i - DC) + 1

                with nc.Block("P1V", no_gpsimd_drain=True) as block:
                    @block.sync
                    def _(s):
                        for half in range(2):
                            for dc in range(DC):
                                i = half * DC + dc
                                if i >= 2:
                                    s.wait_ge(v_sgn, vsgn_count(i - 2))
                                s.dma_start(wvf[:, (i % 2) * KV:((i % 2) + 1) * KV],
                                            wv_t[dc]).then_inc(v_w[i % 2], 16)

                    @block.gpsimd
                    def _(g):
                        for it in range(32):
                            g.wait_ge(v_ev, it + 1)
                            half, tc = it // 16, it % 16
                            g.dma_start(v_s[tc, :, half * 512:(half + 1) * 512],
                                        vstage[:, (it % 2) * 512:((it % 2) + 1) * 512]).then_inc(v_st[it % 2], 16)
                        g.wait_ge(v_st[0], 16 * 16)
                        g.wait_ge(v_st[1], 16 * 16)

                    @block.scalar
                    def _(sc):
                        for half in range(2):
                            if half == 1:
                                sc.wait_ge(v_pe, 16)
                            for dc in range(DC):
                                i = half * DC + dc
                                sc.wait_ge(v_w[i % 2], 16 * (i // 2 + 1))
                                sc.activation(wvh[:, dc * 512:(dc + 1) * 512],
                                              wvf[:, (i % 2) * KV + half * 512:(i % 2) * KV + (half + 1) * 512],
                                              AF.Sign).then_inc(v_sgn, 1)
                                if half == 0:
                                    if dc >= 2:
                                        sc.wait_ge(v_svpe, 2 * (dc - 1))
                                    sc.activation(vabs[:, (dc % 2) * KV:((dc % 2) + 1) * KV],
                                                  wvf[:, (i % 2) * KV:((i % 2) + 1) * KV],
                                                  AF.Abs).then_inc(v_sgn, 1)
                            if half == 0:
                                sc.wait_ge(v_svpe, 2 * DC)
                                sc.activation(vrow_sb[:], psum_sv[:], AF.Copy).then_inc(v_sve, 1)
                                sc.wait_ge(v_bpe, 2)
                                sc.activation(scr_b[:, 0:512], psum_vb[:, 0:512], AF.Copy).then_inc(v_bev, 1)
                                sc.activation(scr_b[:, 512:1024], psum_vb[:, 512:1024], AF.Copy).then_inc(v_bev, 1)

                    @block.tensor
                    def _(t):
                        t.wait_ge(s0_x[0], 16)
                        t.wait_ge(s0_x[1], 16)
                        for half in range(2):
                            if half == 0:
                                for dc in range(DC):
                                    t.wait_ge(v_sgn, 2 * (dc + 1))
                                    for j in range(2):
                                        t.matmul(psum_sv[0:1, j * 512:(j + 1) * 512], ones_col[:],
                                                 vabs[:, (dc % 2) * KV + j * 512:(dc % 2) * KV + (j + 1) * 512],
                                                 start=(dc == 0), stop=(dc == DC - 1)).then_inc(v_svpe, 1)
                                t.wait_ge(v_dvr, 2)
                                for j in range(2):
                                    t.matmul(psum_vb[:, j * 512:(j + 1) * 512], ones_rowf[:],
                                             vrow_sb[0:1, j * 512:(j + 1) * 512],
                                             start=True, stop=True).then_inc(v_bpe, 1)
                            for tc in range(16):
                                it = half * 16 + tc
                                if it >= 2:
                                    t.wait_ge(v_ev, it - 1)
                                for dc in range(DC):
                                    if tc == 0:
                                        t.wait_ge(v_sgn, (2 * dc + 1) if half == 0 else (2 * DC + dc + 1))
                                    ins = t.matmul(psum_v[it % 2][:], xtile_all(dc, tc),
                                                   wvh[:, dc * 512:(dc + 1) * 512],
                                                   start=(dc == 0), stop=(dc == DC - 1))
                                ins.then_inc(v_pe, 1)

                    @block.vector
                    def _(v):
                        v.wait_ge(v_sve, 1)
                        v.wait_ge(s0_c, 176)
                        v.tensor_scalar(vrow_sb[:], vrow_sb[:], INV4096, 1e-8, ALU.mult, ALU.max).then_inc(v_dvr, 1)
                        v.wait_ge(v_dvr, 1)
                        v.tensor_tensor(vrow_sb[:], vrow_sb[:], vsc_row[:], ALU.mult).then_inc(v_dvr, 1)
                        for it in range(32):
                            half = it // 16
                            v.wait_ge(v_pe, it + 1)
                            if it == 0:
                                v.wait_ge(v_bev, 2)
                            if it >= 2:
                                v.wait_ge(v_st[it % 2], 16 * (it // 2))
                            v.tensor_tensor(vstage[:, (it % 2) * 512:((it % 2) + 1) * 512],
                                            psum_v[it % 2][:],
                                            scr_b[:, half * 512:(half + 1) * 512],
                                            ALU.mult).then_inc(v_ev, 1)

        with ExitStack() as scA:
            A = scA.enter_context(nc.sbuf_tensor("A", [128, 33 * TOK], BF16))
            Kg = scA.enter_context(nc.sbuf_tensor("Kg", [128, 2 * SEQ], BF16, side="right"))
            Vg = scA.enter_context(nc.sbuf_tensor("Vg", [128, 2 * SEQ], BF16, side="right"))

            def q_slot(h):
                return A[:, h * TOK:(h + 1) * TOK]

            def att_slot(h):
                s = 32 if h == 0 else h - 1
                return A[:, s * TOK:(s + 1) * TOK]

            with ExitStack() as ph:
                PH = ph.enter_context
                wqf = PH(nc.sbuf_tensor("wqf", [128, 2 * DC * 128], F32, side="right"))
                wqb = PH(nc.sbuf_tensor("wqb", [128, 2 * DC * 128], BF16, side="right"))
                qabsb = PH(nc.sbuf_tensor("qabsb", [128, 2 * 512], BF16, side="right"))
                psum_q = [PH(nc.psum_tensor(f"pq{i}", [128, 512], F32)) for i in range(4)]
                psum_sq = PH(nc.psum_tensor("psq", [1, 128], F32))
                psum_tq = PH(nc.psum_tensor("ptq", [128, 1], F32))
                psum_nq = PH(nc.psum_tensor("pnq", [1, TOK], F32))

                with nc.Block("P2", no_gpsimd_drain=True) as block:
                    @block.sync
                    def _(s):
                        for h in range(NH):
                            if h >= 2:
                                s.wait_ge(q_abs, 9 * (h - 1))
                            s.dma_start(wqf[:, (h % 2) * D:((h % 2) + 1) * D],
                                        wq_t[h]).then_inc(q_w[h % 2], 16)
                        for g in range(2):
                            s.dma_start(Kg[:, (g % 2) * SEQ:((g % 2) + 1) * SEQ],
                                        k_s[g]).then_inc(a_k[g % 2], 16)
                            s.dma_start(
                                Vg[:, (g % 2) * SEQ:((g % 2) + 1) * SEQ].rearrange("p (k c) -> p k c", k=16),
                                v_s[:, :, g * 128:(g + 1) * 128].rearrange("k p c -> p k c"),
                            ).then_inc(a_v[g % 2], 16)

                    @block.scalar
                    def _(sc):
                        def emit_c(hh2):
                            gg = hh2 // 4
                            sc.wait_ge(q_dvc, 2 * hh2 + 1)
                            if hh2 >= 2:
                                sc.wait_ge(q_dvc, 2 * (hh2 - 1))
                            sc.activation(cb_c[0:1, hh2 % 2:hh2 % 2 + 1],
                                          nqx_sb[0:1, hh2 % 2:hh2 % 2 + 1], AF.Sqrt,
                                          scale=Mk2_sb[0:1, gg:gg + 1]).then_inc(q_cev, 1)
                            sc.wait_ge(q_cpe, hh2 + 1)
                            sc.activation(cbias[:, hh2:hh2 + 1], psum_tq[:], AF.Copy).then_inc(q_cev, 1)

                        for h in range(NH):
                            p = h % 2
                            sc.wait_ge(q_w[h % 2], 16 * (h // 2 + 1))
                            if h >= 2:
                                sc.wait_ge(q_pe, 2 * (h - 1))
                            sc.activation(wqb[:, p * D:(p + 1) * D], wqf[:, p * D:(p + 1) * D],
                                          AF.Sign).then_inc(q_abs, 1)
                            for c in range(8):
                                if 8 * h + c >= 2:
                                    sc.wait_ge(q_spe, 8 * h + c - 1)
                                sc.activation(qabsb[:, (c % 2) * 512:((c % 2) + 1) * 512],
                                              wqf[:, p * D + c * 512: p * D + (c + 1) * 512],
                                              AF.Abs).then_inc(q_abs, 1)
                            sc.wait_ge(q_spe, 8 * (h + 1))
                            if h >= 2:
                                sc.wait_ge(q_tpe, h - 1)
                            sc.activation(srow_sb[0:1, p * 128:(p + 1) * 128], psum_sq[:], AF.Copy).then_inc(q_ev1, 1)
                            if h >= 1:
                                emit_c(h - 1)
                            sc.wait_ge(q_tpe, h + 1)
                            if h >= 2:
                                sc.wait_ge(q_dv, 2 * (h - 1))
                            sc.activation(scol_tmp[:, p:p + 1], psum_tq[:], AF.Copy).then_inc(q_ev2, 1)
                            for j in range(2):
                                sc.wait_ge(q_dv, 2 * (h + 1))
                                sc.wait_ge(q_pe, 2 * h + j + 1)
                                sc.activation(q_slot(h)[:, j * 512:(j + 1) * 512],
                                              psum_q[(2 * h + j) % 4][:], AF.Copy,
                                              scale=qvec_sb[:, p:p + 1]).then_inc(q_ev, 1)
                            if h >= 1:
                                sc.wait_ge(q_nqpe, h)
                            sc.wait_ge(q_ev, 2 * (h + 1))
                            for j in range(2):
                                sc.activation(scr_a[:, j * 512:(j + 1) * 512],
                                              q_slot(h)[:, j * 512:(j + 1) * 512],
                                              AF.Square).then_inc(q_sq, 1)
                        emit_c(NH - 1)

                    @block.tensor
                    def _(t):
                        def emit_nq_bc(hh2):
                            if hh2 >= 1:
                                t.wait_ge(q_dvc, 2 * hh2 - 1)
                            for j in range(2):
                                t.wait_ge(q_sq, 2 * hh2 + j + 1)
                                ins = t.matmul(psum_nq[0:1, j * 512:(j + 1) * 512], ones_col[:],
                                               scr_a[:, j * 512:(j + 1) * 512],
                                               start=True, stop=True)
                                if j == 1:
                                    ins.then_inc(q_nqpe, 1)
                            t.wait_ge(q_dvc, 2 * hh2 + 2)
                            t.wait_ge(q_ev2, hh2 + 1)
                            t.matmul(psum_tq[:], ones_rowf[:], cb_n[0:1, hh2 % 2:hh2 % 2 + 1],
                                     start=True, stop=True).then_inc(q_cpe, 1)

                        for h in range(NH):
                            p = h % 2
                            t.wait_ge(q_abs, 9 * h + 1)
                            for j in range(2):
                                n = 2 * h + j
                                if n >= 4:
                                    t.wait_ge(q_ev, n - 3)
                                for dc in range(DC):
                                    ins = t.matmul(psum_q[n % 4][:],
                                                   wqb[:, p * D + dc * 128: p * D + (dc + 1) * 128],
                                                   x_sb[:, dc * TOK + j * 512: dc * TOK + (j + 1) * 512],
                                                   start=(dc == 0), stop=(dc == DC - 1))
                                ins.then_inc(q_pe, 1)
                            if h >= 1:
                                t.wait_ge(q_ev1, h)
                            for c in range(8):
                                t.wait_ge(q_abs, 9 * h + c + 2)
                                for jj in range(4):
                                    dc = 4 * c + jj
                                    ins = t.matmul(psum_sq[:], ones_col[:],
                                                   qabsb[:, (c % 2) * 512 + jj * 128:(c % 2) * 512 + (jj + 1) * 128],
                                                   start=(dc == 0), stop=(dc == DC - 1))
                                ins.then_inc(q_spe, 1)
                            if h >= 1:
                                emit_nq_bc(h - 1)
                            t.wait_ge(q_ev1, h + 1)
                            if h >= 1:
                                t.wait_ge(q_ev2, h)
                                t.wait_ge(q_cev, 2 * h)
                            t.matmul(psum_tq[:], srow_sb[0:1, p * 128:(p + 1) * 128], onef[:],
                                     start=True, stop=True).then_inc(q_tpe, 1)
                        emit_nq_bc(NH - 1)

                    @block.vector
                    def _(v):
                        for h in range(NH):
                            p = h % 2
                            v.wait_ge(q_ev2, h + 1)
                            v.tensor_scalar(scol_tmp[:, p:p + 1], scol_tmp[:, p:p + 1], INV4096, 1e-8,
                                            ALU.mult, ALU.max).then_inc(q_dv, 1)
                            v.wait_ge(q_dv, 2 * h + 1)
                            if h >= 2:
                                v.wait_ge(q_ev, 2 * (h - 1))
                            v.scalar_tensor_tensor(qvec_sb[:, p:p + 1], scol_tmp[:, p:p + 1],
                                                   INV_SQRT_HD, qsc_vec[:, h:h + 1],
                                                   ALU.mult, ALU.mult).then_inc(q_dv, 1)
                            v.wait_ge(q_nqpe, h + 1)
                            if h >= 2:
                                v.wait_ge(q_cev, 2 * h - 3)
                            v.reduce_max(nqx_sb[0:1, h % 2:h % 2 + 1], psum_nq[:],
                                         axis=AX.X).then_inc(q_dvc, 1)
                            v.wait_ge(q_cev, 2 * h + 1)
                            if h >= 2:
                                v.wait_ge(q_cpe, h - 1)
                            v.tensor_scalar_mul(cb_n[0:1, h % 2:h % 2 + 1],
                                                cb_c[0:1, h % 2:h % 2 + 1], -1.0).then_inc(q_dvc, 1)

            if debug:
                with nc.Block("DBGQ", no_gpsimd_drain=True) as block:
                    @block.sync
                    def _(s):
                        s.dma_start(dbg_q[:], A[:]).then_inc(dbg_s, 16)
                        s.wait_ge(dbg_s, 16)

            wof = scA.enter_context(nc.sbuf_tensor("wof", [128, 2 * DC * 128], F32, side="right"))

            with ExitStack() as ph:
                PH = ph.enter_context
                rr_sb = PH(nc.sbuf_tensor("rr_sb", [1, 2 * 512], F32, side="right"))
                ptbuf = PH(nc.sbuf_tensor("ptbuf", [128, 2 * 512], BF16, side="right"))
                psum_st = [PH(nc.psum_tensor(f"pst{i}", [128, 512], F32)) for i in range(2)]
                psum_av = [PH(nc.psum_tensor(f"pav{i}", [128, 512], F32)) for i in range(2)]
                psum_r = [PH(nc.psum_tensor(f"pr{i}", [1, 512], F32)) for i in range(2)]
                psum_bc = PH(nc.psum_tensor("pbc", [128, 512], F32))

                with nc.Block("P3", no_gpsimd_drain=True) as block:
                    @block.sync
                    def _(s):
                        for g in range(2, NG):
                            s.wait_ge(a_pe, 128 * (g - 1))
                            s.dma_start(Kg[:, (g % 2) * SEQ:((g % 2) + 1) * SEQ], k_s[g]).then_inc(a_k[g % 2], 16)
                            s.wait_ge(a_pu, 128 * (g - 1))
                            s.dma_start(
                                Vg[:, (g % 2) * SEQ:((g % 2) + 1) * SEQ].rearrange("p (k c) -> p k c", k=16),
                                v_s[:, :, g * 128:(g + 1) * 128].rearrange("k p c -> p k c"),
                            ).then_inc(a_v[g % 2], 16)
                        for oc in range(2):
                            s.wait_ge(a_pe, 128 * (NG - 1))
                            s.dma_start(wof[:, (oc % 2) * D:((oc % 2) + 1) * D],
                                        wo_t[oc]).then_inc(o_w[oc % 2], 16)

                    @block.tensor
                    def _(t):
                        def emit_bc(itb):
                            t.wait_ge(a_rr, itb + 1)
                            if itb >= 1:
                                t.wait_ge(a_bce, itb)
                            t.matmul(psum_bc[:], ones_rowf[:],
                                     rr_sb[0:1, (itb % 2) * 512:((itb % 2) + 1) * 512],
                                     start=True, stop=True).then_inc(a_pb, 1)

                        for g in range(NG):
                            gp = g % 2
                            for hh in range(4):
                                h = g * 4 + hh
                                for qs in range(2):
                                    it = h * 2 + qs
                                    for kc in range(17):
                                        if kc < 16:
                                            n = it * 16 + kc
                                            if kc == 0:
                                                t.wait_ge(a_k[g % 2], 16 * (g // 2 + 1))
                                                t.wait_ge(a_v[g % 2], 16 * (g // 2 + 1))
                                            if n >= 2:
                                                t.wait_ge(a_pt, n - 1)
                                            t.matmul(psum_st[kc % 2][:],
                                                     Kg[:, gp * SEQ + kc * 128: gp * SEQ + (kc + 1) * 128],
                                                     q_slot(h)[:, qs * 512:(qs + 1) * 512],
                                                     start=True, stop=True).then_inc(a_pe, 1)
                                        if kc == 2 and it >= 1:
                                            emit_bc(it - 1)
                                        if kc >= 1:
                                            km = kc - 1
                                            nm = it * 16 + km
                                            t.wait_ge(a_pt, nm + 1)
                                            if km == 0 and it >= 2:
                                                t.wait_ge(a_rr, it - 1)
                                            ins_r = t.matmul(psum_r[it % 2][:], ones_col[:],
                                                             ptbuf[:, (km % 2) * 512:((km % 2) + 1) * 512],
                                                             start=(km == 0), stop=(km == 15))
                                            if km == 15:
                                                ins_r.then_inc(a_pr, 1)
                                            if km == 0 and it >= 2:
                                                t.wait_ge(a_at, it - 1)
                                            t.matmul(psum_av[it % 2][:],
                                                     Vg[:, gp * SEQ + km * 128: gp * SEQ + (km + 1) * 128],
                                                     ptbuf[:, (km % 2) * 512:((km % 2) + 1) * 512],
                                                     start=(km == 0), stop=(km == 15)).then_inc(a_pu, 1)
                        emit_bc(63)

                    @block.scalar
                    def _(sc):
                        def emit_bcevac(itb):
                            sc.wait_ge(a_pb, itb + 1)
                            if itb >= 2:
                                sc.wait_ge(a_at, itb - 1)
                            sc.activation(scr_b[:, (itb % 2) * 512:((itb % 2) + 1) * 512],
                                          psum_bc[:], AF.Copy).then_inc(a_bce, 1)

                        for it in range(64):
                            for kc in range(16):
                                n = it * 16 + kc
                                sc.wait_ge(a_pe, n + 1)
                                if n >= 2:
                                    sc.wait_ge(a_pu, n - 1)
                                sc.activation(ptbuf[:, (kc % 2) * 512:((kc % 2) + 1) * 512],
                                              psum_st[kc % 2][:], AF.Exp,
                                              bias=cbias[:, (it // 2):(it // 2) + 1]).then_inc(a_pt, 1)
                                if kc == 3 and it >= 1:
                                    emit_bcevac(it - 1)
                        emit_bcevac(63)

                    @block.vector
                    def _(v):
                        for it in range(64):
                            h = it // 2
                            qs = it % 2
                            v.wait_ge(a_pr, it + 1)
                            if it >= 2:
                                v.wait_ge(a_pb, it - 1)
                            v.reciprocal(rr_sb[0:1, (it % 2) * 512:((it % 2) + 1) * 512],
                                         psum_r[it % 2][:]).then_inc(a_rr, 1)
                            v.wait_ge(a_bce, it + 1)
                            v.wait_ge(a_pu, 16 * (it + 1))
                            v.tensor_tensor(att_slot(h)[:, qs * 512:(qs + 1) * 512],
                                            psum_av[it % 2][:],
                                            scr_b[:, (it % 2) * 512:((it % 2) + 1) * 512],
                                            ALU.mult).then_inc(a_at, 1)

            with ExitStack() as ph:
                PH = ph.enter_context
                wob = PH(nc.sbuf_tensor("wob", [128, 2 * DC * 128], BF16, side="right"))
                oabsb = PH(nc.sbuf_tensor("oabsb", [128, 2 * 512], BF16, side="right"))
                psum_o = [PH(nc.psum_tensor(f"po{i}", [128, 512], F32)) for i in range(4)]
                psum_so = PH(nc.psum_tensor("pso", [1, 128], F32))
                psum_to = PH(nc.psum_tensor("pto", [128, 1], F32))

                with nc.Block("P4a", no_gpsimd_drain=True) as block:
                    @block.sync
                    def _(s):
                        for oc in range(2, DC):
                            s.wait_ge(o_abs, 9 * (oc - 1))
                            s.dma_start(wof[:, (oc % 2) * D:((oc % 2) + 1) * D],
                                        wo_t[oc]).then_inc(o_w[oc % 2], 16)

                    @block.scalar
                    def _(sc):
                        for oc in range(DC):
                            p = oc % 2
                            sc.wait_ge(o_w[oc % 2], 16 * (oc // 2 + 1))
                            if oc >= 2:
                                sc.wait_ge(o_pe, 2 * (oc - 1))
                            sc.activation(wob[:, p * D:(p + 1) * D], wof[:, p * D:(p + 1) * D],
                                          AF.Sign).then_inc(o_abs, 1)
                            for c in range(8):
                                if 8 * oc + c >= 2:
                                    sc.wait_ge(o_spe, 8 * oc + c - 1)
                                sc.activation(oabsb[:, (c % 2) * 512:((c % 2) + 1) * 512],
                                              wof[:, p * D + c * 512: p * D + (c + 1) * 512],
                                              AF.Abs).then_inc(o_abs, 1)
                            sc.wait_ge(o_spe, 8 * (oc + 1))
                            if oc >= 2:
                                sc.wait_ge(o_tpe, oc - 1)
                            sc.activation(srow_sb[0:1, p * 128:(p + 1) * 128], psum_so[:], AF.Copy).then_inc(o_ev1, 1)
                            sc.wait_ge(o_tpe, oc + 1)
                            if oc >= 2:
                                sc.wait_ge(o_dv, 2 * (oc - 1))
                            sc.activation(scol_tmp[:, p:p + 1], psum_to[:], AF.Copy).then_inc(o_ev2, 1)

                    @block.tensor
                    def _(t):
                        for oc in range(DC):
                            p = oc % 2
                            t.wait_ge(o_abs, 9 * oc + 1)
                            for qs in range(2):
                                n = 2 * oc + qs
                                if n >= 4:
                                    t.wait_ge(o_ev, n - 3)
                                for hc in range(DC):
                                    ins = t.matmul(psum_o[n % 4][:],
                                                   wob[:, p * D + hc * 128: p * D + (hc + 1) * 128],
                                                   att_slot(hc)[:, qs * 512:(qs + 1) * 512],
                                                   start=(hc == 0), stop=(hc == DC - 1))
                                ins.then_inc(o_pe, 1)
                            if oc >= 1:
                                t.wait_ge(o_ev1, oc)
                            for c in range(8):
                                t.wait_ge(o_abs, 9 * oc + c + 2)
                                for jj in range(4):
                                    hc = 4 * c + jj
                                    ins = t.matmul(psum_so[:], ones_col[:],
                                                   oabsb[:, (c % 2) * 512 + jj * 128:(c % 2) * 512 + (jj + 1) * 128],
                                                   start=(hc == 0), stop=(hc == DC - 1))
                                ins.then_inc(o_spe, 1)
                            t.wait_ge(o_ev1, oc + 1)
                            if oc >= 1:
                                t.wait_ge(o_ev2, oc)
                            t.matmul(psum_to[:], srow_sb[0:1, p * 128:(p + 1) * 128], onef[:],
                                     start=True, stop=True).then_inc(o_tpe, 1)

                    @block.vector
                    def _(v):
                        for oc in range(DC):
                            p = oc % 2
                            v.wait_ge(o_ev2, oc + 1)
                            v.tensor_scalar(scol_tmp[:, p:p + 1], scol_tmp[:, p:p + 1], INV4096, 1e-8,
                                            ALU.mult, ALU.max).then_inc(o_dv, 1)
                            v.wait_ge(o_dv, 2 * oc + 1)
                            if oc >= 2:
                                v.wait_ge(o_ev, 2 * (oc - 1))
                            v.tensor_tensor(ovec_sb[:, p:p + 1], scol_tmp[:, p:p + 1],
                                            osc_vec[:, oc:oc + 1], ALU.mult).then_inc(o_dv, 1)
                            v.wait_ge(o_dv, 2 * (oc + 1))
                            for qs in range(2):
                                n = 2 * oc + qs
                                v.wait_ge(o_pe, n + 1)
                                tgt = x_sb[:, oc * TOK + qs * 512: oc * TOK + (qs + 1) * 512]
                                v.scalar_tensor_tensor(tgt, psum_o[n % 4][:], ovec_sb[:, p:p + 1], tgt,
                                                       ALU.mult, ALU.add).then_inc(o_ev, 1)

        def emit_ln(name, span_lo, span_w, to_out, sq, pe, ev, dv, bpe, bev, nrm, base,
                    store_sem=None, tbi=0, prefetch=None):
            nspan = span_w // 512
            with ExitStack() as ph:
                PH = ph.enter_context
                psum_s1 = PH(nc.psum_tensor(f"ps1{name}", [1, span_w], F32))
                psum_s2 = PH(nc.psum_tensor(f"ps2{name}", [1, span_w], F32))
                psum_ab = PH(nc.psum_tensor(f"pab{name}", [128, 2 * span_w], F32))
                ln_r1 = PH(nc.sbuf_tensor(f"lr1{name}", [1, span_w], F32))
                ln_r2 = PH(nc.sbuf_tensor(f"lr2{name}", [1, span_w], F32))
                ln_mu = ln_r1
                ln_a = PH(nc.sbuf_tensor(f"la{name}", [1, span_w], F32))
                ln_b = PH(nc.sbuf_tensor(f"lb{name}", [1, span_w], F32))
                ln_tmp = PH(nc.sbuf_tensor(f"lt{name}", [1, span_w], F32))
                ln_sd = ln_tmp
                a_bc = PH(nc.sbuf_tensor(f"abc{name}", [128, span_w], BF16))
                b_bc = PH(nc.sbuf_tensor(f"bbc{name}", [128, span_w], BF16))

                NDV = 7 + DC

                with nc.Block(name, no_gpsimd_drain=True) as block:
                    if prefetch is not None:
                        @block.gpsimd
                        def _(g):
                            prefetch(g)

                    @block.scalar
                    def _(sc):
                        B = base * NDV
                        if base >= 1:
                            sc.wait_ge(dv, B)
                        for dc in range(DC):
                            if dc >= 2:
                                sc.wait_ge(pe, base * DC + dc - 1)
                            sc.activation(scr_a[:, (dc % 2) * span_w:((dc % 2) + 1) * span_w],
                                          x_sb[:, dc * TOK + span_lo: dc * TOK + span_lo + span_w],
                                          AF.Square).then_inc(sq, 1)
                        sc.wait_ge(pe, base * DC + DC)
                        sc.activation(ln_r1[0:1, 0:span_w], psum_s1[:], AF.Copy).then_inc(ev, 1)
                        sc.activation(ln_r2[0:1, 0:span_w], psum_s2[:], AF.Copy).then_inc(ev, 1)
                        sc.wait_ge(dv, B + 5)
                        sc.activation(ln_sd[0:1, 0:span_w], ln_r2[0:1, 0:span_w], AF.Sqrt).then_inc(ev, 1)
                        sc.wait_ge(bpe, base * 2 * nspan + 2 * nspan)
                        for j in range(nspan):
                            sc.activation(a_bc[:, j * 512:(j + 1) * 512],
                                          psum_ab[:, j * 512:(j + 1) * 512], AF.Copy).then_inc(bev, 1)
                            sc.activation(b_bc[:, j * 512:(j + 1) * 512],
                                          psum_ab[:, span_w + j * 512: span_w + (j + 1) * 512],
                                          AF.Copy).then_inc(bev, 1)

                    @block.tensor
                    def _(t):
                        B = base * NDV
                        for dc in range(DC):
                            t.wait_ge(sq, base * DC + dc + 1)
                            for j in range(nspan):
                                t.matmul(psum_s1[0:1, j * 512:(j + 1) * 512], ones_col[:],
                                         x_sb[:, dc * TOK + span_lo + j * 512:
                                              dc * TOK + span_lo + (j + 1) * 512],
                                         start=(dc == 0), stop=(dc == DC - 1))
                                ins = t.matmul(psum_s2[0:1, j * 512:(j + 1) * 512], ones_col[:],
                                               scr_a[:, (dc % 2) * span_w + j * 512:
                                                     (dc % 2) * span_w + (j + 1) * 512],
                                               start=(dc == 0), stop=(dc == DC - 1))
                            ins.then_inc(pe, 1)
                        t.wait_ge(dv, B + 7)
                        for j in range(nspan):
                            t.matmul(psum_ab[:, j * 512:(j + 1) * 512], ones_rowf[:],
                                     ln_a[0:1, j * 512:(j + 1) * 512], start=True, stop=True).then_inc(bpe, 1)
                            t.matmul(psum_ab[:, span_w + j * 512: span_w + (j + 1) * 512], ones_rowf[:],
                                     ln_b[0:1, j * 512:(j + 1) * 512], start=True, stop=True).then_inc(bpe, 1)

                    @block.vector
                    def _(v):
                        B = base * NDV
                        v.wait_ge(ev, base * 3 + 2)
                        v.tensor_scalar_mul(ln_mu[0:1, 0:span_w], ln_r1[0:1, 0:span_w], INV4096).then_inc(dv, 1)
                        v.tensor_scalar_mul(ln_r2[0:1, 0:span_w], ln_r2[0:1, 0:span_w], INV4096).then_inc(dv, 1)
                        v.wait_ge(dv, B + 1)
                        v.tensor_tensor(ln_tmp[0:1, 0:span_w], ln_mu[0:1, 0:span_w],
                                        ln_mu[0:1, 0:span_w], ALU.mult).then_inc(dv, 1)
                        v.wait_ge(dv, B + 3)
                        v.tensor_tensor(ln_r2[0:1, 0:span_w], ln_r2[0:1, 0:span_w],
                                        ln_tmp[0:1, 0:span_w], ALU.subtract).then_inc(dv, 1)
                        v.wait_ge(dv, B + 4)
                        v.tensor_scalar_add(ln_r2[0:1, 0:span_w], ln_r2[0:1, 0:span_w], EPS).then_inc(dv, 1)
                        v.wait_ge(ev, base * 3 + 3)
                        v.reciprocal(ln_a[0:1, 0:span_w], ln_sd[0:1, 0:span_w]).then_inc(dv, 1)
                        v.wait_ge(dv, B + 6)
                        v.scalar_tensor_tensor(ln_b[0:1, 0:span_w], ln_mu[0:1, 0:span_w], -1.0,
                                               ln_a[0:1, 0:span_w], ALU.mult, ALU.mult).then_inc(dv, 1)
                        v.wait_ge(bev, base * 2 * nspan + 2 * nspan)
                        for dc in range(DC):
                            src = x_sb[:, dc * TOK + span_lo: dc * TOK + span_lo + span_w]
                            tt = scr_a[:, (dc % 2) * span_w:((dc % 2) + 1) * span_w]
                            if dc >= 2:
                                v.wait_ge(nrm, base * DC + dc - 1)
                            v.tensor_tensor(tt, src, a_bc[:, 0:span_w], ALU.mult).then_inc(dv, 1)
                            v.wait_ge(dv, B + 7 + dc + 1)
                            if to_out:
                                stg = scr_b[:, (dc % 2) * 512:((dc % 2) + 1) * 512]
                                gdc = tbi * DC + dc
                                if gdc >= 2:
                                    v.wait_ge(store_sem[gdc % 2], 16 * (gdc // 2))
                                v.tensor_tensor(stg, tt, b_bc[:, 0:span_w], ALU.add).then_inc(nrm, 1)
                            else:
                                v.tensor_tensor(src, tt, b_bc[:, 0:span_w], ALU.add).then_inc(nrm, 1)

                    if to_out:
                        @block.sync
                        def _(s):
                            for dc in range(DC):
                                s.wait_ge(nrm, base * DC + dc + 1)
                                s.dma_start(out[dc, :, span_lo:span_lo + 512],
                                            scr_b[:, (dc % 2) * 512:((dc % 2) + 1) * 512]
                                            ).then_inc(store_sem[dc % 2], 16)
                            if tbi == 1:
                                s.wait_ge(store_sem[0], 16 * DC)
                                s.wait_ge(store_sem[1], 16 * DC)

        if debug:
            with nc.Block("DBGX1", no_gpsimd_drain=True) as block:
                @block.sync
                def _(s):
                    s.dma_start(dbg_x1[:], x_sb[:]).then_inc(dbg_s, 16)
                    s.wait_ge(dbg_s, 32)

        with ExitStack() as sc5:
            wgb = sc5.enter_context(nc.sbuf_tensor("wgb", [128, 2 * DC * 128], BF16, side="right"))
            wub = sc5.enter_context(nc.sbuf_tensor("wub", [128, 2 * DC * 128], BF16, side="right"))
            wdb = sc5.enter_context(nc.sbuf_tensor("wdb", [128, 4 * 1024], BF16, side="right"))

            def ffn_prefetch(tb):
                def pf(g):
                    for ic in range(2):
                        g.dma_start(wgb[:, (ic % 2) * D:((ic % 2) + 1) * D],
                                    wg_t[ic]).then_inc(f_w[ic % 2], 16)
                        g.dma_start(wub[:, (ic % 2) * D:((ic % 2) + 1) * D],
                                    wu_t[ic]).then_inc(f_w[ic % 2], 16)
                return pf

            emit_ln("P4b", 0, TOK, False, l1_sq, l1_pe, l1_ev, l1_dv, l1_bpe, l1_bev, l1_n, 0,
                    prefetch=ffn_prefetch(0))

            for tb in range(2):
                lo = tb * 512
                schb = ExitStack()
                hbuf = schb.enter_context(nc.sbuf_tensor(f"hbuf{tb}", [128, IC * 512], BF16))
                with ExitStack() as ph:
                    PH = ph.enter_context
                    gstage = PH(nc.sbuf_tensor(f"gstage{tb}", [128, 2 * 512], F32, side="right"))
                    psum_g = [PH(nc.psum_tensor(f"pg{tb}{i}", [128, 512], F32)) for i in range(2)]
                    psum_u = [PH(nc.psum_tensor(f"pu{tb}{i}", [128, 512], F32)) for i in range(2)]

                    with nc.Block(f"P5a{tb}", no_gpsimd_drain=True) as block:
                        @block.gpsimd
                        def _(g):
                            for ic in range(2, IC):
                                g.wait_ge(f_mm, 2 * (tb * IC + ic - 1))
                                g.dma_start(wgb[:, (ic % 2) * D:((ic % 2) + 1) * D],
                                            wg_t[ic]).then_inc(f_w[ic % 2], 16)
                                g.dma_start(wub[:, (ic % 2) * D:((ic % 2) + 1) * D],
                                            wu_t[ic]).then_inc(f_w[ic % 2], 16)
                            for npf in range(4):
                                n0 = tb * 8 * 43 + npf
                                og0, icp0 = npf // 43, npf % 43
                                g.dma_start(
                                    wdb[:, (n0 % 4) * 1024:((n0 % 4) + 1) * 1024].rearrange(
                                        "p (i c) -> p i c", i=2),
                                    wd_t[og0, 2 * icp0:2 * icp0 + 2].rearrange("i p c -> p i c"),
                                ).then_inc(d_w[n0 % 4], 16)

                        @block.tensor
                        def _(t):
                            for ic in range(IC):
                                n = tb * IC + ic
                                t.wait_ge(f_w[ic % 2], 32 * (tb * 43 + ic // 2 + 1))
                                if ic >= 2:
                                    t.wait_ge(f_h, tb * IC + ic - 1)
                                p = ic % 2
                                for dc in range(DC):
                                    ins = t.matmul(psum_g[p][:],
                                                   wgb[:, p * D + dc * 128: p * D + (dc + 1) * 128],
                                                   x_sb[:, dc * TOK + lo: dc * TOK + lo + 512],
                                                   start=(dc == 0), stop=(dc == DC - 1))
                                ins.then_inc(f_mm, 1)
                                for dc in range(DC):
                                    ins = t.matmul(psum_u[p][:],
                                                   wub[:, p * D + dc * 128: p * D + (dc + 1) * 128],
                                                   x_sb[:, dc * TOK + lo: dc * TOK + lo + 512],
                                                   start=(dc == 0), stop=(dc == DC - 1))
                                ins.then_inc(f_mm, 1)

                        @block.scalar
                        def _(sc):
                            for ic in range(IC):
                                n = tb * IC + ic
                                sc.wait_ge(f_mm, 2 * n + 1)
                                sc.activation(gstage[:, (ic % 2) * 512:((ic % 2) + 1) * 512],
                                              psum_g[ic % 2][:], AF.Silu,
                                              scale=gs_vec[:]).then_inc(f_g, 1)

                        @block.vector
                        def _(v):
                            for ic in range(IC):
                                n = tb * IC + ic
                                v.wait_ge(f_g, n + 1)
                                v.wait_ge(f_mm, 2 * n + 2)
                                v.scalar_tensor_tensor(hbuf[:, ic * 512:(ic + 1) * 512],
                                                       psum_u[ic % 2][:], usds_vec[:],
                                                       gstage[:, (ic % 2) * 512:((ic % 2) + 1) * 512],
                                                       ALU.mult, ALU.mult).then_inc(f_h, 1)

                with ExitStack() as ph:
                    PH = ph.enter_context
                    psum_d = [PH(nc.psum_tensor(f"pd{tb}{i}", [128, 512], F32)) for i in range(8)]

                    with nc.Block(f"P5b{tb}", no_gpsimd_drain=True) as block:
                        @block.gpsimd
                        def _(g):
                            for og in range(8):
                                for icp in range(43):
                                    n = (tb * 8 + og) * 43 + icp
                                    if n - tb * 8 * 43 < 4:
                                        continue
                                    g.wait_ge(d_mm, n - 3)
                                    g.dma_start(
                                        wdb[:, (n % 4) * 1024:((n % 4) + 1) * 1024].rearrange(
                                            "p (i c) -> p i c", i=2),
                                        wd_t[og, 2 * icp:2 * icp + 2].rearrange("i p c -> p i c"),
                                    ).then_inc(d_w[n % 4], 16)

                        @block.tensor
                        def _(t):
                            for og in range(8):
                                gp = og % 2
                                if og >= 2:
                                    t.wait_ge(d_ev, (tb * 8 + og - 1) * 4)
                                for icp in range(43):
                                    n = (tb * 8 + og) * 43 + icp
                                    t.wait_ge(d_w[n % 4], 16 * (n // 4 + 1))
                                    for sub in range(2):
                                        ic = 2 * icp + sub
                                        for oc in range(4):
                                            ins = t.matmul(psum_d[gp * 4 + oc][:],
                                                           wdb[:, (n % 4) * 1024 + sub * 512 + oc * 128:
                                                               (n % 4) * 1024 + sub * 512 + (oc + 1) * 128],
                                                           hbuf[:, ic * 512:(ic + 1) * 512],
                                                           start=(ic == 0), stop=(ic == IC - 1))
                                    ins.then_inc(d_mm, 1)

                        @block.vector
                        def _(v):
                            for og in range(8):
                                gp = og % 2
                                v.wait_ge(d_mm, (tb * 8 + og + 1) * 43)
                                for oc in range(4):
                                    ocf = og * 4 + oc
                                    tgt = x_sb[:, ocf * TOK + lo: ocf * TOK + lo + 512]
                                    v.tensor_tensor(tgt, psum_d[gp * 4 + oc][:], tgt, ALU.add).then_inc(d_ev, 1)

                schb.close()
                emit_ln(f"P5c{tb}", lo, 512, True, l2_sq, l2_pe, l2_ev, l2_dv, l2_bpe, l2_bev,
                        l2_n, tb, store_sem=s_out, tbi=tb,
                        prefetch=(ffn_prefetch(1) if tb == 0 else None))

    return nc


def _prep_weights(inputs):
    f32 = np.float32
    wq = np.asarray(inputs["L0_attn_q_weight"], f32)
    wk = np.asarray(inputs["L0_attn_k_weight"], f32)
    wv = np.asarray(inputs["L0_attn_v_weight"], f32)
    wo = np.asarray(inputs["L0_attn_o_weight"], f32)
    wg = np.asarray(inputs["L0_ffn_gate_w"], np.int32)
    wu = np.asarray(inputs["L0_ffn_up_w"], np.int32)
    wd = np.asarray(inputs["L0_ffn_down_w"], np.int32)

    d = {}
    d["wq_t"] = np.ascontiguousarray(
        wq.reshape(NH, 128, DC, 128).transpose(0, 3, 2, 1)).reshape(NH, 128, DC * 128)
    d["wk_t"] = np.ascontiguousarray(
        wk.reshape(NG, 128, DC, 128).transpose(0, 3, 2, 1)).reshape(NG, 128, DC * 128)
    d["wv_t"] = np.ascontiguousarray(wv.T).reshape(DC, 128, KV)
    d["wo_t"] = np.ascontiguousarray(
        wo.reshape(DC, 128, DC, 128).transpose(0, 3, 2, 1)).reshape(DC, 128, DC * 128)
    d["wg_t"] = np.ascontiguousarray(
        wg.reshape(IC, 128, DC, 128).transpose(0, 3, 2, 1)).reshape(IC, 128, DC * 128)
    d["wu_t"] = np.ascontiguousarray(
        wu.reshape(IC, 128, DC, 128).transpose(0, 3, 2, 1)).reshape(IC, 128, DC * 128)
    d["wd_t"] = np.ascontiguousarray(
        wd.reshape(8, 4, 128, IC, 128).transpose(0, 3, 4, 1, 2)).reshape(8, IC, 128, 512)
    d["qsc_r"] = np.ascontiguousarray(np.asarray(inputs["L0_attn_q_scale"], f32).reshape(1, D))
    d["ksc_r"] = np.ascontiguousarray(np.asarray(inputs["L0_attn_k_scale"], f32).reshape(1, KV))
    d["vsc_r"] = np.ascontiguousarray(np.asarray(inputs["L0_attn_v_scale"], f32).reshape(1, KV))
    d["osc_r"] = np.ascontiguousarray(np.asarray(inputs["L0_attn_o_scale"], f32).reshape(1, D))
    d["gs_d"] = np.asarray(inputs["L0_ffn_gate_s"], f32).reshape(1, 1)
    d["us_d"] = np.asarray(inputs["L0_ffn_up_s"], f32).reshape(1, 1)
    d["ds_d"] = np.asarray(inputs["L0_ffn_down_s"], f32).reshape(1, 1)
    return d


def kernel(**inputs):
    x = np.asarray(inputs["L0_attn_hidden_states"], np.float32)
    shared = _prep_weights(inputs)

    nc = build_nc(debug=DEBUG_DUMPS)

    in_maps = []
    for c in range(8):
        b, hh = c // 2, c % 2
        x_own = x[b, hh * TOK:(hh + 1) * TOK, :]
        x_oth = x[b, (1 - hh) * TOK:(2 - hh) * TOK, :]
        m = dict(shared)
        m["xo_t"] = np.ascontiguousarray(x_own.T).reshape(DC, 128, TOK)
        m["xr_t"] = np.ascontiguousarray(x_oth.T).reshape(DC, 128, TOK)
        in_maps.append(m)

    res = run_bass_kernel_spmd(nc, in_maps, core_ids=list(range(8)))

    y = np.empty((4, SEQ, D), np.float32)
    for c in range(8):
        b, hh = c // 2, c % 2
        o = np.asarray(res.results[c]["out"])
        y[b, hh * TOK:(hh + 1) * TOK, :] = o.reshape(D, TOK).T
    return y


if __name__ == "__main__":
    import importlib.util
    spec = importlib.util.spec_from_file_location("_r", "/root/problem/reference.py")
    _r = importlib.util.module_from_spec(spec)
    spec.loader.exec_module(_r)
    ins = _r.setup_inputs()
    got = kernel(**{k: np.asarray(v) for k, v in ins.items()})
    want = np.asarray(_r.reference(**ins))
    err = got - want
    rv = float(np.mean(err ** 2) / np.mean(want ** 2))
    mx = float(np.abs(got.astype(ml_dtypes.bfloat16).astype(np.float32)
                      - want.astype(ml_dtypes.bfloat16).astype(np.float32)).max())
    print("residual variance ratio:", rv, " max_abs_bf16:", mx)


# version 06
# speedup vs baseline: 1.1115x; 1.0691x over previous
import numpy as np
import ml_dtypes
from contextlib import ExitStack

import concourse.bass as bass
import concourse.mybir as mybir
from concourse.bass_utils import run_bass_kernel_spmd

F32 = mybir.dt.float32
BF16 = mybir.dt.bfloat16
I32 = mybir.dt.int32
AF = mybir.ActivationFunctionType
ALU = mybir.AluOpType
AX = mybir.AxisListType

D = 4096
KV = 1024
NH = 32
NG = 8
TOK = 1024
SEQ = 2048
IC = 86
DC = 32
INV4096 = 1.0 / 4096.0
INV_SQRT_HD = float(1.0 / np.sqrt(128.0))
EPS = 1e-5

DEBUG_DUMPS = False


def build_nc(debug=False):
    nc = bass.Bass("TRN2", target_bir_lowering=False)

    xo_t = nc.dram_tensor("xo_t", [DC, 128, TOK], F32, kind="ExternalInput")
    xr_t = nc.dram_tensor("xr_t", [DC, 128, TOK], F32, kind="ExternalInput")
    wq_t = nc.dram_tensor("wq_t", [NH, 128, DC * 128], F32, kind="ExternalInput")
    wk_t = nc.dram_tensor("wk_t", [NG, 128, DC * 128], F32, kind="ExternalInput")
    wv_t = nc.dram_tensor("wv_t", [DC, 128, KV], F32, kind="ExternalInput")
    wo_t = nc.dram_tensor("wo_t", [DC, 128, DC * 128], F32, kind="ExternalInput")
    wg_t = nc.dram_tensor("wg_t", [IC, 128, DC * 128], I32, kind="ExternalInput")
    wu_t = nc.dram_tensor("wu_t", [IC, 128, DC * 128], I32, kind="ExternalInput")
    wd_t = nc.dram_tensor("wd_t", [8, IC, 128, 512], I32, kind="ExternalInput")
    qsc_r = nc.dram_tensor("qsc_r", [1, D], F32, kind="ExternalInput")
    ksc_r = nc.dram_tensor("ksc_r", [1, KV], F32, kind="ExternalInput")
    vsc_r = nc.dram_tensor("vsc_r", [1, KV], F32, kind="ExternalInput")
    osc_r = nc.dram_tensor("osc_r", [1, D], F32, kind="ExternalInput")
    gs_d = nc.dram_tensor("gs_d", [1, 1], F32, kind="ExternalInput")
    us_d = nc.dram_tensor("us_d", [1, 1], F32, kind="ExternalInput")
    ds_d = nc.dram_tensor("ds_d", [1, 1], F32, kind="ExternalInput")
    out = nc.dram_tensor("out", [DC, 128, TOK], F32, kind="ExternalOutput")

    skind = "ExternalOutput" if debug else "Internal"
    k_s = nc.dram_tensor("k_s", [NG, 128, SEQ], BF16, kind=skind)
    v_s = nc.dram_tensor("v_s", [16, 128, KV], BF16, kind=skind)
    if debug:
        dbg_q = nc.dram_tensor("dbg_q", [128, 33 * TOK], BF16, kind="ExternalOutput")
        dbg_x1 = nc.dram_tensor("dbg_x1", [128, DC * TOK], BF16, kind="ExternalOutput")

    c_ones_col = nc.inline_tensor(np.ones((128, 1), dtype=ml_dtypes.bfloat16), name="c_ones_col")
    c_negones = nc.inline_tensor(-np.ones((1, 128), dtype=ml_dtypes.bfloat16), name="c_negones")
    c_ones_rowf = nc.inline_tensor(np.ones((1, 128), dtype=np.float32), name="c_ones_rowf")
    c_onef = nc.inline_tensor(np.ones((1, 1), dtype=np.float32), name="c_onef")

    with ExitStack() as outer:
        E = outer.enter_context
        ones_col = E(nc.sbuf_tensor("ones_col", [128, 1], BF16))
        negones = E(nc.sbuf_tensor("negones", [1, 128], BF16))
        ones_rowf = E(nc.sbuf_tensor("ones_rowf", [1, 128], F32))
        onef = E(nc.sbuf_tensor("onef", [1, 1], F32))
        gs_sb = E(nc.sbuf_tensor("gs_sb", [1, 1], F32))
        us_sb = E(nc.sbuf_tensor("us_sb", [1, 1], F32))
        ds_sb = E(nc.sbuf_tensor("ds_sb", [1, 1], F32))
        usds_row = E(nc.sbuf_tensor("usds_row", [1, 1], F32))
        qsc_vec = E(nc.sbuf_tensor("qsc_vec", [128, NH], F32))
        ksc_vec = E(nc.sbuf_tensor("ksc_vec", [128, NG], F32))
        osc_vec = E(nc.sbuf_tensor("osc_vec", [128, DC], F32))
        gs_vec = E(nc.sbuf_tensor("gs_vec", [128, 1], F32))
        usds_vec = E(nc.sbuf_tensor("usds_vec", [128, 1], F32))
        srow_sb = E(nc.sbuf_tensor("srow_sb", [1, 2 * 128], F32))
        scol_tmp = E(nc.sbuf_tensor("scol_tmp", [128, 2], F32))
        kvec_sb = E(nc.sbuf_tensor("kvec_sb", [128, 2], F32))
        qvec_sb = E(nc.sbuf_tensor("qvec_sb", [128, 2], F32))
        ovec_sb = E(nc.sbuf_tensor("ovec_sb", [128, 2], F32))
        Mk2_sb = E(nc.sbuf_tensor("Mk2_sb", [1, NG], F32))
        nqx_sb = E(nc.sbuf_tensor("nqx_sb", [1, 2], F32))
        Mk2t_sb = E(nc.sbuf_tensor("Mk2t_sb", [1, 4], F32))
        cb_c = E(nc.sbuf_tensor("cb_c", [1, 2], F32))
        cb_n = E(nc.sbuf_tensor("cb_n", [1, 2], F32))
        cbias = E(nc.sbuf_tensor("cbias", [128, NH], F32))
        scr_a = E(nc.sbuf_tensor("scr_a", [128, 2048], BF16))
        scr_b = E(nc.sbuf_tensor("scr_b", [128, 1024], F32))

        x_sb = E(nc.sbuf_tensor("x_sb", [128, DC * TOK], BF16, side="right"))

        def S(name):
            return E(nc.semaphore(name))

        def S2(name):
            return [E(nc.semaphore(name + "0")), E(nc.semaphore(name + "1"))]

        s0_c = S("s0_c"); s0_x = S2("s0_x"); s0_pe = S("s0_pe"); s0_ae = S("s0_ae"); s0_dv = S("s0_dv")
        k_w = S2("k_w"); k_abs = S("k_abs"); k_spe = S("k_spe"); k_tpe = S("k_tpe")
        k_ev1 = S("k_ev1"); k_ev2 = S("k_ev2"); k_dv = S("k_dv"); k_pe = S("k_pe")
        k_ev = S("k_ev"); k_st = S2("k_st"); k_sq = S("k_sq"); k_nkpe = S("k_nkpe"); k_dvnk = S("k_dvnk"); k_dvnk2 = S("k_dvnk2")
        v_w = S2("v_w"); v_sgn = S("v_sgn"); v_svpe = S("v_svpe"); v_sve = S("v_sve")
        v_dvr = S("v_dvr"); v_bpe = S("v_bpe"); v_bev = S("v_bev"); v_pe = S("v_pe")
        v_ev = S("v_ev"); v_st = S2("v_st")
        q_w = S2("q_w"); q_abs = S("q_abs"); q_spe = S("q_spe"); q_tpe = S("q_tpe")
        q_ev1 = S("q_ev1"); q_ev2 = S("q_ev2"); q_dv = S("q_dv"); q_pe = S("q_pe")
        q_ev = S("q_ev"); q_sq = S("q_sq"); q_nqpe = S("q_nqpe"); q_cev = S("q_cev")
        a_k = S2("a_k"); a_v = S2("a_v"); a_pe = S("a_pe"); a_pt = S("a_pt"); a_pu = S("a_pu")
        a_pr = S("a_pr"); a_rr = S("a_rr"); a_pb = S("a_pb"); a_bce = S("a_bce"); a_at = S("a_at")
        o_w = S2("o_w"); o_abs = S("o_abs"); o_spe = S("o_spe"); o_tpe = S("o_tpe")
        o_ev1 = S("o_ev1"); o_ev2 = S("o_ev2"); o_dv = S("o_dv"); o_pe = S("o_pe"); o_ev = S("o_ev")
        l1_sq = S("l1_sq"); l1_pe = S("l1_pe"); l1_ev = S("l1_ev"); l1_dv = S("l1_dv")
        l1_bpe = S("l1_bpe"); l1_bev = S("l1_bev"); l1_n = S("l1_n")
        f_w = S2("f_w"); f_mm = S("f_mm"); f_g = S("f_g"); f_h = S("f_h")
        d_w = [S("d_w0"), S("d_w1"), S("d_w2"), S("d_w3")]; d_mm = S("d_mm"); d_ev = S("d_ev")
        l2_sq = S("l2_sq"); l2_pe = S("l2_pe"); l2_ev = S("l2_ev"); l2_dv = S("l2_dv")
        l2_bpe = S("l2_bpe"); l2_bev = S("l2_bev"); l2_n = S("l2_n"); s_out = S2("s_out")
        q_dvc = S("q_dvc"); q_cpe = S("q_cpe")
        dbg_s = S("dbg_s")

        with ExitStack() as ph:
            psum_t = [ph.enter_context(nc.psum_tensor(f"p0t{i}", [128, 1], F32)) for i in range(2)]
            qsc_row = ph.enter_context(nc.sbuf_tensor("qsc_row", [1, D], F32))
            ksc_row = ph.enter_context(nc.sbuf_tensor("ksc_row", [1, KV], F32))
            osc_row = ph.enter_context(nc.sbuf_tensor("osc_row", [1, D], F32))

            jobs = ([("q", j) for j in range(NH)] + [("k", j) for j in range(NG)]
                    + [("o", j) for j in range(DC)] + [("gs", 0), ("usds", 0)])

            with nc.Block("P0", no_gpsimd_drain=True) as block:
                @block.gpsimd
                def _(g):
                    g.dma_start(
                        x_sb[:].rearrange("p (c t) -> p c t", c=DC),
                        xo_t.rearrange("c p t -> p c t"),
                    ).then_inc(s0_x[0], 16)

                @block.sync
                def _(s):
                    for dst, src in [
                        (ones_col, c_ones_col), (negones, c_negones),
                        (ones_rowf, c_ones_rowf), (onef, c_onef),
                        (qsc_row, qsc_r), (ksc_row, ksc_r),
                        (osc_row, osc_r), (gs_sb, gs_d), (us_sb, us_d), (ds_sb, ds_d),
                    ]:
                        s.dma_start(dst[:], src[:]).then_inc(s0_c, 16)

                @block.vector
                def _(v):
                    v.wait_ge(s0_c, 160)
                    v.tensor_tensor(usds_row[:], us_sb[:], ds_sb[:], ALU.mult).then_inc(s0_dv, 1)

                @block.tensor
                def _(t):
                    t.wait_ge(s0_c, 160)
                    for cnt, (kind, j) in enumerate(jobs):
                        if cnt >= 2:
                            t.wait_ge(s0_ae, cnt - 1)
                        if kind == "q":
                            lhs = qsc_row[0:1, j * 128:(j + 1) * 128]
                            rhs = onef[:]
                        elif kind == "k":
                            lhs = ksc_row[0:1, j * 128:(j + 1) * 128]
                            rhs = onef[:]
                        elif kind == "o":
                            lhs = osc_row[0:1, j * 128:(j + 1) * 128]
                            rhs = onef[:]
                        elif kind == "gs":
                            lhs = ones_rowf[:]
                            rhs = gs_sb[:]
                        else:
                            t.wait_ge(s0_dv, 1)
                            lhs = ones_rowf[:]
                            rhs = usds_row[:]
                        t.matmul(psum_t[cnt % 2][:], lhs, rhs, start=True, stop=True).then_inc(s0_pe, 1)

                @block.scalar
                def _(sc):
                    for cnt, (kind, j) in enumerate(jobs):
                        sc.wait_ge(s0_pe, cnt + 1)
                        if kind == "q":
                            dst = qsc_vec[:, j:j + 1]
                        elif kind == "k":
                            dst = ksc_vec[:, j:j + 1]
                        elif kind == "o":
                            dst = osc_vec[:, j:j + 1]
                        elif kind == "gs":
                            dst = gs_vec[:]
                        else:
                            dst = usds_vec[:]
                        sc.activation(dst, psum_t[cnt % 2][:], AF.Copy).then_inc(s0_ae, 1)

        with ExitStack() as sc01:
            xr_sb = sc01.enter_context(nc.sbuf_tensor("xr_sb", [128, DC * TOK], BF16, side="right"))
            vsc_row = sc01.enter_context(nc.sbuf_tensor("vsc_row", [1, KV], F32))
            vrow_sb = sc01.enter_context(nc.sbuf_tensor("vrow_sb", [1, KV], F32))

            def xall(dc, sp):
                if sp < 2:
                    return x_sb[:, dc * TOK + sp * 512: dc * TOK + (sp + 1) * 512]
                return xr_sb[:, dc * TOK + (sp - 2) * 512: dc * TOK + (sp - 1) * 512]

            def xtile_all(dc, tc):
                if tc < 8:
                    return x_sb[:, dc * TOK + tc * 128: dc * TOK + (tc + 1) * 128]
                return xr_sb[:, dc * TOK + (tc - 8) * 128: dc * TOK + (tc - 7) * 128]

            with nc.Block("P0X", no_gpsimd_drain=True) as block:
                @block.gpsimd
                def _(g):
                    g.dma_start(
                        xr_sb[:].rearrange("p (c t) -> p c t", c=DC),
                        xr_t.rearrange("c p t -> p c t"),
                    ).then_inc(s0_x[1], 16)

                @block.sync
                def _(s):
                    s.dma_start(vsc_row[:], vsc_r[:]).then_inc(s0_c, 16)

            with ExitStack() as ph:
                PH = ph.enter_context
                wkf = PH(nc.sbuf_tensor("wkf", [128, 2 * DC * 128], F32, side="right"))
                wkb = PH(nc.sbuf_tensor("wkb", [128, 2 * DC * 128], BF16, side="right"))
                absb = PH(nc.sbuf_tensor("absb", [128, 2 * 512], BF16, side="right"))
                kstage = PH(nc.sbuf_tensor("kstage", [128, 2 * SEQ], BF16, side="right"))
                psum_k = [PH(nc.psum_tensor(f"pk{i}", [128, 512], F32)) for i in range(4)]
                psum_sk = PH(nc.psum_tensor("psk", [128, 1], F32))
                psum_nk = [PH(nc.psum_tensor(f"pnk{i}", [1, 512], F32)) for i in range(2)]

                def kst(g):
                    return kstage[:, (g % 2) * SEQ:((g % 2) + 1) * SEQ]

                with nc.Block("P1K", no_gpsimd_drain=True) as block:
                    @block.sync
                    def _(s):
                        for g in range(NG):
                            if g >= 2:
                                s.wait_ge(k_abs, 9 * (g - 1))
                            s.dma_start(wkf[:, (g % 2) * D:((g % 2) + 1) * D],
                                        wk_t[g]).then_inc(k_w[g % 2], 16)
                            if g >= 1:
                                s.wait_ge(k_ev, 4 * g)
                                s.dma_start(k_s[g - 1], kst(g - 1)).then_inc(k_st[(g - 1) % 2], 16)
                        s.wait_ge(k_ev, 4 * NG)
                        s.dma_start(k_s[NG - 1], kst(NG - 1)).then_inc(k_st[(NG - 1) % 2], 16)
                        s.wait_ge(k_st[0], 16 * (NG // 2))
                        s.wait_ge(k_st[1], 16 * (NG // 2))

                    @block.scalar
                    def _(sc):
                        for g in range(NG):
                            p = g % 2
                            sc.wait_ge(k_w[g % 2], 16 * (g // 2 + 1))
                            if g >= 2:
                                sc.wait_ge(k_pe, 4 * (g - 1))
                            sc.activation(wkb[:, p * D:(p + 1) * D], wkf[:, p * D:(p + 1) * D],
                                          AF.Sign).then_inc(k_abs, 1)
                            for c in range(8):
                                if 8 * g + c >= 2:
                                    sc.wait_ge(k_spe, 8 * g + c - 1)
                                sc.activation(absb[:, (c % 2) * 512:((c % 2) + 1) * 512],
                                              wkf[:, p * D + c * 512: p * D + (c + 1) * 512],
                                              AF.Abs).then_inc(k_abs, 1)
                            for j in range(4):
                                sc.wait_ge(k_dv, 2 * (g + 1))
                                sc.wait_ge(k_pe, 4 * g + j + 1)
                                if g >= 2 and j == 0:
                                    sc.wait_ge(k_st[g % 2], 16 * (g // 2))
                                sc.activation(kstage[:, p * SEQ + j * 512: p * SEQ + (j + 1) * 512],
                                              psum_k[(4 * g + j) % 4][:], AF.Copy,
                                              scale=kvec_sb[:, p:p + 1]).then_inc(k_ev, 1)
                            if g >= 1:
                                sc.wait_ge(k_nkpe, 4 * g)
                            sc.wait_ge(k_ev, 4 * (g + 1))
                            for j in range(4):
                                sc.activation(scr_a[:, j * 512:(j + 1) * 512],
                                              kstage[:, p * SEQ + j * 512: p * SEQ + (j + 1) * 512],
                                              AF.Square).then_inc(k_sq, 1)

                    @block.tensor
                    def _(t):
                        def emit_nk(gg):
                            for j in range(4):
                                jj4 = 4 * gg + j
                                if jj4 >= 2:
                                    t.wait_ge(k_dvnk, jj4 - 1)
                                t.wait_ge(k_sq, jj4 + 1)
                                t.matmul(psum_nk[jj4 % 2][:], ones_col[:],
                                         scr_a[:, j * 512:(j + 1) * 512],
                                         start=True, stop=True).then_inc(k_nkpe, 1)

                        for g in range(NG):
                            p = g % 2
                            t.wait_ge(k_abs, 9 * g + 1)
                            for j in range(4):
                                t.wait_ge(s0_x[0 if j < 2 else 1], 16)
                                if 4 * g + j >= 4:
                                    t.wait_ge(k_ev, 4 * g + j - 3)
                                for dc in range(DC):
                                    ins = t.matmul(psum_k[(4 * g + j) % 4][:],
                                                   wkb[:, p * D + dc * 128: p * D + (dc + 1) * 128],
                                                   xall(dc, j),
                                                   start=(dc == 0), stop=(dc == DC - 1))
                                ins.then_inc(k_pe, 1)
                            if g >= 1:
                                t.wait_ge(k_dv, 2 * g - 1)
                            for c in range(8):
                                t.wait_ge(k_abs, 9 * g + c + 2)
                                for jj in range(4):
                                    dc = 4 * c + jj
                                    ins = t.matmul(psum_sk[:],
                                                   absb[:, (c % 2) * 512 + jj * 128:(c % 2) * 512 + (jj + 1) * 128],
                                                   ones_col[:],
                                                   start=(dc == 0), stop=(dc == DC - 1))
                                ins.then_inc(k_spe, 1)
                            if g >= 1:
                                emit_nk(g - 1)
                        emit_nk(NG - 1)

                    @block.vector
                    def _(v):
                        for g in range(NG):
                            p = g % 2
                            v.wait_ge(k_spe, 8 * (g + 1))
                            if g >= 2:
                                v.wait_ge(k_dv, 2 * (g - 1))
                            v.tensor_scalar(scol_tmp[:, p:p + 1], psum_sk[:], INV4096, 1e-8,
                                            ALU.mult, ALU.max).then_inc(k_dv, 1)
                            v.wait_ge(k_dv, 2 * g + 1)
                            if g >= 2:
                                v.wait_ge(k_ev, 4 * (g - 1))
                            v.tensor_tensor(kvec_sb[:, p:p + 1], scol_tmp[:, p:p + 1],
                                            ksc_vec[:, g:g + 1], ALU.mult).then_inc(k_dv, 1)
                            for j in range(4):
                                jj4 = 4 * g + j
                                v.wait_ge(k_nkpe, jj4 + 1)
                                if j == 0 and g >= 1:
                                    v.wait_ge(k_dvnk2, g)
                                v.reduce_max(Mk2t_sb[0:1, j:j + 1],
                                             psum_nk[jj4 % 2][:], axis=AX.X).then_inc(k_dvnk, 1)
                            v.wait_ge(k_dvnk, 4 * (g + 1))
                            v.tensor_reduce(Mk2_sb[0:1, g:g + 1], Mk2t_sb[0:1, 0:4], axis=AX.X,
                                            op=ALU.max).then_inc(k_dvnk2, 1)

            with ExitStack() as ph:
                PH = ph.enter_context
                wvf = PH(nc.sbuf_tensor("wvf", [128, 2 * KV], F32, side="right"))
                wvh = PH(nc.sbuf_tensor("wvh", [128, DC * 512], BF16, side="right"))
                vabs = PH(nc.sbuf_tensor("vabs", [128, 2 * KV], BF16, side="right"))
                vstage = PH(nc.sbuf_tensor("vstage", [128, 2 * 512], BF16, side="right"))
                psum_v = [PH(nc.psum_tensor(f"pv{i}", [128, 512], F32)) for i in range(2)]
                psum_sv = PH(nc.psum_tensor("psv", [1, KV], F32))
                psum_vb = PH(nc.psum_tensor("pvb", [128, 1024], F32))

                def vsgn_count(i):
                    return 2 * (i + 1) if i < DC else 2 * DC + (i - DC) + 1

                with nc.Block("P1V", no_gpsimd_drain=True) as block:
                    @block.sync
                    def _(s):
                        for half in range(2):
                            for dc in range(DC):
                                i = half * DC + dc
                                if i >= 2:
                                    s.wait_ge(v_sgn, vsgn_count(i - 2))
                                s.dma_start(wvf[:, (i % 2) * KV:((i % 2) + 1) * KV],
                                            wv_t[dc]).then_inc(v_w[i % 2], 16)

                    @block.gpsimd
                    def _(g):
                        for it in range(32):
                            g.wait_ge(v_ev, it + 1)
                            half, tc = it // 16, it % 16
                            g.dma_start(v_s[tc, :, half * 512:(half + 1) * 512],
                                        vstage[:, (it % 2) * 512:((it % 2) + 1) * 512]).then_inc(v_st[it % 2], 16)
                        g.wait_ge(v_st[0], 16 * 16)
                        g.wait_ge(v_st[1], 16 * 16)

                    @block.scalar
                    def _(sc):
                        for half in range(2):
                            if half == 1:
                                sc.wait_ge(v_pe, 16)
                            for dc in range(DC):
                                i = half * DC + dc
                                sc.wait_ge(v_w[i % 2], 16 * (i // 2 + 1))
                                sc.activation(wvh[:, dc * 512:(dc + 1) * 512],
                                              wvf[:, (i % 2) * KV + half * 512:(i % 2) * KV + (half + 1) * 512],
                                              AF.Sign).then_inc(v_sgn, 1)
                                if half == 0:
                                    if dc >= 2:
                                        sc.wait_ge(v_svpe, 2 * (dc - 1))
                                    sc.activation(vabs[:, (dc % 2) * KV:((dc % 2) + 1) * KV],
                                                  wvf[:, (i % 2) * KV:((i % 2) + 1) * KV],
                                                  AF.Abs).then_inc(v_sgn, 1)
                            if half == 0:
                                sc.wait_ge(v_svpe, 2 * DC)
                                sc.activation(vrow_sb[:], psum_sv[:], AF.Copy).then_inc(v_sve, 1)
                                sc.wait_ge(v_bpe, 2)
                                sc.activation(scr_b[:, 0:512], psum_vb[:, 0:512], AF.Copy).then_inc(v_bev, 1)
                                sc.activation(scr_b[:, 512:1024], psum_vb[:, 512:1024], AF.Copy).then_inc(v_bev, 1)

                    @block.tensor
                    def _(t):
                        t.wait_ge(s0_x[0], 16)
                        t.wait_ge(s0_x[1], 16)
                        for half in range(2):
                            if half == 0:
                                for dc in range(DC):
                                    t.wait_ge(v_sgn, 2 * (dc + 1))
                                    for j in range(2):
                                        t.matmul(psum_sv[0:1, j * 512:(j + 1) * 512], ones_col[:],
                                                 vabs[:, (dc % 2) * KV + j * 512:(dc % 2) * KV + (j + 1) * 512],
                                                 start=(dc == 0), stop=(dc == DC - 1)).then_inc(v_svpe, 1)
                                t.wait_ge(v_dvr, 2)
                                for j in range(2):
                                    t.matmul(psum_vb[:, j * 512:(j + 1) * 512], ones_rowf[:],
                                             vrow_sb[0:1, j * 512:(j + 1) * 512],
                                             start=True, stop=True).then_inc(v_bpe, 1)
                            for tc in range(16):
                                it = half * 16 + tc
                                if it >= 2:
                                    t.wait_ge(v_ev, it - 1)
                                for dc in range(DC):
                                    if tc == 0:
                                        t.wait_ge(v_sgn, (2 * dc + 1) if half == 0 else (2 * DC + dc + 1))
                                    ins = t.matmul(psum_v[it % 2][:], xtile_all(dc, tc),
                                                   wvh[:, dc * 512:(dc + 1) * 512],
                                                   start=(dc == 0), stop=(dc == DC - 1))
                                ins.then_inc(v_pe, 1)

                    @block.vector
                    def _(v):
                        v.wait_ge(v_sve, 1)
                        v.wait_ge(s0_c, 176)
                        v.tensor_scalar(vrow_sb[:], vrow_sb[:], INV4096, 1e-8, ALU.mult, ALU.max).then_inc(v_dvr, 1)
                        v.wait_ge(v_dvr, 1)
                        v.tensor_tensor(vrow_sb[:], vrow_sb[:], vsc_row[:], ALU.mult).then_inc(v_dvr, 1)
                        for it in range(32):
                            half = it // 16
                            v.wait_ge(v_pe, it + 1)
                            if it == 0:
                                v.wait_ge(v_bev, 2)
                            if it >= 2:
                                v.wait_ge(v_st[it % 2], 16 * (it // 2))
                            v.tensor_tensor(vstage[:, (it % 2) * 512:((it % 2) + 1) * 512],
                                            psum_v[it % 2][:],
                                            scr_b[:, half * 512:(half + 1) * 512],
                                            ALU.mult).then_inc(v_ev, 1)

        with ExitStack() as scA:
            A = scA.enter_context(nc.sbuf_tensor("A", [128, 33 * TOK], BF16))
            Kg = scA.enter_context(nc.sbuf_tensor("Kg", [128, 2 * SEQ], BF16, side="right"))
            Vg = scA.enter_context(nc.sbuf_tensor("Vg", [128, 2 * SEQ], BF16, side="right"))

            def q_slot(h):
                return A[:, h * TOK:(h + 1) * TOK]

            def att_slot(h):
                s = 32 if h == 0 else h - 1
                return A[:, s * TOK:(s + 1) * TOK]

            with ExitStack() as ph:
                PH = ph.enter_context
                wqf = PH(nc.sbuf_tensor("wqf", [128, 2 * DC * 128], F32, side="right"))
                wqb = PH(nc.sbuf_tensor("wqb", [128, 2 * DC * 128], BF16, side="right"))
                qabsb = PH(nc.sbuf_tensor("qabsb", [128, 2 * 512], BF16, side="right"))
                psum_q = [PH(nc.psum_tensor(f"pq{i}", [128, 512], F32)) for i in range(4)]
                psum_sq = PH(nc.psum_tensor("psq", [128, 1], F32))
                psum_tq = PH(nc.psum_tensor("ptq", [128, 1], F32))

                dvc_after_max = {}
                dvc_after_neg = {}
                cnt = 0
                for h2 in range(NH + 2):
                    if 1 <= h2 <= NH:
                        cnt += 1
                        dvc_after_max[h2 - 1] = cnt
                    if 2 <= h2 <= NH + 1:
                        cnt += 1
                        dvc_after_neg[h2 - 2] = cnt
                cev_after_sqrt = {}
                cev_after_cbe = {}
                cnt = 0
                for h2 in range(NH + 2):
                    if 1 <= h2 <= NH:
                        cnt += 1
                        cev_after_sqrt[h2 - 1] = cnt
                    if 2 <= h2 <= NH + 1:
                        cnt += 1
                        cev_after_cbe[h2 - 2] = cnt
                psum_nq = PH(nc.psum_tensor("pnq", [1, TOK], F32))

                with nc.Block("P2", no_gpsimd_drain=True) as block:
                    @block.sync
                    def _(s):
                        for h in range(NH):
                            if h >= 2:
                                s.wait_ge(q_abs, 9 * (h - 1))
                            s.dma_start(wqf[:, (h % 2) * D:((h % 2) + 1) * D],
                                        wq_t[h]).then_inc(q_w[h % 2], 16)
                        for g in range(2):
                            s.dma_start(Kg[:, (g % 2) * SEQ:((g % 2) + 1) * SEQ],
                                        k_s[g]).then_inc(a_k[g % 2], 16)
                            s.dma_start(
                                Vg[:, (g % 2) * SEQ:((g % 2) + 1) * SEQ].rearrange("p (k c) -> p k c", k=16),
                                v_s[:, :, g * 128:(g + 1) * 128].rearrange("k p c -> p k c"),
                            ).then_inc(a_v[g % 2], 16)

                    @block.scalar
                    def _(sc):
                        def emit_sqrt(hh2):
                            gg = hh2 // 4
                            sc.wait_ge(q_dvc, dvc_after_max[hh2])
                            if hh2 >= 2:
                                sc.wait_ge(q_dvc, dvc_after_neg[hh2 - 2])
                            sc.activation(cb_c[0:1, hh2 % 2:hh2 % 2 + 1],
                                          nqx_sb[0:1, hh2 % 2:hh2 % 2 + 1], AF.Sqrt,
                                          scale=Mk2_sb[0:1, gg:gg + 1]).then_inc(q_cev, 1)

                        def emit_cbe(hh2):
                            sc.wait_ge(q_cpe, hh2 + 1)
                            sc.activation(cbias[:, hh2:hh2 + 1], psum_tq[:], AF.Copy).then_inc(q_cev, 1)

                        for h in range(NH):
                            p = h % 2
                            sc.wait_ge(q_w[h % 2], 16 * (h // 2 + 1))
                            if h >= 2:
                                sc.wait_ge(q_pe, 2 * (h - 1))
                            sc.activation(wqb[:, p * D:(p + 1) * D], wqf[:, p * D:(p + 1) * D],
                                          AF.Sign).then_inc(q_abs, 1)
                            for c in range(8):
                                if 8 * h + c >= 2:
                                    sc.wait_ge(q_spe, 8 * h + c - 1)
                                sc.activation(qabsb[:, (c % 2) * 512:((c % 2) + 1) * 512],
                                              wqf[:, p * D + c * 512: p * D + (c + 1) * 512],
                                              AF.Abs).then_inc(q_abs, 1)
                            for j in range(2):
                                sc.wait_ge(q_dv, 2 * (h + 1))
                                sc.wait_ge(q_pe, 2 * h + j + 1)
                                sc.activation(q_slot(h)[:, j * 512:(j + 1) * 512],
                                              psum_q[(2 * h + j) % 4][:], AF.Copy,
                                              scale=qvec_sb[:, p:p + 1]).then_inc(q_ev, 1)
                            if h >= 1:
                                sc.wait_ge(q_nqpe, h)
                            sc.wait_ge(q_ev, 2 * (h + 1))
                            for j in range(2):
                                sc.activation(scr_a[:, j * 512:(j + 1) * 512],
                                              q_slot(h)[:, j * 512:(j + 1) * 512],
                                              AF.Square).then_inc(q_sq, 1)
                            if h >= 1:
                                emit_sqrt(h - 1)
                            if h >= 2:
                                emit_cbe(h - 2)
                        emit_sqrt(NH - 1)
                        emit_cbe(NH - 2)
                        emit_cbe(NH - 1)

                    @block.tensor
                    def _(t):
                        def emit_nq(hh2):
                            if hh2 >= 1:
                                t.wait_ge(q_dvc, dvc_after_max[hh2 - 1])
                            for j in range(2):
                                t.wait_ge(q_sq, 2 * hh2 + j + 1)
                                ins = t.matmul(psum_nq[0:1, j * 512:(j + 1) * 512], ones_col[:],
                                               scr_a[:, j * 512:(j + 1) * 512],
                                               start=True, stop=True)
                                if j == 1:
                                    ins.then_inc(q_nqpe, 1)

                        def emit_bc(hh2):
                            t.wait_ge(q_dvc, dvc_after_neg[hh2])
                            if hh2 >= 1:
                                t.wait_ge(q_cev, cev_after_cbe[hh2 - 1])
                            t.matmul(psum_tq[:], ones_rowf[:], cb_n[0:1, hh2 % 2:hh2 % 2 + 1],
                                     start=True, stop=True).then_inc(q_cpe, 1)

                        for h in range(NH):
                            p = h % 2
                            t.wait_ge(q_abs, 9 * h + 1)
                            for j in range(2):
                                n = 2 * h + j
                                if n >= 4:
                                    t.wait_ge(q_ev, n - 3)
                                for dc in range(DC):
                                    ins = t.matmul(psum_q[n % 4][:],
                                                   wqb[:, p * D + dc * 128: p * D + (dc + 1) * 128],
                                                   x_sb[:, dc * TOK + j * 512: dc * TOK + (j + 1) * 512],
                                                   start=(dc == 0), stop=(dc == DC - 1))
                                ins.then_inc(q_pe, 1)
                            if h >= 1:
                                t.wait_ge(q_dv, 2 * h - 1)
                            for c in range(8):
                                t.wait_ge(q_abs, 9 * h + c + 2)
                                for jj in range(4):
                                    dc = 4 * c + jj
                                    ins = t.matmul(psum_sq[:],
                                                   qabsb[:, (c % 2) * 512 + jj * 128:(c % 2) * 512 + (jj + 1) * 128],
                                                   ones_col[:],
                                                   start=(dc == 0), stop=(dc == DC - 1))
                                ins.then_inc(q_spe, 1)
                            if h >= 1:
                                emit_nq(h - 1)
                            if h >= 2:
                                emit_bc(h - 2)
                        emit_nq(NH - 1)
                        emit_bc(NH - 2)
                        emit_bc(NH - 1)

                    @block.vector
                    def _(v):
                        for h in range(NH + 2):
                            if h < NH:
                                p = h % 2
                                v.wait_ge(q_spe, 8 * (h + 1))
                                if h >= 2:
                                    v.wait_ge(q_dv, 2 * (h - 1))
                                v.tensor_scalar(scol_tmp[:, p:p + 1], psum_sq[:], INV4096, 1e-8,
                                                ALU.mult, ALU.max).then_inc(q_dv, 1)
                                v.wait_ge(q_dv, 2 * h + 1)
                                if h >= 2:
                                    v.wait_ge(q_ev, 2 * (h - 1))
                                v.scalar_tensor_tensor(qvec_sb[:, p:p + 1], scol_tmp[:, p:p + 1],
                                                       INV_SQRT_HD, qsc_vec[:, h:h + 1],
                                                       ALU.mult, ALU.mult).then_inc(q_dv, 1)
                            if 1 <= h <= NH:
                                hm = h - 1
                                v.wait_ge(q_nqpe, hm + 1)
                                if hm >= 2:
                                    v.wait_ge(q_cev, cev_after_sqrt[hm - 2])
                                v.reduce_max(nqx_sb[0:1, hm % 2:hm % 2 + 1], psum_nq[:],
                                             axis=AX.X).then_inc(q_dvc, 1)
                            if 2 <= h <= NH + 1:
                                hn = h - 2
                                v.wait_ge(q_cev, cev_after_sqrt[hn])
                                if hn >= 2:
                                    v.wait_ge(q_cpe, hn - 1)
                                v.tensor_scalar_mul(cb_n[0:1, hn % 2:hn % 2 + 1],
                                                    cb_c[0:1, hn % 2:hn % 2 + 1], -1.0).then_inc(q_dvc, 1)

            if debug:
                with nc.Block("DBGQ", no_gpsimd_drain=True) as block:
                    @block.sync
                    def _(s):
                        s.dma_start(dbg_q[:], A[:]).then_inc(dbg_s, 16)
                        s.wait_ge(dbg_s, 16)

            wof = scA.enter_context(nc.sbuf_tensor("wof", [128, 2 * DC * 128], F32, side="right"))

            with ExitStack() as ph:
                PH = ph.enter_context
                rr_sb = PH(nc.sbuf_tensor("rr_sb", [1, 2 * 512], F32, side="right"))
                ptbuf = PH(nc.sbuf_tensor("ptbuf", [128, 2 * 512], BF16, side="right"))
                psum_st = [PH(nc.psum_tensor(f"pst{i}", [128, 512], F32)) for i in range(2)]
                psum_av = [PH(nc.psum_tensor(f"pav{i}", [128, 512], F32)) for i in range(2)]
                psum_r = [PH(nc.psum_tensor(f"pr{i}", [1, 512], F32)) for i in range(2)]
                psum_bc = PH(nc.psum_tensor("pbc", [128, 512], F32))

                with nc.Block("P3", no_gpsimd_drain=True) as block:
                    @block.sync
                    def _(s):
                        for g in range(2, NG):
                            s.wait_ge(a_pe, 128 * (g - 1))
                            s.dma_start(Kg[:, (g % 2) * SEQ:((g % 2) + 1) * SEQ], k_s[g]).then_inc(a_k[g % 2], 16)
                            s.wait_ge(a_pu, 128 * (g - 1))
                            s.dma_start(
                                Vg[:, (g % 2) * SEQ:((g % 2) + 1) * SEQ].rearrange("p (k c) -> p k c", k=16),
                                v_s[:, :, g * 128:(g + 1) * 128].rearrange("k p c -> p k c"),
                            ).then_inc(a_v[g % 2], 16)
                        for oc in range(2):
                            s.wait_ge(a_pe, 128 * (NG - 1))
                            s.dma_start(wof[:, (oc % 2) * D:((oc % 2) + 1) * D],
                                        wo_t[oc]).then_inc(o_w[oc % 2], 16)

                    @block.tensor
                    def _(t):
                        def emit_bc(itb):
                            t.wait_ge(a_rr, itb + 1)
                            if itb >= 1:
                                t.wait_ge(a_bce, itb)
                            t.matmul(psum_bc[:], ones_rowf[:],
                                     rr_sb[0:1, (itb % 2) * 512:((itb % 2) + 1) * 512],
                                     start=True, stop=True).then_inc(a_pb, 1)

                        for g in range(NG):
                            gp = g % 2
                            for hh in range(4):
                                h = g * 4 + hh
                                for qs in range(2):
                                    it = h * 2 + qs
                                    for kc in range(17):
                                        if kc < 16:
                                            n = it * 16 + kc
                                            if kc == 0:
                                                t.wait_ge(a_k[g % 2], 16 * (g // 2 + 1))
                                                t.wait_ge(a_v[g % 2], 16 * (g // 2 + 1))
                                            if n >= 2:
                                                t.wait_ge(a_pt, n - 1)
                                            t.matmul(psum_st[kc % 2][:],
                                                     Kg[:, gp * SEQ + kc * 128: gp * SEQ + (kc + 1) * 128],
                                                     q_slot(h)[:, qs * 512:(qs + 1) * 512],
                                                     start=True, stop=True).then_inc(a_pe, 1)
                                        if kc == 2 and it >= 1:
                                            emit_bc(it - 1)
                                        if kc >= 1:
                                            km = kc - 1
                                            nm = it * 16 + km
                                            t.wait_ge(a_pt, nm + 1)
                                            if km == 0 and it >= 2:
                                                t.wait_ge(a_rr, it - 1)
                                            ins_r = t.matmul(psum_r[it % 2][:], ones_col[:],
                                                             ptbuf[:, (km % 2) * 512:((km % 2) + 1) * 512],
                                                             start=(km == 0), stop=(km == 15))
                                            if km == 15:
                                                ins_r.then_inc(a_pr, 1)
                                            if km == 0 and it >= 2:
                                                t.wait_ge(a_at, it - 1)
                                            t.matmul(psum_av[it % 2][:],
                                                     Vg[:, gp * SEQ + km * 128: gp * SEQ + (km + 1) * 128],
                                                     ptbuf[:, (km % 2) * 512:((km % 2) + 1) * 512],
                                                     start=(km == 0), stop=(km == 15)).then_inc(a_pu, 1)
                        emit_bc(63)

                    @block.scalar
                    def _(sc):
                        def emit_bcevac(itb):
                            sc.wait_ge(a_pb, itb + 1)
                            if itb >= 2:
                                sc.wait_ge(a_at, itb - 1)
                            sc.activation(scr_b[:, (itb % 2) * 512:((itb % 2) + 1) * 512],
                                          psum_bc[:], AF.Copy).then_inc(a_bce, 1)

                        for it in range(64):
                            for kc in range(16):
                                n = it * 16 + kc
                                sc.wait_ge(a_pe, n + 1)
                                if n >= 2:
                                    sc.wait_ge(a_pu, n - 1)
                                sc.activation(ptbuf[:, (kc % 2) * 512:((kc % 2) + 1) * 512],
                                              psum_st[kc % 2][:], AF.Exp,
                                              bias=cbias[:, (it // 2):(it // 2) + 1]).then_inc(a_pt, 1)
                                if kc == 3 and it >= 1:
                                    emit_bcevac(it - 1)
                        emit_bcevac(63)

                    @block.vector
                    def _(v):
                        for it in range(64):
                            h = it // 2
                            qs = it % 2
                            v.wait_ge(a_pr, it + 1)
                            if it >= 2:
                                v.wait_ge(a_pb, it - 1)
                            v.reciprocal(rr_sb[0:1, (it % 2) * 512:((it % 2) + 1) * 512],
                                         psum_r[it % 2][:]).then_inc(a_rr, 1)
                            v.wait_ge(a_bce, it + 1)
                            v.wait_ge(a_pu, 16 * (it + 1))
                            v.tensor_tensor(att_slot(h)[:, qs * 512:(qs + 1) * 512],
                                            psum_av[it % 2][:],
                                            scr_b[:, (it % 2) * 512:((it % 2) + 1) * 512],
                                            ALU.mult).then_inc(a_at, 1)

            with ExitStack() as ph:
                PH = ph.enter_context
                wob = PH(nc.sbuf_tensor("wob", [128, 2 * DC * 128], BF16, side="right"))
                oabsb = PH(nc.sbuf_tensor("oabsb", [128, 2 * 512], BF16, side="right"))
                psum_o = [PH(nc.psum_tensor(f"po{i}", [128, 512], F32)) for i in range(4)]
                psum_so = PH(nc.psum_tensor("pso", [128, 1], F32))

                with nc.Block("P4a", no_gpsimd_drain=True) as block:
                    @block.sync
                    def _(s):
                        for oc in range(2, DC):
                            s.wait_ge(o_abs, 9 * (oc - 1))
                            s.dma_start(wof[:, (oc % 2) * D:((oc % 2) + 1) * D],
                                        wo_t[oc]).then_inc(o_w[oc % 2], 16)

                    @block.scalar
                    def _(sc):
                        for oc in range(DC):
                            p = oc % 2
                            sc.wait_ge(o_w[oc % 2], 16 * (oc // 2 + 1))
                            if oc >= 2:
                                sc.wait_ge(o_pe, 2 * (oc - 1))
                            sc.activation(wob[:, p * D:(p + 1) * D], wof[:, p * D:(p + 1) * D],
                                          AF.Sign).then_inc(o_abs, 1)
                            for c in range(8):
                                if 8 * oc + c >= 2:
                                    sc.wait_ge(o_spe, 8 * oc + c - 1)
                                sc.activation(oabsb[:, (c % 2) * 512:((c % 2) + 1) * 512],
                                              wof[:, p * D + c * 512: p * D + (c + 1) * 512],
                                              AF.Abs).then_inc(o_abs, 1)

                    @block.tensor
                    def _(t):
                        for oc in range(DC):
                            p = oc % 2
                            t.wait_ge(o_abs, 9 * oc + 1)
                            for qs in range(2):
                                n = 2 * oc + qs
                                if n >= 4:
                                    t.wait_ge(o_ev, n - 3)
                                for hc in range(DC):
                                    ins = t.matmul(psum_o[n % 4][:],
                                                   wob[:, p * D + hc * 128: p * D + (hc + 1) * 128],
                                                   att_slot(hc)[:, qs * 512:(qs + 1) * 512],
                                                   start=(hc == 0), stop=(hc == DC - 1))
                                ins.then_inc(o_pe, 1)
                            if oc >= 1:
                                t.wait_ge(o_dv, 2 * oc - 1)
                            for c in range(8):
                                t.wait_ge(o_abs, 9 * oc + c + 2)
                                for jj in range(4):
                                    hc = 4 * c + jj
                                    ins = t.matmul(psum_so[:],
                                                   oabsb[:, (c % 2) * 512 + jj * 128:(c % 2) * 512 + (jj + 1) * 128],
                                                   ones_col[:],
                                                   start=(hc == 0), stop=(hc == DC - 1))
                                ins.then_inc(o_spe, 1)

                    @block.vector
                    def _(v):
                        for oc in range(DC):
                            p = oc % 2
                            v.wait_ge(o_spe, 8 * (oc + 1))
                            if oc >= 2:
                                v.wait_ge(o_dv, 2 * (oc - 1))
                            v.tensor_scalar(scol_tmp[:, p:p + 1], psum_so[:], INV4096, 1e-8,
                                            ALU.mult, ALU.max).then_inc(o_dv, 1)
                            v.wait_ge(o_dv, 2 * oc + 1)
                            v.tensor_tensor(ovec_sb[:, p:p + 1], scol_tmp[:, p:p + 1],
                                            osc_vec[:, oc:oc + 1], ALU.mult).then_inc(o_dv, 1)
                            v.wait_ge(o_dv, 2 * (oc + 1))
                            for qs in range(2):
                                n = 2 * oc + qs
                                v.wait_ge(o_pe, n + 1)
                                tgt = x_sb[:, oc * TOK + qs * 512: oc * TOK + (qs + 1) * 512]
                                v.scalar_tensor_tensor(tgt, psum_o[n % 4][:], ovec_sb[:, p:p + 1], tgt,
                                                       ALU.mult, ALU.add).then_inc(o_ev, 1)

        def emit_ln(name, span_lo, span_w, to_out, sq, pe, ev, dv, bpe, bev, nrm, base,
                    store_sem=None, tbi=0, prefetch=None):
            nspan = span_w // 512
            with ExitStack() as ph:
                PH = ph.enter_context
                psum_s1 = PH(nc.psum_tensor(f"ps1{name}", [1, span_w], F32))
                psum_s2 = PH(nc.psum_tensor(f"ps2{name}", [1, span_w], F32))
                psum_ab = PH(nc.psum_tensor(f"pab{name}", [128, 2 * span_w], F32))
                ln_r1 = PH(nc.sbuf_tensor(f"lr1{name}", [1, span_w], F32))
                ln_r2 = PH(nc.sbuf_tensor(f"lr2{name}", [1, span_w], F32))
                ln_mu = ln_r1
                ln_a = PH(nc.sbuf_tensor(f"la{name}", [1, span_w], F32))
                ln_b = PH(nc.sbuf_tensor(f"lb{name}", [1, span_w], F32))
                ln_tmp = PH(nc.sbuf_tensor(f"lt{name}", [1, span_w], F32))
                ln_sd = ln_tmp
                a_bc = PH(nc.sbuf_tensor(f"abc{name}", [128, span_w], BF16))
                b_bc = PH(nc.sbuf_tensor(f"bbc{name}", [128, span_w], BF16))

                NDV = 7 + DC

                with nc.Block(name, no_gpsimd_drain=True) as block:
                    if prefetch is not None:
                        @block.gpsimd
                        def _(g):
                            prefetch(g)

                    @block.scalar
                    def _(sc):
                        B = base * NDV
                        if base >= 1:
                            sc.wait_ge(dv, B)
                        for dc in range(DC):
                            if dc >= 2:
                                sc.wait_ge(pe, base * DC + dc - 1)
                            sc.activation(scr_a[:, (dc % 2) * span_w:((dc % 2) + 1) * span_w],
                                          x_sb[:, dc * TOK + span_lo: dc * TOK + span_lo + span_w],
                                          AF.Square).then_inc(sq, 1)
                        sc.wait_ge(pe, base * DC + DC)
                        sc.activation(ln_r1[0:1, 0:span_w], psum_s1[:], AF.Copy).then_inc(ev, 1)
                        sc.activation(ln_r2[0:1, 0:span_w], psum_s2[:], AF.Copy).then_inc(ev, 1)
                        sc.wait_ge(dv, B + 5)
                        sc.activation(ln_sd[0:1, 0:span_w], ln_r2[0:1, 0:span_w], AF.Sqrt).then_inc(ev, 1)
                        sc.wait_ge(bpe, base * 2 * nspan + 2 * nspan)
                        for j in range(nspan):
                            sc.activation(a_bc[:, j * 512:(j + 1) * 512],
                                          psum_ab[:, j * 512:(j + 1) * 512], AF.Copy).then_inc(bev, 1)
                            sc.activation(b_bc[:, j * 512:(j + 1) * 512],
                                          psum_ab[:, span_w + j * 512: span_w + (j + 1) * 512],
                                          AF.Copy).then_inc(bev, 1)

                    @block.tensor
                    def _(t):
                        B = base * NDV
                        for dc in range(DC):
                            t.wait_ge(sq, base * DC + dc + 1)
                            for j in range(nspan):
                                t.matmul(psum_s1[0:1, j * 512:(j + 1) * 512], ones_col[:],
                                         x_sb[:, dc * TOK + span_lo + j * 512:
                                              dc * TOK + span_lo + (j + 1) * 512],
                                         start=(dc == 0), stop=(dc == DC - 1))
                                ins = t.matmul(psum_s2[0:1, j * 512:(j + 1) * 512], ones_col[:],
                                               scr_a[:, (dc % 2) * span_w + j * 512:
                                                     (dc % 2) * span_w + (j + 1) * 512],
                                               start=(dc == 0), stop=(dc == DC - 1))
                            ins.then_inc(pe, 1)
                        t.wait_ge(dv, B + 7)
                        for j in range(nspan):
                            t.matmul(psum_ab[:, j * 512:(j + 1) * 512], ones_rowf[:],
                                     ln_a[0:1, j * 512:(j + 1) * 512], start=True, stop=True).then_inc(bpe, 1)
                            t.matmul(psum_ab[:, span_w + j * 512: span_w + (j + 1) * 512], ones_rowf[:],
                                     ln_b[0:1, j * 512:(j + 1) * 512], start=True, stop=True).then_inc(bpe, 1)

                    @block.vector
                    def _(v):
                        B = base * NDV
                        v.wait_ge(ev, base * 3 + 2)
                        v.tensor_scalar_mul(ln_mu[0:1, 0:span_w], ln_r1[0:1, 0:span_w], INV4096).then_inc(dv, 1)
                        v.tensor_scalar_mul(ln_r2[0:1, 0:span_w], ln_r2[0:1, 0:span_w], INV4096).then_inc(dv, 1)
                        v.wait_ge(dv, B + 1)
                        v.tensor_tensor(ln_tmp[0:1, 0:span_w], ln_mu[0:1, 0:span_w],
                                        ln_mu[0:1, 0:span_w], ALU.mult).then_inc(dv, 1)
                        v.wait_ge(dv, B + 3)
                        v.tensor_tensor(ln_r2[0:1, 0:span_w], ln_r2[0:1, 0:span_w],
                                        ln_tmp[0:1, 0:span_w], ALU.subtract).then_inc(dv, 1)
                        v.wait_ge(dv, B + 4)
                        v.tensor_scalar_add(ln_r2[0:1, 0:span_w], ln_r2[0:1, 0:span_w], EPS).then_inc(dv, 1)
                        v.wait_ge(ev, base * 3 + 3)
                        v.reciprocal(ln_a[0:1, 0:span_w], ln_sd[0:1, 0:span_w]).then_inc(dv, 1)
                        v.wait_ge(dv, B + 6)
                        v.scalar_tensor_tensor(ln_b[0:1, 0:span_w], ln_mu[0:1, 0:span_w], -1.0,
                                               ln_a[0:1, 0:span_w], ALU.mult, ALU.mult).then_inc(dv, 1)
                        v.wait_ge(bev, base * 2 * nspan + 2 * nspan)
                        for dc in range(DC):
                            src = x_sb[:, dc * TOK + span_lo: dc * TOK + span_lo + span_w]
                            tt = scr_a[:, (dc % 2) * span_w:((dc % 2) + 1) * span_w]
                            if dc >= 2:
                                v.wait_ge(nrm, base * DC + dc - 1)
                            v.tensor_tensor(tt, src, a_bc[:, 0:span_w], ALU.mult).then_inc(dv, 1)
                            v.wait_ge(dv, B + 7 + dc + 1)
                            if to_out:
                                stg = scr_b[:, (dc % 2) * 512:((dc % 2) + 1) * 512]
                                gdc = tbi * DC + dc
                                if gdc >= 2:
                                    v.wait_ge(store_sem[gdc % 2], 16 * (gdc // 2))
                                v.tensor_tensor(stg, tt, b_bc[:, 0:span_w], ALU.add).then_inc(nrm, 1)
                            else:
                                v.tensor_tensor(src, tt, b_bc[:, 0:span_w], ALU.add).then_inc(nrm, 1)

                    if to_out:
                        @block.sync
                        def _(s):
                            for dc in range(DC):
                                s.wait_ge(nrm, base * DC + dc + 1)
                                s.dma_start(out[dc, :, span_lo:span_lo + 512],
                                            scr_b[:, (dc % 2) * 512:((dc % 2) + 1) * 512]
                                            ).then_inc(store_sem[dc % 2], 16)
                            if tbi == 1:
                                s.wait_ge(store_sem[0], 16 * DC)
                                s.wait_ge(store_sem[1], 16 * DC)

        if debug:
            with nc.Block("DBGX1", no_gpsimd_drain=True) as block:
                @block.sync
                def _(s):
                    s.dma_start(dbg_x1[:], x_sb[:]).then_inc(dbg_s, 16)
                    s.wait_ge(dbg_s, 32)

        with ExitStack() as sc5:
            wgb = sc5.enter_context(nc.sbuf_tensor("wgb", [128, 2 * DC * 128], BF16, side="right"))
            wub = sc5.enter_context(nc.sbuf_tensor("wub", [128, 2 * DC * 128], BF16, side="right"))
            wdb = sc5.enter_context(nc.sbuf_tensor("wdb", [128, 4 * 1024], BF16, side="right"))

            def ffn_prefetch(tb):
                def pf(g):
                    for ic in range(2):
                        g.dma_start(wgb[:, (ic % 2) * D:((ic % 2) + 1) * D],
                                    wg_t[ic]).then_inc(f_w[ic % 2], 16)
                        g.dma_start(wub[:, (ic % 2) * D:((ic % 2) + 1) * D],
                                    wu_t[ic]).then_inc(f_w[ic % 2], 16)
                return pf

            emit_ln("P4b", 0, TOK, False, l1_sq, l1_pe, l1_ev, l1_dv, l1_bpe, l1_bev, l1_n, 0,
                    prefetch=ffn_prefetch(0))

            for tb in range(2):
                lo = tb * 512
                schb = ExitStack()
                hbuf = schb.enter_context(nc.sbuf_tensor(f"hbuf{tb}", [128, IC * 512], BF16))
                with ExitStack() as ph:
                    PH = ph.enter_context
                    gstage = PH(nc.sbuf_tensor(f"gstage{tb}", [128, 2 * 512], F32, side="right"))
                    psum_g = [PH(nc.psum_tensor(f"pg{tb}{i}", [128, 512], F32)) for i in range(2)]
                    psum_u = [PH(nc.psum_tensor(f"pu{tb}{i}", [128, 512], F32)) for i in range(2)]

                    with nc.Block(f"P5a{tb}", no_gpsimd_drain=True) as block:
                        @block.gpsimd
                        def _(g):
                            for ic in range(2, IC):
                                g.wait_ge(f_mm, 2 * (tb * IC + ic - 1))
                                g.dma_start(wgb[:, (ic % 2) * D:((ic % 2) + 1) * D],
                                            wg_t[ic]).then_inc(f_w[ic % 2], 16)
                                g.dma_start(wub[:, (ic % 2) * D:((ic % 2) + 1) * D],
                                            wu_t[ic]).then_inc(f_w[ic % 2], 16)
                            for npf in range(4):
                                n0 = tb * 8 * 43 + npf
                                og0, icp0 = npf // 43, npf % 43
                                g.dma_start(
                                    wdb[:, (n0 % 4) * 1024:((n0 % 4) + 1) * 1024].rearrange(
                                        "p (i c) -> p i c", i=2),
                                    wd_t[og0, 2 * icp0:2 * icp0 + 2].rearrange("i p c -> p i c"),
                                ).then_inc(d_w[n0 % 4], 16)

                        @block.tensor
                        def _(t):
                            for ic in range(IC):
                                n = tb * IC + ic
                                t.wait_ge(f_w[ic % 2], 32 * (tb * 43 + ic // 2 + 1))
                                if ic >= 2:
                                    t.wait_ge(f_h, tb * IC + ic - 1)
                                p = ic % 2
                                for dc in range(DC):
                                    ins = t.matmul(psum_g[p][:],
                                                   wgb[:, p * D + dc * 128: p * D + (dc + 1) * 128],
                                                   x_sb[:, dc * TOK + lo: dc * TOK + lo + 512],
                                                   start=(dc == 0), stop=(dc == DC - 1))
                                ins.then_inc(f_mm, 1)
                                for dc in range(DC):
                                    ins = t.matmul(psum_u[p][:],
                                                   wub[:, p * D + dc * 128: p * D + (dc + 1) * 128],
                                                   x_sb[:, dc * TOK + lo: dc * TOK + lo + 512],
                                                   start=(dc == 0), stop=(dc == DC - 1))
                                ins.then_inc(f_mm, 1)

                        @block.scalar
                        def _(sc):
                            for ic in range(IC):
                                n = tb * IC + ic
                                sc.wait_ge(f_mm, 2 * n + 1)
                                sc.activation(gstage[:, (ic % 2) * 512:((ic % 2) + 1) * 512],
                                              psum_g[ic % 2][:], AF.Silu,
                                              scale=gs_vec[:]).then_inc(f_g, 1)

                        @block.vector
                        def _(v):
                            for ic in range(IC):
                                n = tb * IC + ic
                                v.wait_ge(f_g, n + 1)
                                v.wait_ge(f_mm, 2 * n + 2)
                                v.scalar_tensor_tensor(hbuf[:, ic * 512:(ic + 1) * 512],
                                                       psum_u[ic % 2][:], usds_vec[:],
                                                       gstage[:, (ic % 2) * 512:((ic % 2) + 1) * 512],
                                                       ALU.mult, ALU.mult).then_inc(f_h, 1)

                with ExitStack() as ph:
                    PH = ph.enter_context
                    psum_d = [PH(nc.psum_tensor(f"pd{tb}{i}", [128, 512], F32)) for i in range(8)]

                    with nc.Block(f"P5b{tb}", no_gpsimd_drain=True) as block:
                        @block.gpsimd
                        def _(g):
                            for og in range(8):
                                for icp in range(43):
                                    n = (tb * 8 + og) * 43 + icp
                                    if n - tb * 8 * 43 < 4:
                                        continue
                                    g.wait_ge(d_mm, n - 3)
                                    g.dma_start(
                                        wdb[:, (n % 4) * 1024:((n % 4) + 1) * 1024].rearrange(
                                            "p (i c) -> p i c", i=2),
                                        wd_t[og, 2 * icp:2 * icp + 2].rearrange("i p c -> p i c"),
                                    ).then_inc(d_w[n % 4], 16)

                        @block.tensor
                        def _(t):
                            for og in range(8):
                                gp = og % 2
                                if og >= 2:
                                    t.wait_ge(d_ev, (tb * 8 + og - 1) * 4)
                                for icp in range(43):
                                    n = (tb * 8 + og) * 43 + icp
                                    t.wait_ge(d_w[n % 4], 16 * (n // 4 + 1))
                                    for sub in range(2):
                                        ic = 2 * icp + sub
                                        for oc in range(4):
                                            ins = t.matmul(psum_d[gp * 4 + oc][:],
                                                           wdb[:, (n % 4) * 1024 + sub * 512 + oc * 128:
                                                               (n % 4) * 1024 + sub * 512 + (oc + 1) * 128],
                                                           hbuf[:, ic * 512:(ic + 1) * 512],
                                                           start=(ic == 0), stop=(ic == IC - 1))
                                    ins.then_inc(d_mm, 1)

                        @block.vector
                        def _(v):
                            for og in range(8):
                                gp = og % 2
                                v.wait_ge(d_mm, (tb * 8 + og + 1) * 43)
                                for oc in range(4):
                                    ocf = og * 4 + oc
                                    tgt = x_sb[:, ocf * TOK + lo: ocf * TOK + lo + 512]
                                    v.tensor_tensor(tgt, psum_d[gp * 4 + oc][:], tgt, ALU.add).then_inc(d_ev, 1)

                schb.close()
                emit_ln(f"P5c{tb}", lo, 512, True, l2_sq, l2_pe, l2_ev, l2_dv, l2_bpe, l2_bev,
                        l2_n, tb, store_sem=s_out, tbi=tb,
                        prefetch=(ffn_prefetch(1) if tb == 0 else None))

    return nc


def _prep_weights(inputs):
    f32 = np.float32
    wq = np.asarray(inputs["L0_attn_q_weight"], f32)
    wk = np.asarray(inputs["L0_attn_k_weight"], f32)
    wv = np.asarray(inputs["L0_attn_v_weight"], f32)
    wo = np.asarray(inputs["L0_attn_o_weight"], f32)
    wg = np.asarray(inputs["L0_ffn_gate_w"], np.int32)
    wu = np.asarray(inputs["L0_ffn_up_w"], np.int32)
    wd = np.asarray(inputs["L0_ffn_down_w"], np.int32)

    d = {}
    d["wq_t"] = np.ascontiguousarray(
        wq.reshape(NH, 128, DC, 128).transpose(0, 3, 2, 1)).reshape(NH, 128, DC * 128)
    d["wk_t"] = np.ascontiguousarray(
        wk.reshape(NG, 128, DC, 128).transpose(0, 3, 2, 1)).reshape(NG, 128, DC * 128)
    d["wv_t"] = np.ascontiguousarray(wv.T).reshape(DC, 128, KV)
    d["wo_t"] = np.ascontiguousarray(
        wo.reshape(DC, 128, DC, 128).transpose(0, 3, 2, 1)).reshape(DC, 128, DC * 128)
    d["wg_t"] = np.ascontiguousarray(
        wg.reshape(IC, 128, DC, 128).transpose(0, 3, 2, 1)).reshape(IC, 128, DC * 128)
    d["wu_t"] = np.ascontiguousarray(
        wu.reshape(IC, 128, DC, 128).transpose(0, 3, 2, 1)).reshape(IC, 128, DC * 128)
    d["wd_t"] = np.ascontiguousarray(
        wd.reshape(8, 4, 128, IC, 128).transpose(0, 3, 4, 1, 2)).reshape(8, IC, 128, 512)
    d["qsc_r"] = np.ascontiguousarray(np.asarray(inputs["L0_attn_q_scale"], f32).reshape(1, D))
    d["ksc_r"] = np.ascontiguousarray(np.asarray(inputs["L0_attn_k_scale"], f32).reshape(1, KV))
    d["vsc_r"] = np.ascontiguousarray(np.asarray(inputs["L0_attn_v_scale"], f32).reshape(1, KV))
    d["osc_r"] = np.ascontiguousarray(np.asarray(inputs["L0_attn_o_scale"], f32).reshape(1, D))
    d["gs_d"] = np.asarray(inputs["L0_ffn_gate_s"], f32).reshape(1, 1)
    d["us_d"] = np.asarray(inputs["L0_ffn_up_s"], f32).reshape(1, 1)
    d["ds_d"] = np.asarray(inputs["L0_ffn_down_s"], f32).reshape(1, 1)
    return d


def kernel(**inputs):
    x = np.asarray(inputs["L0_attn_hidden_states"], np.float32)
    shared = _prep_weights(inputs)

    nc = build_nc(debug=DEBUG_DUMPS)

    in_maps = []
    for c in range(8):
        b, hh = c // 2, c % 2
        x_own = x[b, hh * TOK:(hh + 1) * TOK, :]
        x_oth = x[b, (1 - hh) * TOK:(2 - hh) * TOK, :]
        m = dict(shared)
        m["xo_t"] = np.ascontiguousarray(x_own.T).reshape(DC, 128, TOK)
        m["xr_t"] = np.ascontiguousarray(x_oth.T).reshape(DC, 128, TOK)
        in_maps.append(m)

    res = run_bass_kernel_spmd(nc, in_maps, core_ids=list(range(8)))

    y = np.empty((4, SEQ, D), np.float32)
    for c in range(8):
        b, hh = c // 2, c % 2
        o = np.asarray(res.results[c]["out"])
        y[b, hh * TOK:(hh + 1) * TOK, :] = o.reshape(D, TOK).T
    return y


if __name__ == "__main__":
    import importlib.util
    spec = importlib.util.spec_from_file_location("_r", "/root/problem/reference.py")
    _r = importlib.util.module_from_spec(spec)
    spec.loader.exec_module(_r)
    ins = _r.setup_inputs()
    got = kernel(**{k: np.asarray(v) for k, v in ins.items()})
    want = np.asarray(_r.reference(**ins))
    err = got - want
    rv = float(np.mean(err ** 2) / np.mean(want ** 2))
    mx = float(np.abs(got.astype(ml_dtypes.bfloat16).astype(np.float32)
                      - want.astype(ml_dtypes.bfloat16).astype(np.float32)).max())
    print("residual variance ratio:", rv, " max_abs_bf16:", mx)


# version 07
# speedup vs baseline: 1.1377x; 1.0236x over previous
import numpy as np
import ml_dtypes
from contextlib import ExitStack

import concourse.bass as bass
import concourse.mybir as mybir
from concourse.bass_utils import run_bass_kernel_spmd

F32 = mybir.dt.float32
BF16 = mybir.dt.bfloat16
I32 = mybir.dt.int32
AF = mybir.ActivationFunctionType
ALU = mybir.AluOpType
AX = mybir.AxisListType

D = 4096
KV = 1024
NH = 32
NG = 8
TOK = 1024
SEQ = 2048
IC = 86
DC = 32
INV4096 = 1.0 / 4096.0
INV_SQRT_HD = float(1.0 / np.sqrt(128.0))
EPS = 1e-5

DEBUG_DUMPS = False


def build_nc(debug=False):
    nc = bass.Bass("TRN2", target_bir_lowering=False)

    xo_t = nc.dram_tensor("xo_t", [DC, 128, TOK], F32, kind="ExternalInput")
    xr_t = nc.dram_tensor("xr_t", [DC, 128, TOK], F32, kind="ExternalInput")
    wq_t = nc.dram_tensor("wq_t", [NH, 128, DC * 128], F32, kind="ExternalInput")
    wk_t = nc.dram_tensor("wk_t", [NG, 128, DC * 128], F32, kind="ExternalInput")
    wv_t = nc.dram_tensor("wv_t", [DC, 128, KV], F32, kind="ExternalInput")
    wo_t = nc.dram_tensor("wo_t", [DC, 128, DC * 128], F32, kind="ExternalInput")
    wg_t = nc.dram_tensor("wg_t", [IC, 128, DC * 128], I32, kind="ExternalInput")
    wu_t = nc.dram_tensor("wu_t", [IC, 128, DC * 128], I32, kind="ExternalInput")
    wd_t = nc.dram_tensor("wd_t", [8, IC, 128, 512], I32, kind="ExternalInput")
    qsc_r = nc.dram_tensor("qsc_r", [1, D], F32, kind="ExternalInput")
    ksc_r = nc.dram_tensor("ksc_r", [1, KV], F32, kind="ExternalInput")
    vsc_r = nc.dram_tensor("vsc_r", [1, KV], F32, kind="ExternalInput")
    osc_r = nc.dram_tensor("osc_r", [1, D], F32, kind="ExternalInput")
    gs_d = nc.dram_tensor("gs_d", [1, 1], F32, kind="ExternalInput")
    us_d = nc.dram_tensor("us_d", [1, 1], F32, kind="ExternalInput")
    ds_d = nc.dram_tensor("ds_d", [1, 1], F32, kind="ExternalInput")
    out = nc.dram_tensor("out", [DC, 128, TOK], F32, kind="ExternalOutput")

    skind = "ExternalOutput" if debug else "Internal"
    k_s = nc.dram_tensor("k_s", [NG, 128, SEQ], BF16, kind=skind)
    v_s = nc.dram_tensor("v_s", [16, 128, KV], BF16, kind=skind)
    if debug:
        dbg_q = nc.dram_tensor("dbg_q", [128, 33 * TOK], BF16, kind="ExternalOutput")
        dbg_x1 = nc.dram_tensor("dbg_x1", [128, DC * TOK], BF16, kind="ExternalOutput")

    c_ones_col = nc.inline_tensor(np.ones((128, 1), dtype=ml_dtypes.bfloat16), name="c_ones_col")
    c_negones = nc.inline_tensor(-np.ones((1, 128), dtype=ml_dtypes.bfloat16), name="c_negones")
    c_ones_rowf = nc.inline_tensor(np.ones((1, 128), dtype=np.float32), name="c_ones_rowf")
    c_onef = nc.inline_tensor(np.ones((1, 1), dtype=np.float32), name="c_onef")

    with ExitStack() as outer:
        E = outer.enter_context
        ones_col = E(nc.sbuf_tensor("ones_col", [128, 1], BF16))
        negones = E(nc.sbuf_tensor("negones", [1, 128], BF16))
        ones_rowf = E(nc.sbuf_tensor("ones_rowf", [1, 128], F32))
        onef = E(nc.sbuf_tensor("onef", [1, 1], F32))
        gs_sb = E(nc.sbuf_tensor("gs_sb", [1, 1], F32))
        us_sb = E(nc.sbuf_tensor("us_sb", [1, 1], F32))
        ds_sb = E(nc.sbuf_tensor("ds_sb", [1, 1], F32))
        usds_row = E(nc.sbuf_tensor("usds_row", [1, 1], F32))
        qsc_vec = E(nc.sbuf_tensor("qsc_vec", [128, NH], F32))
        ksc_vec = E(nc.sbuf_tensor("ksc_vec", [128, NG], F32))
        osc_vec = E(nc.sbuf_tensor("osc_vec", [128, DC], F32))
        gs_vec = E(nc.sbuf_tensor("gs_vec", [128, 1], F32))
        usds_vec = E(nc.sbuf_tensor("usds_vec", [128, 1], F32))
        srow_sb = E(nc.sbuf_tensor("srow_sb", [1, 2 * 128], F32))
        scol_tmp = E(nc.sbuf_tensor("scol_tmp", [128, 2], F32))
        kvec_sb = E(nc.sbuf_tensor("kvec_sb", [128, 2], F32))
        qvec_sb = E(nc.sbuf_tensor("qvec_sb", [128, 2], F32))
        ovec_sb = E(nc.sbuf_tensor("ovec_sb", [128, 2], F32))
        Mk2_sb = E(nc.sbuf_tensor("Mk2_sb", [1, NG], F32))
        nqx_sb = E(nc.sbuf_tensor("nqx_sb", [1, 2], F32))
        Mk2t_sb = E(nc.sbuf_tensor("Mk2t_sb", [1, 4], F32))
        cb_c = E(nc.sbuf_tensor("cb_c", [1, 2], F32))
        cb_n = E(nc.sbuf_tensor("cb_n", [1, 2], F32))
        cbias = E(nc.sbuf_tensor("cbias", [128, NH], F32))
        scr_a = E(nc.sbuf_tensor("scr_a", [128, 2048], BF16))
        scr_b = E(nc.sbuf_tensor("scr_b", [128, 1024], F32))

        x_sb = E(nc.sbuf_tensor("x_sb", [128, DC * TOK], BF16, side="right"))

        def S(name):
            return E(nc.semaphore(name))

        def S2(name):
            return [E(nc.semaphore(name + "0")), E(nc.semaphore(name + "1"))]

        s0_c = S("s0_c"); s0_x = S2("s0_x"); s0_pe = S("s0_pe"); s0_ae = S("s0_ae"); s0_dv = S("s0_dv")
        k_w = S2("k_w"); k_abs = S("k_abs"); k_spe = S("k_spe"); k_tpe = S("k_tpe")
        k_ev1 = S("k_ev1"); k_ev2 = S("k_ev2"); k_dv = S("k_dv"); k_pe = S("k_pe")
        k_ev = S("k_ev"); k_st = S2("k_st"); k_sq = S("k_sq"); k_nkpe = S("k_nkpe"); k_dvnk = S("k_dvnk"); k_dvnk2 = S("k_dvnk2")
        v_w = S2("v_w"); v_sgn = S("v_sgn"); v_svpe = S("v_svpe"); v_sve = S("v_sve")
        v_dvr = S("v_dvr"); v_bpe = S("v_bpe"); v_bev = S("v_bev"); v_pe = S("v_pe")
        v_ev = S("v_ev"); v_st = S2("v_st")
        q_w = S2("q_w"); q_abs = S("q_abs"); q_spe = S("q_spe"); q_tpe = S("q_tpe")
        q_ev1 = S("q_ev1"); q_ev2 = S("q_ev2"); q_dv = S("q_dv"); q_pe = S("q_pe")
        q_ev = S("q_ev"); q_sq = S("q_sq"); q_nqpe = S("q_nqpe"); q_cev = S("q_cev")
        a_k = S2("a_k"); a_v = S2("a_v"); a_pe = S("a_pe"); a_pt = S("a_pt"); a_pu = S("a_pu")
        a_pr = S("a_pr"); a_rr = S("a_rr"); a_pb = S("a_pb"); a_bce = S("a_bce"); a_at = S("a_at")
        o_w = S2("o_w"); o_abs = S("o_abs"); o_spe = S("o_spe"); o_tpe = S("o_tpe")
        o_ev1 = S("o_ev1"); o_ev2 = S("o_ev2"); o_dv = S("o_dv"); o_pe = S("o_pe"); o_ev = S("o_ev")
        l1_sq = S("l1_sq"); l1_pe = S("l1_pe"); l1_ev = S("l1_ev"); l1_dv = S("l1_dv")
        l1_bpe = S("l1_bpe"); l1_bev = S("l1_bev"); l1_n = S("l1_n")
        f_w = S2("f_w"); f_wu = S2("f_wu"); f_mm = S("f_mm"); f_g = S("f_g"); f_h = S("f_h")
        d_w = [S("d_w0"), S("d_w1"), S("d_w2"), S("d_w3")]; d_mm = S("d_mm"); d_ev = S("d_ev")
        l2_sq = S("l2_sq"); l2_pe = S("l2_pe"); l2_ev = S("l2_ev"); l2_dv = S("l2_dv")
        l2_bpe = S("l2_bpe"); l2_bev = S("l2_bev"); l2_n = S("l2_n"); s_out = S2("s_out")
        q_dvc = S("q_dvc"); q_cpe = S("q_cpe")
        dbg_s = S("dbg_s")

        with ExitStack() as ph:
            psum_t = [ph.enter_context(nc.psum_tensor(f"p0t{i}", [128, 1], F32)) for i in range(2)]
            qsc_row = ph.enter_context(nc.sbuf_tensor("qsc_row", [1, D], F32))
            ksc_row = ph.enter_context(nc.sbuf_tensor("ksc_row", [1, KV], F32))
            osc_row = ph.enter_context(nc.sbuf_tensor("osc_row", [1, D], F32))

            jobs = ([("q", j) for j in range(NH)] + [("k", j) for j in range(NG)]
                    + [("o", j) for j in range(DC)] + [("gs", 0), ("usds", 0)])

            with nc.Block("P0", no_gpsimd_drain=True) as block:
                @block.gpsimd
                def _(g):
                    g.dma_start(
                        x_sb[:].rearrange("p (c t) -> p c t", c=DC),
                        xo_t.rearrange("c p t -> p c t"),
                    ).then_inc(s0_x[0], 16)

                @block.sync
                def _(s):
                    for dst, src in [
                        (ones_col, c_ones_col), (negones, c_negones),
                        (ones_rowf, c_ones_rowf), (onef, c_onef),
                        (qsc_row, qsc_r), (ksc_row, ksc_r),
                        (osc_row, osc_r), (gs_sb, gs_d), (us_sb, us_d), (ds_sb, ds_d),
                    ]:
                        s.dma_start(dst[:], src[:]).then_inc(s0_c, 16)

                @block.vector
                def _(v):
                    v.wait_ge(s0_c, 160)
                    v.tensor_tensor(usds_row[:], us_sb[:], ds_sb[:], ALU.mult).then_inc(s0_dv, 1)

                @block.tensor
                def _(t):
                    t.wait_ge(s0_c, 160)
                    for cnt, (kind, j) in enumerate(jobs):
                        if cnt >= 2:
                            t.wait_ge(s0_ae, cnt - 1)
                        if kind == "q":
                            lhs = qsc_row[0:1, j * 128:(j + 1) * 128]
                            rhs = onef[:]
                        elif kind == "k":
                            lhs = ksc_row[0:1, j * 128:(j + 1) * 128]
                            rhs = onef[:]
                        elif kind == "o":
                            lhs = osc_row[0:1, j * 128:(j + 1) * 128]
                            rhs = onef[:]
                        elif kind == "gs":
                            lhs = ones_rowf[:]
                            rhs = gs_sb[:]
                        else:
                            t.wait_ge(s0_dv, 1)
                            lhs = ones_rowf[:]
                            rhs = usds_row[:]
                        t.matmul(psum_t[cnt % 2][:], lhs, rhs, start=True, stop=True).then_inc(s0_pe, 1)

                @block.scalar
                def _(sc):
                    for cnt, (kind, j) in enumerate(jobs):
                        sc.wait_ge(s0_pe, cnt + 1)
                        if kind == "q":
                            dst = qsc_vec[:, j:j + 1]
                        elif kind == "k":
                            dst = ksc_vec[:, j:j + 1]
                        elif kind == "o":
                            dst = osc_vec[:, j:j + 1]
                        elif kind == "gs":
                            dst = gs_vec[:]
                        else:
                            dst = usds_vec[:]
                        sc.activation(dst, psum_t[cnt % 2][:], AF.Copy).then_inc(s0_ae, 1)

        with ExitStack() as sc01:
            xr_sb = sc01.enter_context(nc.sbuf_tensor("xr_sb", [128, DC * TOK], BF16, side="right"))
            vsc_row = sc01.enter_context(nc.sbuf_tensor("vsc_row", [1, KV], F32))
            vrow_sb = sc01.enter_context(nc.sbuf_tensor("vrow_sb", [1, KV], F32))

            def xall(dc, sp):
                if sp < 2:
                    return x_sb[:, dc * TOK + sp * 512: dc * TOK + (sp + 1) * 512]
                return xr_sb[:, dc * TOK + (sp - 2) * 512: dc * TOK + (sp - 1) * 512]

            def xtile_all(dc, tc):
                if tc < 8:
                    return x_sb[:, dc * TOK + tc * 128: dc * TOK + (tc + 1) * 128]
                return xr_sb[:, dc * TOK + (tc - 8) * 128: dc * TOK + (tc - 7) * 128]

            with nc.Block("P0X", no_gpsimd_drain=True) as block:
                @block.gpsimd
                def _(g):
                    g.dma_start(
                        xr_sb[:].rearrange("p (c t) -> p c t", c=DC),
                        xr_t.rearrange("c p t -> p c t"),
                    ).then_inc(s0_x[1], 16)

                @block.sync
                def _(s):
                    s.dma_start(vsc_row[:], vsc_r[:]).then_inc(s0_c, 16)

            with ExitStack() as ph:
                PH = ph.enter_context
                wkf = PH(nc.sbuf_tensor("wkf", [128, 2 * DC * 128], F32, side="right"))
                wkb = PH(nc.sbuf_tensor("wkb", [128, 2 * DC * 128], BF16, side="right"))
                absb = PH(nc.sbuf_tensor("absb", [128, 2 * 512], BF16, side="right"))
                kstage = PH(nc.sbuf_tensor("kstage", [128, 2 * SEQ], BF16, side="right"))
                psum_k = [PH(nc.psum_tensor(f"pk{i}", [128, 512], F32)) for i in range(4)]
                psum_sk = PH(nc.psum_tensor("psk", [128, 1], F32))
                psum_nk = [PH(nc.psum_tensor(f"pnk{i}", [1, 512], F32)) for i in range(2)]

                def kst(g):
                    return kstage[:, (g % 2) * SEQ:((g % 2) + 1) * SEQ]

                with nc.Block("P1K", no_gpsimd_drain=True) as block:
                    @block.sync
                    def _(s):
                        for g in range(NG):
                            if g >= 2:
                                s.wait_ge(k_abs, 10 * (g - 1))
                            s.dma_start(wkf[:, (g % 2) * D:((g % 2) + 1) * D],
                                        wk_t[g]).then_inc(k_w[g % 2], 16)
                            if g >= 1:
                                s.wait_ge(k_ev, 4 * g)
                                s.dma_start(k_s[g - 1], kst(g - 1)).then_inc(k_st[(g - 1) % 2], 16)
                        s.wait_ge(k_ev, 4 * NG)
                        s.dma_start(k_s[NG - 1], kst(NG - 1)).then_inc(k_st[(NG - 1) % 2], 16)
                        s.wait_ge(k_st[0], 16 * (NG // 2))
                        s.wait_ge(k_st[1], 16 * (NG // 2))

                    @block.scalar
                    def _(sc):
                        for g in range(NG):
                            p = g % 2
                            sc.wait_ge(k_w[g % 2], 16 * (g // 2 + 1))
                            if g >= 2:
                                sc.wait_ge(k_pe, 4 * (g - 1))
                            sc.activation(wkb[:, p * D:p * D + 2048], wkf[:, p * D:p * D + 2048],
                                          AF.Sign).then_inc(k_abs, 1)
                            sc.activation(wkb[:, p * D + 2048:(p + 1) * D],
                                          wkf[:, p * D + 2048:(p + 1) * D],
                                          AF.Sign).then_inc(k_abs, 1)
                            for c in range(8):
                                if 8 * g + c >= 2:
                                    sc.wait_ge(k_spe, 8 * g + c - 1)
                                sc.activation(absb[:, (c % 2) * 512:((c % 2) + 1) * 512],
                                              wkf[:, p * D + c * 512: p * D + (c + 1) * 512],
                                              AF.Abs).then_inc(k_abs, 1)
                            for j in range(4):
                                sc.wait_ge(k_dv, 2 * (g + 1))
                                sc.wait_ge(k_pe, 4 * g + j + 1)
                                if g >= 2 and j == 0:
                                    sc.wait_ge(k_st[g % 2], 16 * (g // 2))
                                sc.activation(kstage[:, p * SEQ + j * 512: p * SEQ + (j + 1) * 512],
                                              psum_k[(4 * g + j) % 4][:], AF.Copy,
                                              scale=kvec_sb[:, p:p + 1]).then_inc(k_ev, 1)
                            if g >= 1:
                                sc.wait_ge(k_nkpe, 4 * g)
                            sc.wait_ge(k_ev, 4 * (g + 1))
                            for j in range(4):
                                sc.activation(scr_a[:, j * 512:(j + 1) * 512],
                                              kstage[:, p * SEQ + j * 512: p * SEQ + (j + 1) * 512],
                                              AF.Square).then_inc(k_sq, 1)

                    @block.tensor
                    def _(t):
                        def emit_nk(gg):
                            for j in range(4):
                                jj4 = 4 * gg + j
                                if jj4 >= 2:
                                    t.wait_ge(k_dvnk, jj4 - 1)
                                t.wait_ge(k_sq, jj4 + 1)
                                t.matmul(psum_nk[jj4 % 2][:], ones_col[:],
                                         scr_a[:, j * 512:(j + 1) * 512],
                                         start=True, stop=True).then_inc(k_nkpe, 1)

                        for g in range(NG):
                            p = g % 2
                            for j in range(4):
                                t.wait_ge(s0_x[0 if j < 2 else 1], 16)
                                if 4 * g + j >= 4:
                                    t.wait_ge(k_ev, 4 * g + j - 3)
                                for dc in range(DC):
                                    if j == 0 and dc == 0:
                                        t.wait_ge(k_abs, 10 * g + 1)
                                    if j == 0 and dc == 16:
                                        t.wait_ge(k_abs, 10 * g + 2)
                                    ins = t.matmul(psum_k[(4 * g + j) % 4][:],
                                                   wkb[:, p * D + dc * 128: p * D + (dc + 1) * 128],
                                                   xall(dc, j),
                                                   start=(dc == 0), stop=(dc == DC - 1))
                                ins.then_inc(k_pe, 1)
                            if g >= 1:
                                t.wait_ge(k_dv, 2 * g - 1)
                            for c in range(8):
                                t.wait_ge(k_abs, 10 * g + c + 3)
                                for jj in range(4):
                                    dc = 4 * c + jj
                                    ins = t.matmul(psum_sk[:],
                                                   absb[:, (c % 2) * 512 + jj * 128:(c % 2) * 512 + (jj + 1) * 128],
                                                   ones_col[:],
                                                   start=(dc == 0), stop=(dc == DC - 1))
                                ins.then_inc(k_spe, 1)
                            if g >= 1:
                                emit_nk(g - 1)
                        emit_nk(NG - 1)

                    @block.vector
                    def _(v):
                        for g in range(NG):
                            p = g % 2
                            v.wait_ge(k_spe, 8 * (g + 1))
                            if g >= 2:
                                v.wait_ge(k_dv, 2 * (g - 1))
                            v.tensor_scalar(scol_tmp[:, p:p + 1], psum_sk[:], INV4096, 1e-8,
                                            ALU.mult, ALU.max).then_inc(k_dv, 1)
                            v.wait_ge(k_dv, 2 * g + 1)
                            if g >= 2:
                                v.wait_ge(k_ev, 4 * (g - 1))
                            v.tensor_tensor(kvec_sb[:, p:p + 1], scol_tmp[:, p:p + 1],
                                            ksc_vec[:, g:g + 1], ALU.mult).then_inc(k_dv, 1)
                            for j in range(4):
                                jj4 = 4 * g + j
                                v.wait_ge(k_nkpe, jj4 + 1)
                                if j == 0 and g >= 1:
                                    v.wait_ge(k_dvnk2, g)
                                v.reduce_max(Mk2t_sb[0:1, j:j + 1],
                                             psum_nk[jj4 % 2][:], axis=AX.X).then_inc(k_dvnk, 1)
                            v.wait_ge(k_dvnk, 4 * (g + 1))
                            v.tensor_reduce(Mk2_sb[0:1, g:g + 1], Mk2t_sb[0:1, 0:4], axis=AX.X,
                                            op=ALU.max).then_inc(k_dvnk2, 1)

            with ExitStack() as ph:
                PH = ph.enter_context
                wvf = PH(nc.sbuf_tensor("wvf", [128, 2 * KV], F32, side="right"))
                wvh = PH(nc.sbuf_tensor("wvh", [128, DC * 512], BF16, side="right"))
                vabs = PH(nc.sbuf_tensor("vabs", [128, 2 * KV], BF16, side="right"))
                vstage = PH(nc.sbuf_tensor("vstage", [128, 2 * 512], BF16, side="right"))
                psum_v = [PH(nc.psum_tensor(f"pv{i}", [128, 512], F32)) for i in range(2)]
                psum_sv = PH(nc.psum_tensor("psv", [1, KV], F32))
                psum_vb = PH(nc.psum_tensor("pvb", [128, 1024], F32))

                def vsgn_count(i):
                    return 2 * (i + 1) if i < DC else 2 * DC + (i - DC) + 1

                with nc.Block("P1V", no_gpsimd_drain=True) as block:
                    @block.sync
                    def _(s):
                        for half in range(2):
                            for dc in range(DC):
                                i = half * DC + dc
                                if i >= 2:
                                    s.wait_ge(v_sgn, vsgn_count(i - 2))
                                s.dma_start(wvf[:, (i % 2) * KV:((i % 2) + 1) * KV],
                                            wv_t[dc]).then_inc(v_w[i % 2], 16)

                    @block.gpsimd
                    def _(g):
                        for it in range(32):
                            g.wait_ge(v_ev, it + 1)
                            half, tc = it // 16, it % 16
                            g.dma_start(v_s[tc, :, half * 512:(half + 1) * 512],
                                        vstage[:, (it % 2) * 512:((it % 2) + 1) * 512]).then_inc(v_st[it % 2], 16)
                        g.wait_ge(v_st[0], 16 * 16)
                        g.wait_ge(v_st[1], 16 * 16)

                    @block.scalar
                    def _(sc):
                        for half in range(2):
                            if half == 1:
                                sc.wait_ge(v_pe, 16)
                            for dc in range(DC):
                                i = half * DC + dc
                                sc.wait_ge(v_w[i % 2], 16 * (i // 2 + 1))
                                sc.activation(wvh[:, dc * 512:(dc + 1) * 512],
                                              wvf[:, (i % 2) * KV + half * 512:(i % 2) * KV + (half + 1) * 512],
                                              AF.Sign).then_inc(v_sgn, 1)
                                if half == 0:
                                    if dc >= 2:
                                        sc.wait_ge(v_svpe, 2 * (dc - 1))
                                    sc.activation(vabs[:, (dc % 2) * KV:((dc % 2) + 1) * KV],
                                                  wvf[:, (i % 2) * KV:((i % 2) + 1) * KV],
                                                  AF.Abs).then_inc(v_sgn, 1)
                            if half == 0:
                                sc.wait_ge(v_svpe, 2 * DC)
                                sc.activation(vrow_sb[:], psum_sv[:], AF.Copy).then_inc(v_sve, 1)
                                sc.wait_ge(v_bpe, 2)
                                sc.activation(scr_b[:, 0:512], psum_vb[:, 0:512], AF.Copy).then_inc(v_bev, 1)
                                sc.activation(scr_b[:, 512:1024], psum_vb[:, 512:1024], AF.Copy).then_inc(v_bev, 1)

                    @block.tensor
                    def _(t):
                        t.wait_ge(s0_x[0], 16)
                        t.wait_ge(s0_x[1], 16)
                        for half in range(2):
                            if half == 0:
                                for dc in range(DC):
                                    t.wait_ge(v_sgn, 2 * (dc + 1))
                                    for j in range(2):
                                        t.matmul(psum_sv[0:1, j * 512:(j + 1) * 512], ones_col[:],
                                                 vabs[:, (dc % 2) * KV + j * 512:(dc % 2) * KV + (j + 1) * 512],
                                                 start=(dc == 0), stop=(dc == DC - 1)).then_inc(v_svpe, 1)
                                t.wait_ge(v_dvr, 2)
                                for j in range(2):
                                    t.matmul(psum_vb[:, j * 512:(j + 1) * 512], ones_rowf[:],
                                             vrow_sb[0:1, j * 512:(j + 1) * 512],
                                             start=True, stop=True).then_inc(v_bpe, 1)
                            for tc in range(16):
                                it = half * 16 + tc
                                if it >= 2:
                                    t.wait_ge(v_ev, it - 1)
                                for dc in range(DC):
                                    if tc == 0:
                                        t.wait_ge(v_sgn, (2 * dc + 1) if half == 0 else (2 * DC + dc + 1))
                                    ins = t.matmul(psum_v[it % 2][:], xtile_all(dc, tc),
                                                   wvh[:, dc * 512:(dc + 1) * 512],
                                                   start=(dc == 0), stop=(dc == DC - 1))
                                ins.then_inc(v_pe, 1)

                    @block.vector
                    def _(v):
                        v.wait_ge(v_sve, 1)
                        v.wait_ge(s0_c, 176)
                        v.tensor_scalar(vrow_sb[:], vrow_sb[:], INV4096, 1e-8, ALU.mult, ALU.max).then_inc(v_dvr, 1)
                        v.wait_ge(v_dvr, 1)
                        v.tensor_tensor(vrow_sb[:], vrow_sb[:], vsc_row[:], ALU.mult).then_inc(v_dvr, 1)
                        for it in range(32):
                            half = it // 16
                            v.wait_ge(v_pe, it + 1)
                            if it == 0:
                                v.wait_ge(v_bev, 2)
                            if it >= 2:
                                v.wait_ge(v_st[it % 2], 16 * (it // 2))
                            v.tensor_tensor(vstage[:, (it % 2) * 512:((it % 2) + 1) * 512],
                                            psum_v[it % 2][:],
                                            scr_b[:, half * 512:(half + 1) * 512],
                                            ALU.mult).then_inc(v_ev, 1)

        with ExitStack() as scA:
            A = scA.enter_context(nc.sbuf_tensor("A", [128, 33 * TOK], BF16))
            Kg = scA.enter_context(nc.sbuf_tensor("Kg", [128, 2 * SEQ], BF16, side="right"))
            Vg = scA.enter_context(nc.sbuf_tensor("Vg", [128, 2 * SEQ], BF16, side="right"))

            def q_slot(h):
                return A[:, h * TOK:(h + 1) * TOK]

            def att_slot(h):
                s = 32 if h == 0 else h - 1
                return A[:, s * TOK:(s + 1) * TOK]

            with ExitStack() as ph:
                PH = ph.enter_context
                wqf = PH(nc.sbuf_tensor("wqf", [128, 2 * DC * 128], F32, side="right"))
                wqb = PH(nc.sbuf_tensor("wqb", [128, 2 * DC * 128], BF16, side="right"))
                qabsb = PH(nc.sbuf_tensor("qabsb", [128, 2 * 512], BF16, side="right"))
                psum_q = [PH(nc.psum_tensor(f"pq{i}", [128, 512], F32)) for i in range(4)]
                psum_sq = PH(nc.psum_tensor("psq", [128, 1], F32))
                psum_tq = PH(nc.psum_tensor("ptq", [128, 1], F32))

                dvc_after_max = {}
                dvc_after_neg = {}
                cnt = 0
                for h2 in range(NH + 2):
                    if 1 <= h2 <= NH:
                        cnt += 1
                        dvc_after_max[h2 - 1] = cnt
                    if 2 <= h2 <= NH + 1:
                        cnt += 1
                        dvc_after_neg[h2 - 2] = cnt
                cev_after_sqrt = {}
                cev_after_cbe = {}
                cnt = 0
                for h2 in range(NH + 2):
                    if 1 <= h2 <= NH:
                        cnt += 1
                        cev_after_sqrt[h2 - 1] = cnt
                    if 2 <= h2 <= NH + 1:
                        cnt += 1
                        cev_after_cbe[h2 - 2] = cnt
                psum_nq = PH(nc.psum_tensor("pnq", [1, TOK], F32))

                with nc.Block("P2", no_gpsimd_drain=True) as block:
                    @block.sync
                    def _(s):
                        for h in range(NH):
                            if h >= 2:
                                s.wait_ge(q_abs, 10 * (h - 1))
                            s.dma_start(wqf[:, (h % 2) * D:((h % 2) + 1) * D],
                                        wq_t[h]).then_inc(q_w[h % 2], 16)
                        for g in range(2):
                            s.dma_start(Kg[:, (g % 2) * SEQ:((g % 2) + 1) * SEQ],
                                        k_s[g]).then_inc(a_k[g % 2], 16)
                            s.dma_start(
                                Vg[:, (g % 2) * SEQ:((g % 2) + 1) * SEQ].rearrange("p (k c) -> p k c", k=16),
                                v_s[:, :, g * 128:(g + 1) * 128].rearrange("k p c -> p k c"),
                            ).then_inc(a_v[g % 2], 16)

                    @block.scalar
                    def _(sc):
                        def emit_sqrt(hh2):
                            gg = hh2 // 4
                            sc.wait_ge(q_dvc, dvc_after_max[hh2])
                            if hh2 >= 2:
                                sc.wait_ge(q_dvc, dvc_after_neg[hh2 - 2])
                            sc.activation(cb_c[0:1, hh2 % 2:hh2 % 2 + 1],
                                          nqx_sb[0:1, hh2 % 2:hh2 % 2 + 1], AF.Sqrt,
                                          scale=Mk2_sb[0:1, gg:gg + 1]).then_inc(q_cev, 1)

                        def emit_cbe(hh2):
                            sc.wait_ge(q_cpe, hh2 + 1)
                            sc.activation(cbias[:, hh2:hh2 + 1], psum_tq[:], AF.Copy).then_inc(q_cev, 1)

                        for h in range(NH):
                            p = h % 2
                            sc.wait_ge(q_w[h % 2], 16 * (h // 2 + 1))
                            if h >= 2:
                                sc.wait_ge(q_pe, 2 * (h - 1))
                            sc.activation(wqb[:, p * D:p * D + 2048], wqf[:, p * D:p * D + 2048],
                                          AF.Sign).then_inc(q_abs, 1)
                            sc.activation(wqb[:, p * D + 2048:(p + 1) * D],
                                          wqf[:, p * D + 2048:(p + 1) * D],
                                          AF.Sign).then_inc(q_abs, 1)
                            for c in range(8):
                                if 8 * h + c >= 2:
                                    sc.wait_ge(q_spe, 8 * h + c - 1)
                                sc.activation(qabsb[:, (c % 2) * 512:((c % 2) + 1) * 512],
                                              wqf[:, p * D + c * 512: p * D + (c + 1) * 512],
                                              AF.Abs).then_inc(q_abs, 1)
                            for j in range(2):
                                sc.wait_ge(q_dv, 2 * (h + 1))
                                sc.wait_ge(q_pe, 2 * h + j + 1)
                                sc.activation(q_slot(h)[:, j * 512:(j + 1) * 512],
                                              psum_q[(2 * h + j) % 4][:], AF.Copy,
                                              scale=qvec_sb[:, p:p + 1]).then_inc(q_ev, 1)
                            if h >= 1:
                                sc.wait_ge(q_nqpe, h)
                            sc.wait_ge(q_ev, 2 * (h + 1))
                            for j in range(2):
                                sc.activation(scr_a[:, j * 512:(j + 1) * 512],
                                              q_slot(h)[:, j * 512:(j + 1) * 512],
                                              AF.Square).then_inc(q_sq, 1)
                            if h >= 1:
                                emit_sqrt(h - 1)
                            if h >= 2:
                                emit_cbe(h - 2)
                        emit_sqrt(NH - 1)
                        emit_cbe(NH - 2)
                        emit_cbe(NH - 1)

                    @block.tensor
                    def _(t):
                        def emit_nq(hh2):
                            if hh2 >= 1:
                                t.wait_ge(q_dvc, dvc_after_max[hh2 - 1])
                            for j in range(2):
                                t.wait_ge(q_sq, 2 * hh2 + j + 1)
                                ins = t.matmul(psum_nq[0:1, j * 512:(j + 1) * 512], ones_col[:],
                                               scr_a[:, j * 512:(j + 1) * 512],
                                               start=True, stop=True)
                                if j == 1:
                                    ins.then_inc(q_nqpe, 1)

                        def emit_bc(hh2):
                            t.wait_ge(q_dvc, dvc_after_neg[hh2])
                            if hh2 >= 1:
                                t.wait_ge(q_cev, cev_after_cbe[hh2 - 1])
                            t.matmul(psum_tq[:], ones_rowf[:], cb_n[0:1, hh2 % 2:hh2 % 2 + 1],
                                     start=True, stop=True).then_inc(q_cpe, 1)

                        for h in range(NH):
                            p = h % 2
                            for j in range(2):
                                n = 2 * h + j
                                if n >= 4:
                                    t.wait_ge(q_ev, n - 3)
                                for dc in range(DC):
                                    if j == 0 and dc == 0:
                                        t.wait_ge(q_abs, 10 * h + 1)
                                    if j == 0 and dc == 16:
                                        t.wait_ge(q_abs, 10 * h + 2)
                                    ins = t.matmul(psum_q[n % 4][:],
                                                   wqb[:, p * D + dc * 128: p * D + (dc + 1) * 128],
                                                   x_sb[:, dc * TOK + j * 512: dc * TOK + (j + 1) * 512],
                                                   start=(dc == 0), stop=(dc == DC - 1))
                                ins.then_inc(q_pe, 1)
                            if h >= 1:
                                t.wait_ge(q_dv, 2 * h - 1)
                            for c in range(8):
                                t.wait_ge(q_abs, 10 * h + c + 3)
                                for jj in range(4):
                                    dc = 4 * c + jj
                                    ins = t.matmul(psum_sq[:],
                                                   qabsb[:, (c % 2) * 512 + jj * 128:(c % 2) * 512 + (jj + 1) * 128],
                                                   ones_col[:],
                                                   start=(dc == 0), stop=(dc == DC - 1))
                                ins.then_inc(q_spe, 1)
                            if h >= 1:
                                emit_nq(h - 1)
                            if h >= 2:
                                emit_bc(h - 2)
                        emit_nq(NH - 1)
                        emit_bc(NH - 2)
                        emit_bc(NH - 1)

                    @block.vector
                    def _(v):
                        for h in range(NH + 2):
                            if h < NH:
                                p = h % 2
                                v.wait_ge(q_spe, 8 * (h + 1))
                                if h >= 2:
                                    v.wait_ge(q_dv, 2 * (h - 1))
                                v.tensor_scalar(scol_tmp[:, p:p + 1], psum_sq[:], INV4096, 1e-8,
                                                ALU.mult, ALU.max).then_inc(q_dv, 1)
                                v.wait_ge(q_dv, 2 * h + 1)
                                if h >= 2:
                                    v.wait_ge(q_ev, 2 * (h - 1))
                                v.scalar_tensor_tensor(qvec_sb[:, p:p + 1], scol_tmp[:, p:p + 1],
                                                       INV_SQRT_HD, qsc_vec[:, h:h + 1],
                                                       ALU.mult, ALU.mult).then_inc(q_dv, 1)
                            if 1 <= h <= NH:
                                hm = h - 1
                                v.wait_ge(q_nqpe, hm + 1)
                                if hm >= 2:
                                    v.wait_ge(q_cev, cev_after_sqrt[hm - 2])
                                v.reduce_max(nqx_sb[0:1, hm % 2:hm % 2 + 1], psum_nq[:],
                                             axis=AX.X).then_inc(q_dvc, 1)
                            if 2 <= h <= NH + 1:
                                hn = h - 2
                                v.wait_ge(q_cev, cev_after_sqrt[hn])
                                if hn >= 2:
                                    v.wait_ge(q_cpe, hn - 1)
                                v.tensor_scalar_mul(cb_n[0:1, hn % 2:hn % 2 + 1],
                                                    cb_c[0:1, hn % 2:hn % 2 + 1], -1.0).then_inc(q_dvc, 1)

            if debug:
                with nc.Block("DBGQ", no_gpsimd_drain=True) as block:
                    @block.sync
                    def _(s):
                        s.dma_start(dbg_q[:], A[:]).then_inc(dbg_s, 16)
                        s.wait_ge(dbg_s, 16)

            wof = scA.enter_context(nc.sbuf_tensor("wof", [128, 2 * DC * 128], F32, side="right"))

            with ExitStack() as ph:
                PH = ph.enter_context
                rr_sb = PH(nc.sbuf_tensor("rr_sb", [1, 2 * 512], F32, side="right"))
                ptbuf = PH(nc.sbuf_tensor("ptbuf", [128, 3 * 512], BF16, side="right"))
                psum_st = [PH(nc.psum_tensor(f"pst{i}", [128, 512], F32)) for i in range(3)]
                psum_av = [PH(nc.psum_tensor(f"pav{i}", [128, 512], F32)) for i in range(2)]
                psum_r = [PH(nc.psum_tensor(f"pr{i}", [1, 512], F32)) for i in range(2)]
                psum_bc = PH(nc.psum_tensor("pbc", [128, 512], F32))

                with nc.Block("P3", no_gpsimd_drain=True) as block:
                    @block.sync
                    def _(s):
                        for g in range(2, NG):
                            s.wait_ge(a_pe, 128 * (g - 1))
                            s.dma_start(Kg[:, (g % 2) * SEQ:((g % 2) + 1) * SEQ], k_s[g]).then_inc(a_k[g % 2], 16)
                            s.wait_ge(a_pu, 128 * (g - 1))
                            s.dma_start(
                                Vg[:, (g % 2) * SEQ:((g % 2) + 1) * SEQ].rearrange("p (k c) -> p k c", k=16),
                                v_s[:, :, g * 128:(g + 1) * 128].rearrange("k p c -> p k c"),
                            ).then_inc(a_v[g % 2], 16)
                        for oc in range(2):
                            s.wait_ge(a_pe, 128 * (NG - 1))
                            s.dma_start(wof[:, (oc % 2) * D:((oc % 2) + 1) * D],
                                        wo_t[oc]).then_inc(o_w[oc % 2], 16)

                    @block.tensor
                    def _(t):
                        def emit_bc(itb):
                            t.wait_ge(a_rr, itb + 1)
                            if itb >= 1:
                                t.wait_ge(a_bce, itb)
                            t.matmul(psum_bc[:], ones_rowf[:],
                                     rr_sb[0:1, (itb % 2) * 512:((itb % 2) + 1) * 512],
                                     start=True, stop=True).then_inc(a_pb, 1)

                        for g in range(NG):
                            gp = g % 2
                            for hh in range(4):
                                h = g * 4 + hh
                                for qs in range(2):
                                    it = h * 2 + qs
                                    for kc in range(17):
                                        if kc < 16:
                                            n = it * 16 + kc
                                            if kc == 0:
                                                t.wait_ge(a_k[g % 2], 16 * (g // 2 + 1))
                                                t.wait_ge(a_v[g % 2], 16 * (g // 2 + 1))
                                            if n >= 3:
                                                t.wait_ge(a_pt, n - 2)
                                            t.matmul(psum_st[n % 3][:],
                                                     Kg[:, gp * SEQ + kc * 128: gp * SEQ + (kc + 1) * 128],
                                                     q_slot(h)[:, qs * 512:(qs + 1) * 512],
                                                     start=True, stop=True).then_inc(a_pe, 1)
                                        if kc == 2 and it >= 1:
                                            emit_bc(it - 1)
                                        if kc >= 1:
                                            km = kc - 1
                                            nm = it * 16 + km
                                            t.wait_ge(a_pt, nm + 1)
                                            if km == 0 and it >= 2:
                                                t.wait_ge(a_rr, it - 1)
                                            ins_r = t.matmul(psum_r[it % 2][:], ones_col[:],
                                                             ptbuf[:, (nm % 3) * 512:((nm % 3) + 1) * 512],
                                                             start=(km == 0), stop=(km == 15))
                                            if km == 15:
                                                ins_r.then_inc(a_pr, 1)
                                            if km == 0 and it >= 2:
                                                t.wait_ge(a_at, it - 1)
                                            t.matmul(psum_av[it % 2][:],
                                                     Vg[:, gp * SEQ + km * 128: gp * SEQ + (km + 1) * 128],
                                                     ptbuf[:, (nm % 3) * 512:((nm % 3) + 1) * 512],
                                                     start=(km == 0), stop=(km == 15)).then_inc(a_pu, 1)
                        emit_bc(63)

                    @block.scalar
                    def _(sc):
                        def emit_bcevac(itb):
                            sc.wait_ge(a_pb, itb + 1)
                            if itb >= 2:
                                sc.wait_ge(a_at, itb - 1)
                            sc.activation(scr_b[:, (itb % 2) * 512:((itb % 2) + 1) * 512],
                                          psum_bc[:], AF.Copy).then_inc(a_bce, 1)

                        for it in range(64):
                            for kc in range(16):
                                n = it * 16 + kc
                                sc.wait_ge(a_pe, n + 1)
                                if n >= 3:
                                    sc.wait_ge(a_pu, n - 2)
                                sc.activation(ptbuf[:, (n % 3) * 512:((n % 3) + 1) * 512],
                                              psum_st[n % 3][:], AF.Exp,
                                              bias=cbias[:, (it // 2):(it // 2) + 1]).then_inc(a_pt, 1)
                                if kc == 3 and it >= 1:
                                    emit_bcevac(it - 1)
                        emit_bcevac(63)

                    @block.vector
                    def _(v):
                        for it in range(64):
                            h = it // 2
                            qs = it % 2
                            v.wait_ge(a_pr, it + 1)
                            if it >= 2:
                                v.wait_ge(a_pb, it - 1)
                            v.reciprocal(rr_sb[0:1, (it % 2) * 512:((it % 2) + 1) * 512],
                                         psum_r[it % 2][:]).then_inc(a_rr, 1)
                            v.wait_ge(a_bce, it + 1)
                            v.wait_ge(a_pu, 16 * (it + 1))
                            v.tensor_tensor(att_slot(h)[:, qs * 512:(qs + 1) * 512],
                                            psum_av[it % 2][:],
                                            scr_b[:, (it % 2) * 512:((it % 2) + 1) * 512],
                                            ALU.mult).then_inc(a_at, 1)

            with ExitStack() as ph:
                PH = ph.enter_context
                wob = PH(nc.sbuf_tensor("wob", [128, 2 * DC * 128], BF16, side="right"))
                oabsb = PH(nc.sbuf_tensor("oabsb", [128, 2 * 512], BF16, side="right"))
                psum_o = [PH(nc.psum_tensor(f"po{i}", [128, 512], F32)) for i in range(4)]
                psum_so = PH(nc.psum_tensor("pso", [128, 1], F32))

                with nc.Block("P4a", no_gpsimd_drain=True) as block:
                    @block.sync
                    def _(s):
                        for oc in range(2, DC):
                            s.wait_ge(o_abs, 10 * (oc - 1))
                            s.dma_start(wof[:, (oc % 2) * D:((oc % 2) + 1) * D],
                                        wo_t[oc]).then_inc(o_w[oc % 2], 16)

                    @block.scalar
                    def _(sc):
                        for oc in range(DC):
                            p = oc % 2
                            sc.wait_ge(o_w[oc % 2], 16 * (oc // 2 + 1))
                            if oc >= 2:
                                sc.wait_ge(o_pe, 2 * (oc - 1))
                            sc.activation(wob[:, p * D:p * D + 2048], wof[:, p * D:p * D + 2048],
                                          AF.Sign).then_inc(o_abs, 1)
                            sc.activation(wob[:, p * D + 2048:(p + 1) * D],
                                          wof[:, p * D + 2048:(p + 1) * D],
                                          AF.Sign).then_inc(o_abs, 1)
                            for c in range(8):
                                if 8 * oc + c >= 2:
                                    sc.wait_ge(o_spe, 8 * oc + c - 1)
                                sc.activation(oabsb[:, (c % 2) * 512:((c % 2) + 1) * 512],
                                              wof[:, p * D + c * 512: p * D + (c + 1) * 512],
                                              AF.Abs).then_inc(o_abs, 1)

                    @block.tensor
                    def _(t):
                        for oc in range(DC):
                            p = oc % 2
                            for qs in range(2):
                                n = 2 * oc + qs
                                if n >= 4:
                                    t.wait_ge(o_ev, n - 3)
                                for hc in range(DC):
                                    if qs == 0 and hc == 0:
                                        t.wait_ge(o_abs, 10 * oc + 1)
                                    if qs == 0 and hc == 16:
                                        t.wait_ge(o_abs, 10 * oc + 2)
                                    ins = t.matmul(psum_o[n % 4][:],
                                                   wob[:, p * D + hc * 128: p * D + (hc + 1) * 128],
                                                   att_slot(hc)[:, qs * 512:(qs + 1) * 512],
                                                   start=(hc == 0), stop=(hc == DC - 1))
                                ins.then_inc(o_pe, 1)
                            if oc >= 1:
                                t.wait_ge(o_dv, 2 * oc - 1)
                            for c in range(8):
                                t.wait_ge(o_abs, 10 * oc + c + 3)
                                for jj in range(4):
                                    hc = 4 * c + jj
                                    ins = t.matmul(psum_so[:],
                                                   oabsb[:, (c % 2) * 512 + jj * 128:(c % 2) * 512 + (jj + 1) * 128],
                                                   ones_col[:],
                                                   start=(hc == 0), stop=(hc == DC - 1))
                                ins.then_inc(o_spe, 1)

                    @block.vector
                    def _(v):
                        for oc in range(DC):
                            p = oc % 2
                            v.wait_ge(o_spe, 8 * (oc + 1))
                            if oc >= 2:
                                v.wait_ge(o_dv, 2 * (oc - 1))
                            v.tensor_scalar(scol_tmp[:, p:p + 1], psum_so[:], INV4096, 1e-8,
                                            ALU.mult, ALU.max).then_inc(o_dv, 1)
                            v.wait_ge(o_dv, 2 * oc + 1)
                            v.tensor_tensor(ovec_sb[:, p:p + 1], scol_tmp[:, p:p + 1],
                                            osc_vec[:, oc:oc + 1], ALU.mult).then_inc(o_dv, 1)
                            v.wait_ge(o_dv, 2 * (oc + 1))
                            for qs in range(2):
                                n = 2 * oc + qs
                                v.wait_ge(o_pe, n + 1)
                                tgt = x_sb[:, oc * TOK + qs * 512: oc * TOK + (qs + 1) * 512]
                                v.scalar_tensor_tensor(tgt, psum_o[n % 4][:], ovec_sb[:, p:p + 1], tgt,
                                                       ALU.mult, ALU.add).then_inc(o_ev, 1)

        def emit_ln(name, span_lo, span_w, to_out, sq, pe, ev, dv, bpe, bev, nrm, base,
                    store_sem=None, tbi=0, prefetch=None):
            nspan = span_w // 512
            with ExitStack() as ph:
                PH = ph.enter_context
                psum_s1 = PH(nc.psum_tensor(f"ps1{name}", [1, span_w], F32))
                psum_s2 = PH(nc.psum_tensor(f"ps2{name}", [1, span_w], F32))
                psum_ab = PH(nc.psum_tensor(f"pab{name}", [128, 2 * span_w], F32))
                ln_r1 = PH(nc.sbuf_tensor(f"lr1{name}", [1, span_w], F32))
                ln_r2 = PH(nc.sbuf_tensor(f"lr2{name}", [1, span_w], F32))
                ln_mu = ln_r1
                ln_a = PH(nc.sbuf_tensor(f"la{name}", [1, span_w], F32))
                ln_b = PH(nc.sbuf_tensor(f"lb{name}", [1, span_w], F32))
                ln_tmp = PH(nc.sbuf_tensor(f"lt{name}", [1, span_w], F32))
                ln_sd = ln_tmp
                a_bc = PH(nc.sbuf_tensor(f"abc{name}", [128, span_w], BF16))
                b_bc = PH(nc.sbuf_tensor(f"bbc{name}", [128, span_w], BF16))

                NDV = 7 + DC

                with nc.Block(name, no_gpsimd_drain=True) as block:
                    if prefetch is not None:
                        @block.gpsimd
                        def _(g):
                            prefetch(g)

                    @block.scalar
                    def _(sc):
                        B = base * NDV
                        if base >= 1:
                            sc.wait_ge(dv, B)
                        for dc in range(DC):
                            if dc >= 2:
                                sc.wait_ge(pe, base * DC + dc - 1)
                            sc.activation(scr_a[:, (dc % 2) * span_w:((dc % 2) + 1) * span_w],
                                          x_sb[:, dc * TOK + span_lo: dc * TOK + span_lo + span_w],
                                          AF.Square).then_inc(sq, 1)
                        sc.wait_ge(pe, base * DC + DC)
                        sc.activation(ln_r1[0:1, 0:span_w], psum_s1[:], AF.Copy).then_inc(ev, 1)
                        sc.activation(ln_r2[0:1, 0:span_w], psum_s2[:], AF.Copy).then_inc(ev, 1)
                        sc.wait_ge(dv, B + 5)
                        sc.activation(ln_sd[0:1, 0:span_w], ln_r2[0:1, 0:span_w], AF.Sqrt).then_inc(ev, 1)
                        sc.wait_ge(bpe, base * 2 * nspan + 2 * nspan)
                        for j in range(nspan):
                            sc.activation(a_bc[:, j * 512:(j + 1) * 512],
                                          psum_ab[:, j * 512:(j + 1) * 512], AF.Copy).then_inc(bev, 1)
                            sc.activation(b_bc[:, j * 512:(j + 1) * 512],
                                          psum_ab[:, span_w + j * 512: span_w + (j + 1) * 512],
                                          AF.Copy).then_inc(bev, 1)

                    @block.tensor
                    def _(t):
                        B = base * NDV
                        for dc in range(DC):
                            t.wait_ge(sq, base * DC + dc + 1)
                            for j in range(nspan):
                                t.matmul(psum_s1[0:1, j * 512:(j + 1) * 512], ones_col[:],
                                         x_sb[:, dc * TOK + span_lo + j * 512:
                                              dc * TOK + span_lo + (j + 1) * 512],
                                         start=(dc == 0), stop=(dc == DC - 1))
                                ins = t.matmul(psum_s2[0:1, j * 512:(j + 1) * 512], ones_col[:],
                                               scr_a[:, (dc % 2) * span_w + j * 512:
                                                     (dc % 2) * span_w + (j + 1) * 512],
                                               start=(dc == 0), stop=(dc == DC - 1))
                            ins.then_inc(pe, 1)
                        t.wait_ge(dv, B + 7)
                        for j in range(nspan):
                            t.matmul(psum_ab[:, j * 512:(j + 1) * 512], ones_rowf[:],
                                     ln_a[0:1, j * 512:(j + 1) * 512], start=True, stop=True).then_inc(bpe, 1)
                            t.matmul(psum_ab[:, span_w + j * 512: span_w + (j + 1) * 512], ones_rowf[:],
                                     ln_b[0:1, j * 512:(j + 1) * 512], start=True, stop=True).then_inc(bpe, 1)

                    @block.vector
                    def _(v):
                        B = base * NDV
                        v.wait_ge(ev, base * 3 + 2)
                        v.tensor_scalar_mul(ln_mu[0:1, 0:span_w], ln_r1[0:1, 0:span_w], INV4096).then_inc(dv, 1)
                        v.tensor_scalar_mul(ln_r2[0:1, 0:span_w], ln_r2[0:1, 0:span_w], INV4096).then_inc(dv, 1)
                        v.wait_ge(dv, B + 1)
                        v.tensor_tensor(ln_tmp[0:1, 0:span_w], ln_mu[0:1, 0:span_w],
                                        ln_mu[0:1, 0:span_w], ALU.mult).then_inc(dv, 1)
                        v.wait_ge(dv, B + 3)
                        v.tensor_tensor(ln_r2[0:1, 0:span_w], ln_r2[0:1, 0:span_w],
                                        ln_tmp[0:1, 0:span_w], ALU.subtract).then_inc(dv, 1)
                        v.wait_ge(dv, B + 4)
                        v.tensor_scalar_add(ln_r2[0:1, 0:span_w], ln_r2[0:1, 0:span_w], EPS).then_inc(dv, 1)
                        v.wait_ge(ev, base * 3 + 3)
                        v.reciprocal(ln_a[0:1, 0:span_w], ln_sd[0:1, 0:span_w]).then_inc(dv, 1)
                        v.wait_ge(dv, B + 6)
                        v.scalar_tensor_tensor(ln_b[0:1, 0:span_w], ln_mu[0:1, 0:span_w], -1.0,
                                               ln_a[0:1, 0:span_w], ALU.mult, ALU.mult).then_inc(dv, 1)
                        v.wait_ge(bev, base * 2 * nspan + 2 * nspan)
                        for dc in range(DC):
                            src = x_sb[:, dc * TOK + span_lo: dc * TOK + span_lo + span_w]
                            tt = scr_a[:, (dc % 2) * span_w:((dc % 2) + 1) * span_w]
                            if dc >= 2:
                                v.wait_ge(nrm, base * DC + dc - 1)
                            v.tensor_tensor(tt, src, a_bc[:, 0:span_w], ALU.mult).then_inc(dv, 1)
                            v.wait_ge(dv, B + 7 + dc + 1)
                            if to_out:
                                stg = scr_b[:, (dc % 2) * 512:((dc % 2) + 1) * 512]
                                gdc = tbi * DC + dc
                                if gdc >= 2:
                                    v.wait_ge(store_sem[gdc % 2], 16 * (gdc // 2))
                                v.tensor_tensor(stg, tt, b_bc[:, 0:span_w], ALU.add).then_inc(nrm, 1)
                            else:
                                v.tensor_tensor(src, tt, b_bc[:, 0:span_w], ALU.add).then_inc(nrm, 1)

                    if to_out:
                        @block.sync
                        def _(s):
                            for dc in range(DC):
                                s.wait_ge(nrm, base * DC + dc + 1)
                                s.dma_start(out[dc, :, span_lo:span_lo + 512],
                                            scr_b[:, (dc % 2) * 512:((dc % 2) + 1) * 512]
                                            ).then_inc(store_sem[dc % 2], 16)
                            if tbi == 1:
                                s.wait_ge(store_sem[0], 16 * DC)
                                s.wait_ge(store_sem[1], 16 * DC)

        if debug:
            with nc.Block("DBGX1", no_gpsimd_drain=True) as block:
                @block.sync
                def _(s):
                    s.dma_start(dbg_x1[:], x_sb[:]).then_inc(dbg_s, 16)
                    s.wait_ge(dbg_s, 32)

        with ExitStack() as sc5:
            wgb = sc5.enter_context(nc.sbuf_tensor("wgb", [128, 2 * DC * 128], BF16, side="right"))
            wub = sc5.enter_context(nc.sbuf_tensor("wub", [128, 2 * DC * 128], BF16, side="right"))
            wdb = sc5.enter_context(nc.sbuf_tensor("wdb", [128, 4 * 1024], BF16, side="right"))

            def ffn_prefetch(tb):
                def pf(g):
                    for ic in range(2):
                        g.dma_start(wgb[:, (ic % 2) * D:((ic % 2) + 1) * D],
                                    wg_t[ic]).then_inc(f_w[ic % 2], 16)
                        g.dma_start(wub[:, (ic % 2) * D:((ic % 2) + 1) * D],
                                    wu_t[ic]).then_inc(f_wu[ic % 2], 16)
                return pf

            emit_ln("P4b", 0, TOK, False, l1_sq, l1_pe, l1_ev, l1_dv, l1_bpe, l1_bev, l1_n, 0,
                    prefetch=ffn_prefetch(0))

            for tb in range(2):
                lo = tb * 512
                schb = ExitStack()
                hbuf = schb.enter_context(nc.sbuf_tensor(f"hbuf{tb}", [128, IC * 512], BF16))
                with ExitStack() as ph:
                    PH = ph.enter_context
                    gstage = PH(nc.sbuf_tensor(f"gstage{tb}", [128, 2 * 512], F32, side="right"))
                    psum_g = [PH(nc.psum_tensor(f"pg{tb}{i}", [128, 512], F32)) for i in range(2)]
                    psum_u = [PH(nc.psum_tensor(f"pu{tb}{i}", [128, 512], F32)) for i in range(2)]

                    with nc.Block(f"P5a{tb}", no_gpsimd_drain=True) as block:
                        @block.gpsimd
                        def _(g):
                            for ic in range(2, IC):
                                g.wait_ge(f_mm, 2 * (tb * IC + ic - 1))
                                g.dma_start(wgb[:, (ic % 2) * D:((ic % 2) + 1) * D],
                                            wg_t[ic]).then_inc(f_w[ic % 2], 16)
                                g.dma_start(wub[:, (ic % 2) * D:((ic % 2) + 1) * D],
                                            wu_t[ic]).then_inc(f_wu[ic % 2], 16)
                            for npf in range(4):
                                n0 = tb * 8 * 43 + npf
                                og0, icp0 = npf // 43, npf % 43
                                g.dma_start(
                                    wdb[:, (n0 % 4) * 1024:((n0 % 4) + 1) * 1024].rearrange(
                                        "p (i c) -> p i c", i=2),
                                    wd_t[og0, 2 * icp0:2 * icp0 + 2].rearrange("i p c -> p i c"),
                                ).then_inc(d_w[n0 % 4], 16)

                        @block.tensor
                        def _(t):
                            for ic in range(IC):
                                n = tb * IC + ic
                                t.wait_ge(f_w[ic % 2], 16 * (tb * 43 + ic // 2 + 1))
                                if ic >= 2:
                                    t.wait_ge(f_h, tb * IC + ic - 1)
                                p = ic % 2
                                for dc in range(DC):
                                    ins = t.matmul(psum_g[p][:],
                                                   wgb[:, p * D + dc * 128: p * D + (dc + 1) * 128],
                                                   x_sb[:, dc * TOK + lo: dc * TOK + lo + 512],
                                                   start=(dc == 0), stop=(dc == DC - 1))
                                ins.then_inc(f_mm, 1)
                                t.wait_ge(f_wu[ic % 2], 16 * (tb * 43 + ic // 2 + 1))
                                for dc in range(DC):
                                    ins = t.matmul(psum_u[p][:],
                                                   wub[:, p * D + dc * 128: p * D + (dc + 1) * 128],
                                                   x_sb[:, dc * TOK + lo: dc * TOK + lo + 512],
                                                   start=(dc == 0), stop=(dc == DC - 1))
                                ins.then_inc(f_mm, 1)

                        @block.scalar
                        def _(sc):
                            for ic in range(IC):
                                n = tb * IC + ic
                                sc.wait_ge(f_mm, 2 * n + 1)
                                sc.activation(gstage[:, (ic % 2) * 512:((ic % 2) + 1) * 512],
                                              psum_g[ic % 2][:], AF.Silu,
                                              scale=gs_vec[:]).then_inc(f_g, 1)

                        @block.vector
                        def _(v):
                            for ic in range(IC):
                                n = tb * IC + ic
                                v.wait_ge(f_g, n + 1)
                                v.wait_ge(f_mm, 2 * n + 2)
                                v.scalar_tensor_tensor(hbuf[:, ic * 512:(ic + 1) * 512],
                                                       psum_u[ic % 2][:], usds_vec[:],
                                                       gstage[:, (ic % 2) * 512:((ic % 2) + 1) * 512],
                                                       ALU.mult, ALU.mult).then_inc(f_h, 1)

                with ExitStack() as ph:
                    PH = ph.enter_context
                    psum_d = [PH(nc.psum_tensor(f"pd{tb}{i}", [128, 512], F32)) for i in range(8)]

                    with nc.Block(f"P5b{tb}", no_gpsimd_drain=True) as block:
                        @block.gpsimd
                        def _(g):
                            for og in range(8):
                                for icp in range(43):
                                    n = (tb * 8 + og) * 43 + icp
                                    if n - tb * 8 * 43 < 4:
                                        continue
                                    g.wait_ge(d_mm, n - 3)
                                    g.dma_start(
                                        wdb[:, (n % 4) * 1024:((n % 4) + 1) * 1024].rearrange(
                                            "p (i c) -> p i c", i=2),
                                        wd_t[og, 2 * icp:2 * icp + 2].rearrange("i p c -> p i c"),
                                    ).then_inc(d_w[n % 4], 16)

                        @block.tensor
                        def _(t):
                            for og in range(8):
                                gp = og % 2
                                if og >= 2:
                                    t.wait_ge(d_ev, (tb * 8 + og - 1) * 4)
                                for icp in range(43):
                                    n = (tb * 8 + og) * 43 + icp
                                    t.wait_ge(d_w[n % 4], 16 * (n // 4 + 1))
                                    for sub in range(2):
                                        ic = 2 * icp + sub
                                        for oc in range(4):
                                            ins = t.matmul(psum_d[gp * 4 + oc][:],
                                                           wdb[:, (n % 4) * 1024 + sub * 512 + oc * 128:
                                                               (n % 4) * 1024 + sub * 512 + (oc + 1) * 128],
                                                           hbuf[:, ic * 512:(ic + 1) * 512],
                                                           start=(ic == 0), stop=(ic == IC - 1))
                                    ins.then_inc(d_mm, 1)

                        @block.vector
                        def _(v):
                            for og in range(8):
                                gp = og % 2
                                v.wait_ge(d_mm, (tb * 8 + og + 1) * 43)
                                for oc in range(4):
                                    ocf = og * 4 + oc
                                    tgt = x_sb[:, ocf * TOK + lo: ocf * TOK + lo + 512]
                                    v.tensor_tensor(tgt, psum_d[gp * 4 + oc][:], tgt, ALU.add).then_inc(d_ev, 1)

                schb.close()
                emit_ln(f"P5c{tb}", lo, 512, True, l2_sq, l2_pe, l2_ev, l2_dv, l2_bpe, l2_bev,
                        l2_n, tb, store_sem=s_out, tbi=tb,
                        prefetch=(ffn_prefetch(1) if tb == 0 else None))

    return nc


def _prep_weights(inputs):
    f32 = np.float32
    wq = np.asarray(inputs["L0_attn_q_weight"], f32)
    wk = np.asarray(inputs["L0_attn_k_weight"], f32)
    wv = np.asarray(inputs["L0_attn_v_weight"], f32)
    wo = np.asarray(inputs["L0_attn_o_weight"], f32)
    wg = np.asarray(inputs["L0_ffn_gate_w"], np.int32)
    wu = np.asarray(inputs["L0_ffn_up_w"], np.int32)
    wd = np.asarray(inputs["L0_ffn_down_w"], np.int32)

    d = {}
    d["wq_t"] = np.ascontiguousarray(
        wq.reshape(NH, 128, DC, 128).transpose(0, 3, 2, 1)).reshape(NH, 128, DC * 128)
    d["wk_t"] = np.ascontiguousarray(
        wk.reshape(NG, 128, DC, 128).transpose(0, 3, 2, 1)).reshape(NG, 128, DC * 128)
    d["wv_t"] = np.ascontiguousarray(wv.T).reshape(DC, 128, KV)
    d["wo_t"] = np.ascontiguousarray(
        wo.reshape(DC, 128, DC, 128).transpose(0, 3, 2, 1)).reshape(DC, 128, DC * 128)
    d["wg_t"] = np.ascontiguousarray(
        wg.reshape(IC, 128, DC, 128).transpose(0, 3, 2, 1)).reshape(IC, 128, DC * 128)
    d["wu_t"] = np.ascontiguousarray(
        wu.reshape(IC, 128, DC, 128).transpose(0, 3, 2, 1)).reshape(IC, 128, DC * 128)
    d["wd_t"] = np.ascontiguousarray(
        wd.reshape(8, 4, 128, IC, 128).transpose(0, 3, 4, 1, 2)).reshape(8, IC, 128, 512)
    d["qsc_r"] = np.ascontiguousarray(np.asarray(inputs["L0_attn_q_scale"], f32).reshape(1, D))
    d["ksc_r"] = np.ascontiguousarray(np.asarray(inputs["L0_attn_k_scale"], f32).reshape(1, KV))
    d["vsc_r"] = np.ascontiguousarray(np.asarray(inputs["L0_attn_v_scale"], f32).reshape(1, KV))
    d["osc_r"] = np.ascontiguousarray(np.asarray(inputs["L0_attn_o_scale"], f32).reshape(1, D))
    d["gs_d"] = np.asarray(inputs["L0_ffn_gate_s"], f32).reshape(1, 1)
    d["us_d"] = np.asarray(inputs["L0_ffn_up_s"], f32).reshape(1, 1)
    d["ds_d"] = np.asarray(inputs["L0_ffn_down_s"], f32).reshape(1, 1)
    return d


def kernel(**inputs):
    x = np.asarray(inputs["L0_attn_hidden_states"], np.float32)
    shared = _prep_weights(inputs)

    nc = build_nc(debug=DEBUG_DUMPS)

    in_maps = []
    for c in range(8):
        b, hh = c // 2, c % 2
        x_own = x[b, hh * TOK:(hh + 1) * TOK, :]
        x_oth = x[b, (1 - hh) * TOK:(2 - hh) * TOK, :]
        m = dict(shared)
        m["xo_t"] = np.ascontiguousarray(x_own.T).reshape(DC, 128, TOK)
        m["xr_t"] = np.ascontiguousarray(x_oth.T).reshape(DC, 128, TOK)
        in_maps.append(m)

    res = run_bass_kernel_spmd(nc, in_maps, core_ids=list(range(8)))

    y = np.empty((4, SEQ, D), np.float32)
    for c in range(8):
        b, hh = c // 2, c % 2
        o = np.asarray(res.results[c]["out"])
        y[b, hh * TOK:(hh + 1) * TOK, :] = o.reshape(D, TOK).T
    return y


if __name__ == "__main__":
    import importlib.util
    spec = importlib.util.spec_from_file_location("_r", "/root/problem/reference.py")
    _r = importlib.util.module_from_spec(spec)
    spec.loader.exec_module(_r)
    ins = _r.setup_inputs()
    got = kernel(**{k: np.asarray(v) for k, v in ins.items()})
    want = np.asarray(_r.reference(**ins))
    err = got - want
    rv = float(np.mean(err ** 2) / np.mean(want ** 2))
    mx = float(np.abs(got.astype(ml_dtypes.bfloat16).astype(np.float32)
                      - want.astype(ml_dtypes.bfloat16).astype(np.float32)).max())
    print("residual variance ratio:", rv, " max_abs_bf16:", mx)


# version 08
# speedup vs baseline: 1.2183x; 1.0708x over previous
import numpy as np
import ml_dtypes
from contextlib import ExitStack

import concourse.bass as bass
import concourse.mybir as mybir
from concourse.bass_utils import run_bass_kernel_spmd

F32 = mybir.dt.float32
BF16 = mybir.dt.bfloat16
I32 = mybir.dt.int32
AF = mybir.ActivationFunctionType
ALU = mybir.AluOpType
AX = mybir.AxisListType

D = 4096
KV = 1024
NH = 32
NG = 8
TOK = 1024
SEQ = 2048
IC = 86
DC = 32
INV4096 = 1.0 / 4096.0
INV_SQRT_HD = float(1.0 / np.sqrt(128.0))
EPS = 1e-5

DEBUG_DUMPS = False


def build_nc(debug=False):
    nc = bass.Bass("TRN2", target_bir_lowering=False)

    xo_t = nc.dram_tensor("xo_t", [DC, 128, TOK], F32, kind="ExternalInput")
    xr_t = nc.dram_tensor("xr_t", [DC, 128, TOK], F32, kind="ExternalInput")
    wq_t = nc.dram_tensor("wq_t", [NH, 128, DC * 128], F32, kind="ExternalInput")
    wk_t = nc.dram_tensor("wk_t", [NG, 128, DC * 128], F32, kind="ExternalInput")
    wv_t = nc.dram_tensor("wv_t", [DC, 128, KV], F32, kind="ExternalInput")
    wo_t = nc.dram_tensor("wo_t", [DC, 128, DC * 128], F32, kind="ExternalInput")
    wg_t = nc.dram_tensor("wg_t", [IC, 128, DC * 128], I32, kind="ExternalInput")
    wu_t = nc.dram_tensor("wu_t", [IC, 128, DC * 128], I32, kind="ExternalInput")
    wd_t = nc.dram_tensor("wd_t", [8, IC, 128, 512], I32, kind="ExternalInput")
    qsc_r = nc.dram_tensor("qsc_r", [1, D], F32, kind="ExternalInput")
    ksc_r = nc.dram_tensor("ksc_r", [1, KV], F32, kind="ExternalInput")
    vsc_r = nc.dram_tensor("vsc_r", [1, KV], F32, kind="ExternalInput")
    osc_r = nc.dram_tensor("osc_r", [1, D], F32, kind="ExternalInput")
    gs_d = nc.dram_tensor("gs_d", [1, 1], F32, kind="ExternalInput")
    us_d = nc.dram_tensor("us_d", [1, 1], F32, kind="ExternalInput")
    ds_d = nc.dram_tensor("ds_d", [1, 1], F32, kind="ExternalInput")
    out = nc.dram_tensor("out", [DC, 128, TOK], F32, kind="ExternalOutput")

    skind = "ExternalOutput" if debug else "Internal"
    k_s = nc.dram_tensor("k_s", [NG, 128, SEQ], BF16, kind=skind)
    v_s = nc.dram_tensor("v_s", [16, 128, KV], BF16, kind=skind)
    if debug:
        dbg_q = nc.dram_tensor("dbg_q", [128, 33 * TOK], BF16, kind="ExternalOutput")
        dbg_x1 = nc.dram_tensor("dbg_x1", [128, DC * TOK], BF16, kind="ExternalOutput")

    c_ones_col = nc.inline_tensor(np.ones((128, 1), dtype=ml_dtypes.bfloat16), name="c_ones_col")
    c_negones = nc.inline_tensor(-np.ones((1, 128), dtype=ml_dtypes.bfloat16), name="c_negones")
    c_ones_rowf = nc.inline_tensor(np.ones((1, 128), dtype=np.float32), name="c_ones_rowf")
    c_onef = nc.inline_tensor(np.ones((1, 1), dtype=np.float32), name="c_onef")

    with ExitStack() as outer:
        E = outer.enter_context
        ones_col = E(nc.sbuf_tensor("ones_col", [128, 1], BF16))
        negones = E(nc.sbuf_tensor("negones", [1, 128], BF16))
        ones_rowf = E(nc.sbuf_tensor("ones_rowf", [1, 128], F32))
        onef = E(nc.sbuf_tensor("onef", [1, 1], F32))
        gs_sb = E(nc.sbuf_tensor("gs_sb", [1, 1], F32))
        us_sb = E(nc.sbuf_tensor("us_sb", [1, 1], F32))
        ds_sb = E(nc.sbuf_tensor("ds_sb", [1, 1], F32))
        usds_row = E(nc.sbuf_tensor("usds_row", [1, 1], F32))
        qsc_vec = E(nc.sbuf_tensor("qsc_vec", [128, NH], F32))
        ksc_vec = E(nc.sbuf_tensor("ksc_vec", [128, NG], F32))
        osc_vec = E(nc.sbuf_tensor("osc_vec", [128, DC], F32))
        gs_vec = E(nc.sbuf_tensor("gs_vec", [128, 1], F32))
        usds_vec = E(nc.sbuf_tensor("usds_vec", [128, 1], F32))
        srow_sb = E(nc.sbuf_tensor("srow_sb", [1, 2 * 128], F32))
        spool = E(nc.sbuf_tensor("spool", [128, 2], F32))
        scol_tmp = E(nc.sbuf_tensor("scol_tmp", [128, 2], F32))
        kvec_sb = E(nc.sbuf_tensor("kvec_sb", [128, 2], F32))
        qvec_sb = E(nc.sbuf_tensor("qvec_sb", [128, 2], F32))
        ovec_sb = E(nc.sbuf_tensor("ovec_sb", [128, 2], F32))
        Mk2_sb = E(nc.sbuf_tensor("Mk2_sb", [1, NG], F32))
        nqx_sb = E(nc.sbuf_tensor("nqx_sb", [1, 2], F32))
        Mk2t_sb = E(nc.sbuf_tensor("Mk2t_sb", [1, 4], F32))
        cb_c = E(nc.sbuf_tensor("cb_c", [1, 2], F32))
        cb_n = E(nc.sbuf_tensor("cb_n", [1, 2], F32))
        cbias = E(nc.sbuf_tensor("cbias", [128, NH], F32))
        scr_a = E(nc.sbuf_tensor("scr_a", [128, 2048], BF16))
        scr_b = E(nc.sbuf_tensor("scr_b", [128, 1024], F32))

        x_sb = E(nc.sbuf_tensor("x_sb", [128, DC * TOK], BF16, side="right"))

        def S(name):
            return E(nc.semaphore(name))

        def S2(name):
            return [E(nc.semaphore(name + "0")), E(nc.semaphore(name + "1"))]

        s0_c = S("s0_c"); s0_x = S2("s0_x"); s0_pe = S("s0_pe"); s0_ae = S("s0_ae"); s0_dv = S("s0_dv")
        k_w = S2("k_w"); k_abs = S("k_abs")
        k_dv = S("k_dv"); k_pe = S("k_pe")
        k_ev = S("k_ev"); k_st = S2("k_st"); k_sq = S("k_sq"); k_nkpe = S("k_nkpe"); k_dvnk = S("k_dvnk"); k_dvnk2 = S("k_dvnk2")
        v_w = S2("v_w"); v_sgn = S("v_sgn"); v_svpe = S("v_svpe"); v_sve = S("v_sve")
        v_dvr = S("v_dvr"); v_bpe = S("v_bpe"); v_bev = S("v_bev"); v_pe = S("v_pe")
        v_ev = S("v_ev"); v_st = S2("v_st")
        q_w = S2("q_w"); q_abs = S("q_abs")
        q_dv = S("q_dv"); q_pe = S("q_pe")
        q_ev = S("q_ev"); q_sq = S("q_sq"); q_nqpe = S("q_nqpe"); q_cev = S("q_cev")
        a_k = S2("a_k"); a_v = S2("a_v"); a_pe = S("a_pe"); a_pt = S("a_pt"); a_pu = S("a_pu")
        a_pr = S("a_pr"); a_rr = S("a_rr"); a_pb = S("a_pb"); a_bce = S("a_bce"); a_at = S("a_at")
        o_w = S2("o_w"); o_abs = S("o_abs")
        o_dv = S("o_dv"); o_pe = S("o_pe"); o_ev = S("o_ev")
        l1_sq = S("l1_sq"); l1_pe = S("l1_pe"); l1_ev = S("l1_ev"); l1_dv = S("l1_dv")
        l1_bpe = S("l1_bpe"); l1_bev = S("l1_bev"); l1_n = S("l1_n")
        f_w = S2("f_w"); f_wu = S2("f_wu"); f_mm = S("f_mm"); f_g = S("f_g"); f_h = S("f_h")
        d_w = [S("d_w0"), S("d_w1"), S("d_w2"), S("d_w3")]; d_mm = S("d_mm"); d_ev = S("d_ev")
        l2_sq = S("l2_sq"); l2_pe = S("l2_pe"); l2_ev = S("l2_ev"); l2_dv = S("l2_dv")
        l2_bpe = S("l2_bpe"); l2_bev = S("l2_bev"); l2_n = S("l2_n"); s_out = S2("s_out")
        q_dvc = S("q_dvc"); q_cpe = S("q_cpe")
        k_pl = S("k_pl"); q_pl = S("q_pl"); o_pl = S("o_pl")
        dbg_s = S("dbg_s")

        with ExitStack() as ph:
            psum_t = [ph.enter_context(nc.psum_tensor(f"p0t{i}", [128, 1], F32)) for i in range(2)]
            qsc_row = ph.enter_context(nc.sbuf_tensor("qsc_row", [1, D], F32))
            ksc_row = ph.enter_context(nc.sbuf_tensor("ksc_row", [1, KV], F32))
            osc_row = ph.enter_context(nc.sbuf_tensor("osc_row", [1, D], F32))

            jobs = ([("q", j) for j in range(NH)] + [("k", j) for j in range(NG)]
                    + [("o", j) for j in range(DC)] + [("gs", 0), ("usds", 0)])

            with nc.Block("P0", no_gpsimd_drain=True) as block:
                @block.gpsimd
                def _(g):
                    g.dma_start(
                        x_sb[:].rearrange("p (c t) -> p c t", c=DC),
                        xo_t.rearrange("c p t -> p c t"),
                    ).then_inc(s0_x[0], 16)

                @block.sync
                def _(s):
                    for dst, src in [
                        (ones_col, c_ones_col), (negones, c_negones),
                        (ones_rowf, c_ones_rowf), (onef, c_onef),
                        (qsc_row, qsc_r), (ksc_row, ksc_r),
                        (osc_row, osc_r), (gs_sb, gs_d), (us_sb, us_d), (ds_sb, ds_d),
                    ]:
                        s.dma_start(dst[:], src[:]).then_inc(s0_c, 16)

                @block.vector
                def _(v):
                    v.wait_ge(s0_c, 160)
                    v.tensor_tensor(usds_row[:], us_sb[:], ds_sb[:], ALU.mult).then_inc(s0_dv, 1)

                @block.tensor
                def _(t):
                    t.wait_ge(s0_c, 160)
                    for cnt, (kind, j) in enumerate(jobs):
                        if cnt >= 2:
                            t.wait_ge(s0_ae, cnt - 1)
                        if kind == "q":
                            lhs = qsc_row[0:1, j * 128:(j + 1) * 128]
                            rhs = onef[:]
                        elif kind == "k":
                            lhs = ksc_row[0:1, j * 128:(j + 1) * 128]
                            rhs = onef[:]
                        elif kind == "o":
                            lhs = osc_row[0:1, j * 128:(j + 1) * 128]
                            rhs = onef[:]
                        elif kind == "gs":
                            lhs = ones_rowf[:]
                            rhs = gs_sb[:]
                        else:
                            t.wait_ge(s0_dv, 1)
                            lhs = ones_rowf[:]
                            rhs = usds_row[:]
                        t.matmul(psum_t[cnt % 2][:], lhs, rhs, start=True, stop=True).then_inc(s0_pe, 1)

                @block.scalar
                def _(sc):
                    for cnt, (kind, j) in enumerate(jobs):
                        sc.wait_ge(s0_pe, cnt + 1)
                        if kind == "q":
                            dst = qsc_vec[:, j:j + 1]
                        elif kind == "k":
                            dst = ksc_vec[:, j:j + 1]
                        elif kind == "o":
                            dst = osc_vec[:, j:j + 1]
                        elif kind == "gs":
                            dst = gs_vec[:]
                        else:
                            dst = usds_vec[:]
                        sc.activation(dst, psum_t[cnt % 2][:], AF.Copy).then_inc(s0_ae, 1)

        with ExitStack() as sc01:
            xr_sb = sc01.enter_context(nc.sbuf_tensor("xr_sb", [128, DC * TOK], BF16, side="right"))
            vsc_row = sc01.enter_context(nc.sbuf_tensor("vsc_row", [1, KV], F32))
            vrow_sb = sc01.enter_context(nc.sbuf_tensor("vrow_sb", [1, KV], F32))

            def xall(dc, sp):
                if sp < 2:
                    return x_sb[:, dc * TOK + sp * 512: dc * TOK + (sp + 1) * 512]
                return xr_sb[:, dc * TOK + (sp - 2) * 512: dc * TOK + (sp - 1) * 512]

            def xtile_all(dc, tc):
                if tc < 8:
                    return x_sb[:, dc * TOK + tc * 128: dc * TOK + (tc + 1) * 128]
                return xr_sb[:, dc * TOK + (tc - 8) * 128: dc * TOK + (tc - 7) * 128]

            with nc.Block("P0X", no_gpsimd_drain=True) as block:
                @block.gpsimd
                def _(g):
                    g.dma_start(
                        xr_sb[:].rearrange("p (c t) -> p c t", c=DC),
                        xr_t.rearrange("c p t -> p c t"),
                    ).then_inc(s0_x[1], 16)

                @block.sync
                def _(s):
                    s.dma_start(vsc_row[:], vsc_r[:]).then_inc(s0_c, 16)

            with ExitStack() as ph:
                PH = ph.enter_context
                wkf = PH(nc.sbuf_tensor("wkf", [128, 2 * DC * 128], F32, side="right"))
                wkb = PH(nc.sbuf_tensor("wkb", [128, 2 * DC * 128], BF16, side="right"))
                kstage = PH(nc.sbuf_tensor("kstage", [128, 2 * SEQ], BF16, side="right"))
                psum_k = [PH(nc.psum_tensor(f"pk{i}", [128, 512], F32)) for i in range(4)]
                psum_nk = [PH(nc.psum_tensor(f"pnk{i}", [1, 512], F32)) for i in range(2)]

                def kst(g):
                    return kstage[:, (g % 2) * SEQ:((g % 2) + 1) * SEQ]

                with nc.Block("P1K", no_gpsimd_drain=True) as block:
                    @block.sync
                    def _(s):
                        for g in range(NG):
                            if g >= 2:
                                s.wait_ge(k_abs, 2 * (g - 1))
                                s.wait_ge(k_pl, g - 1)
                            s.dma_start(wkf[:, (g % 2) * D:((g % 2) + 1) * D],
                                        wk_t[g]).then_inc(k_w[g % 2], 16)
                            if g >= 1:
                                s.wait_ge(k_ev, 4 * g)
                                s.dma_start(k_s[g - 1], kst(g - 1)).then_inc(k_st[(g - 1) % 2], 16)
                        s.wait_ge(k_ev, 4 * NG)
                        s.dma_start(k_s[NG - 1], kst(NG - 1)).then_inc(k_st[(NG - 1) % 2], 16)
                        s.wait_ge(k_st[0], 16 * (NG // 2))
                        s.wait_ge(k_st[1], 16 * (NG // 2))

                    @block.scalar
                    def _(sc):
                        for g in range(NG):
                            p = g % 2
                            sc.wait_ge(k_w[g % 2], 16 * (g // 2 + 1))
                            if g >= 2:
                                sc.wait_ge(k_pe, 4 * (g - 1))
                            sc.activation(wkb[:, p * D:p * D + 2048], wkf[:, p * D:p * D + 2048],
                                          AF.Sign).then_inc(k_abs, 1)
                            sc.activation(wkb[:, p * D + 2048:(p + 1) * D],
                                          wkf[:, p * D + 2048:(p + 1) * D],
                                          AF.Sign).then_inc(k_abs, 1)
                            for j in range(4):
                                sc.wait_ge(k_dv, 2 * (g + 1))
                                sc.wait_ge(k_pe, 4 * g + j + 1)
                                if g >= 2 and j == 0:
                                    sc.wait_ge(k_st[g % 2], 16 * (g // 2))
                                sc.activation(kstage[:, p * SEQ + j * 512: p * SEQ + (j + 1) * 512],
                                              psum_k[(4 * g + j) % 4][:], AF.Copy,
                                              scale=kvec_sb[:, p:p + 1]).then_inc(k_ev, 1)
                            if g >= 1:
                                sc.wait_ge(k_nkpe, 4 * g)
                            sc.wait_ge(k_ev, 4 * (g + 1))
                            for j in range(4):
                                sc.activation(scr_a[:, j * 512:(j + 1) * 512],
                                              kstage[:, p * SEQ + j * 512: p * SEQ + (j + 1) * 512],
                                              AF.Square).then_inc(k_sq, 1)

                    @block.tensor
                    def _(t):
                        def emit_nk(gg):
                            for j in range(4):
                                jj4 = 4 * gg + j
                                if jj4 >= 2:
                                    t.wait_ge(k_dvnk, jj4 - 1)
                                t.wait_ge(k_sq, jj4 + 1)
                                t.matmul(psum_nk[jj4 % 2][:], ones_col[:],
                                         scr_a[:, j * 512:(j + 1) * 512],
                                         start=True, stop=True).then_inc(k_nkpe, 1)

                        for g in range(NG):
                            p = g % 2
                            for j in range(4):
                                t.wait_ge(s0_x[0 if j < 2 else 1], 16)
                                if 4 * g + j >= 4:
                                    t.wait_ge(k_ev, 4 * g + j - 3)
                                for dc in range(DC):
                                    if j == 0 and dc == 0:
                                        t.wait_ge(k_abs, 2 * g + 1)
                                    if j == 0 and dc == 16:
                                        t.wait_ge(k_abs, 2 * g + 2)
                                    ins = t.matmul(psum_k[(4 * g + j) % 4][:],
                                                   wkb[:, p * D + dc * 128: p * D + (dc + 1) * 128],
                                                   xall(dc, j),
                                                   start=(dc == 0), stop=(dc == DC - 1))
                                ins.then_inc(k_pe, 1)
                            if g >= 1:
                                emit_nk(g - 1)
                        emit_nk(NG - 1)

                    @block.vector
                    def _(v):
                        for g in range(NG):
                            p = g % 2
                            v.wait_ge(k_w[g % 2], 16 * (g // 2 + 1))
                            if g >= 2:
                                v.wait_ge(k_dv, 2 * (g - 1))
                            v.tensor_reduce(spool[:, p:p + 1], wkf[:, p * D:(p + 1) * D],
                                            axis=AX.X, op=ALU.add,
                                            apply_absolute_value=True).then_inc(k_pl, 1)
                            v.wait_ge(k_pl, g + 1)
                            v.tensor_scalar(scol_tmp[:, p:p + 1], spool[:, p:p + 1], INV4096, 1e-8,
                                            ALU.mult, ALU.max).then_inc(k_dv, 1)
                            v.wait_ge(k_dv, 2 * g + 1)
                            if g >= 2:
                                v.wait_ge(k_ev, 4 * (g - 1))
                            v.tensor_tensor(kvec_sb[:, p:p + 1], scol_tmp[:, p:p + 1],
                                            ksc_vec[:, g:g + 1], ALU.mult).then_inc(k_dv, 1)
                            for j in range(4):
                                jj4 = 4 * g + j
                                v.wait_ge(k_nkpe, jj4 + 1)
                                if j == 0 and g >= 1:
                                    v.wait_ge(k_dvnk2, g)
                                v.reduce_max(Mk2t_sb[0:1, j:j + 1],
                                             psum_nk[jj4 % 2][:], axis=AX.X).then_inc(k_dvnk, 1)
                            v.wait_ge(k_dvnk, 4 * (g + 1))
                            v.tensor_reduce(Mk2_sb[0:1, g:g + 1], Mk2t_sb[0:1, 0:4], axis=AX.X,
                                            op=ALU.max).then_inc(k_dvnk2, 1)

            with ExitStack() as ph:
                PH = ph.enter_context
                wvf = PH(nc.sbuf_tensor("wvf", [128, 2 * KV], F32, side="right"))
                wvh = PH(nc.sbuf_tensor("wvh", [128, DC * 512], BF16, side="right"))
                vabs = PH(nc.sbuf_tensor("vabs", [128, 2 * KV], BF16, side="right"))
                vstage = PH(nc.sbuf_tensor("vstage", [128, 2 * 512], BF16, side="right"))
                psum_v = [PH(nc.psum_tensor(f"pv{i}", [128, 512], F32)) for i in range(2)]
                psum_sv = PH(nc.psum_tensor("psv", [1, KV], F32))
                psum_vb = PH(nc.psum_tensor("pvb", [128, 1024], F32))

                def vsgn_count(i):
                    return 2 * (i + 1) if i < DC else 2 * DC + (i - DC) + 1

                with nc.Block("P1V", no_gpsimd_drain=True) as block:
                    @block.sync
                    def _(s):
                        for half in range(2):
                            for dc in range(DC):
                                i = half * DC + dc
                                if i >= 2:
                                    s.wait_ge(v_sgn, vsgn_count(i - 2))
                                s.dma_start(wvf[:, (i % 2) * KV:((i % 2) + 1) * KV],
                                            wv_t[dc]).then_inc(v_w[i % 2], 16)

                    @block.gpsimd
                    def _(g):
                        for it in range(32):
                            g.wait_ge(v_ev, it + 1)
                            half, tc = it // 16, it % 16
                            g.dma_start(v_s[tc, :, half * 512:(half + 1) * 512],
                                        vstage[:, (it % 2) * 512:((it % 2) + 1) * 512]).then_inc(v_st[it % 2], 16)
                        g.wait_ge(v_st[0], 16 * 16)
                        g.wait_ge(v_st[1], 16 * 16)

                    @block.scalar
                    def _(sc):
                        for half in range(2):
                            if half == 1:
                                sc.wait_ge(v_pe, 16)
                            for dc in range(DC):
                                i = half * DC + dc
                                sc.wait_ge(v_w[i % 2], 16 * (i // 2 + 1))
                                sc.activation(wvh[:, dc * 512:(dc + 1) * 512],
                                              wvf[:, (i % 2) * KV + half * 512:(i % 2) * KV + (half + 1) * 512],
                                              AF.Sign).then_inc(v_sgn, 1)
                                if half == 0:
                                    if dc >= 2:
                                        sc.wait_ge(v_svpe, 2 * (dc - 1))
                                    sc.activation(vabs[:, (dc % 2) * KV:((dc % 2) + 1) * KV],
                                                  wvf[:, (i % 2) * KV:((i % 2) + 1) * KV],
                                                  AF.Abs).then_inc(v_sgn, 1)
                            if half == 0:
                                sc.wait_ge(v_svpe, 2 * DC)
                                sc.activation(vrow_sb[:], psum_sv[:], AF.Copy).then_inc(v_sve, 1)
                                sc.wait_ge(v_bpe, 2)
                                sc.activation(scr_b[:, 0:512], psum_vb[:, 0:512], AF.Copy).then_inc(v_bev, 1)
                                sc.activation(scr_b[:, 512:1024], psum_vb[:, 512:1024], AF.Copy).then_inc(v_bev, 1)

                    @block.tensor
                    def _(t):
                        t.wait_ge(s0_x[0], 16)
                        t.wait_ge(s0_x[1], 16)
                        for half in range(2):
                            if half == 0:
                                for dc in range(DC):
                                    t.wait_ge(v_sgn, 2 * (dc + 1))
                                    for j in range(2):
                                        t.matmul(psum_sv[0:1, j * 512:(j + 1) * 512], ones_col[:],
                                                 vabs[:, (dc % 2) * KV + j * 512:(dc % 2) * KV + (j + 1) * 512],
                                                 start=(dc == 0), stop=(dc == DC - 1)).then_inc(v_svpe, 1)
                                t.wait_ge(v_dvr, 2)
                                for j in range(2):
                                    t.matmul(psum_vb[:, j * 512:(j + 1) * 512], ones_rowf[:],
                                             vrow_sb[0:1, j * 512:(j + 1) * 512],
                                             start=True, stop=True).then_inc(v_bpe, 1)
                            for tc in range(16):
                                it = half * 16 + tc
                                if it >= 2:
                                    t.wait_ge(v_ev, it - 1)
                                for dc in range(DC):
                                    if tc == 0:
                                        t.wait_ge(v_sgn, (2 * dc + 1) if half == 0 else (2 * DC + dc + 1))
                                    ins = t.matmul(psum_v[it % 2][:], xtile_all(dc, tc),
                                                   wvh[:, dc * 512:(dc + 1) * 512],
                                                   start=(dc == 0), stop=(dc == DC - 1))
                                ins.then_inc(v_pe, 1)

                    @block.vector
                    def _(v):
                        v.wait_ge(v_sve, 1)
                        v.wait_ge(s0_c, 176)
                        v.tensor_scalar(vrow_sb[:], vrow_sb[:], INV4096, 1e-8, ALU.mult, ALU.max).then_inc(v_dvr, 1)
                        v.wait_ge(v_dvr, 1)
                        v.tensor_tensor(vrow_sb[:], vrow_sb[:], vsc_row[:], ALU.mult).then_inc(v_dvr, 1)
                        for it in range(32):
                            half = it // 16
                            v.wait_ge(v_pe, it + 1)
                            if it == 0:
                                v.wait_ge(v_bev, 2)
                            if it >= 2:
                                v.wait_ge(v_st[it % 2], 16 * (it // 2))
                            v.tensor_tensor(vstage[:, (it % 2) * 512:((it % 2) + 1) * 512],
                                            psum_v[it % 2][:],
                                            scr_b[:, half * 512:(half + 1) * 512],
                                            ALU.mult).then_inc(v_ev, 1)

        with ExitStack() as scA:
            A = scA.enter_context(nc.sbuf_tensor("A", [128, 33 * TOK], BF16))
            Kg = scA.enter_context(nc.sbuf_tensor("Kg", [128, 2 * SEQ], BF16, side="right"))
            Vg = scA.enter_context(nc.sbuf_tensor("Vg", [128, 2 * SEQ], BF16, side="right"))

            def q_slot(h):
                return A[:, h * TOK:(h + 1) * TOK]

            def att_slot(h):
                s = 32 if h == 0 else h - 1
                return A[:, s * TOK:(s + 1) * TOK]

            with ExitStack() as ph:
                PH = ph.enter_context
                wqf = PH(nc.sbuf_tensor("wqf", [128, 2 * DC * 128], F32, side="right"))
                wqb = PH(nc.sbuf_tensor("wqb", [128, 2 * DC * 128], BF16, side="right"))
                psum_q = [PH(nc.psum_tensor(f"pq{i}", [128, 512], F32)) for i in range(4)]
                psum_tq = PH(nc.psum_tensor("ptq", [128, 1], F32))

                dvc_after_max = {}
                dvc_after_neg = {}
                cnt = 0
                for h2 in range(NH + 2):
                    if 1 <= h2 <= NH:
                        cnt += 1
                        dvc_after_max[h2 - 1] = cnt
                    if 2 <= h2 <= NH + 1:
                        cnt += 1
                        dvc_after_neg[h2 - 2] = cnt
                cev_after_sqrt = {}
                cev_after_cbe = {}
                cnt = 0
                for h2 in range(NH + 2):
                    if 1 <= h2 <= NH:
                        cnt += 1
                        cev_after_sqrt[h2 - 1] = cnt
                    if 2 <= h2 <= NH + 1:
                        cnt += 1
                        cev_after_cbe[h2 - 2] = cnt
                psum_nq = PH(nc.psum_tensor("pnq", [1, TOK], F32))

                with nc.Block("P2", no_gpsimd_drain=True) as block:
                    @block.sync
                    def _(s):
                        for h in range(NH):
                            if h >= 2:
                                s.wait_ge(q_abs, 2 * (h - 1))
                                s.wait_ge(q_pl, h - 1)
                            s.dma_start(wqf[:, (h % 2) * D:((h % 2) + 1) * D],
                                        wq_t[h]).then_inc(q_w[h % 2], 16)
                        for g in range(2):
                            s.dma_start(Kg[:, (g % 2) * SEQ:((g % 2) + 1) * SEQ],
                                        k_s[g]).then_inc(a_k[g % 2], 16)
                            s.dma_start(
                                Vg[:, (g % 2) * SEQ:((g % 2) + 1) * SEQ].rearrange("p (k c) -> p k c", k=16),
                                v_s[:, :, g * 128:(g + 1) * 128].rearrange("k p c -> p k c"),
                            ).then_inc(a_v[g % 2], 16)

                    @block.scalar
                    def _(sc):
                        def emit_sqrt(hh2):
                            gg = hh2 // 4
                            sc.wait_ge(q_dvc, dvc_after_max[hh2])
                            if hh2 >= 2:
                                sc.wait_ge(q_dvc, dvc_after_neg[hh2 - 2])
                            sc.activation(cb_c[0:1, hh2 % 2:hh2 % 2 + 1],
                                          nqx_sb[0:1, hh2 % 2:hh2 % 2 + 1], AF.Sqrt,
                                          scale=Mk2_sb[0:1, gg:gg + 1]).then_inc(q_cev, 1)

                        def emit_cbe(hh2):
                            sc.wait_ge(q_cpe, hh2 + 1)
                            sc.activation(cbias[:, hh2:hh2 + 1], psum_tq[:], AF.Copy).then_inc(q_cev, 1)

                        for h in range(NH):
                            p = h % 2
                            sc.wait_ge(q_w[h % 2], 16 * (h // 2 + 1))
                            if h >= 2:
                                sc.wait_ge(q_pe, 2 * (h - 1))
                            sc.activation(wqb[:, p * D:p * D + 2048], wqf[:, p * D:p * D + 2048],
                                          AF.Sign).then_inc(q_abs, 1)
                            sc.activation(wqb[:, p * D + 2048:(p + 1) * D],
                                          wqf[:, p * D + 2048:(p + 1) * D],
                                          AF.Sign).then_inc(q_abs, 1)
                            for j in range(2):
                                sc.wait_ge(q_dv, 2 * (h + 1))
                                sc.wait_ge(q_pe, 2 * h + j + 1)
                                sc.activation(q_slot(h)[:, j * 512:(j + 1) * 512],
                                              psum_q[(2 * h + j) % 4][:], AF.Copy,
                                              scale=qvec_sb[:, p:p + 1]).then_inc(q_ev, 1)
                            if h >= 1:
                                sc.wait_ge(q_nqpe, h)
                            sc.wait_ge(q_ev, 2 * (h + 1))
                            for j in range(2):
                                sc.activation(scr_a[:, j * 512:(j + 1) * 512],
                                              q_slot(h)[:, j * 512:(j + 1) * 512],
                                              AF.Square).then_inc(q_sq, 1)
                            if h >= 1:
                                emit_sqrt(h - 1)
                            if h >= 2:
                                emit_cbe(h - 2)
                        emit_sqrt(NH - 1)
                        emit_cbe(NH - 2)
                        emit_cbe(NH - 1)

                    @block.tensor
                    def _(t):
                        def emit_nq(hh2):
                            if hh2 >= 1:
                                t.wait_ge(q_dvc, dvc_after_max[hh2 - 1])
                            for j in range(2):
                                t.wait_ge(q_sq, 2 * hh2 + j + 1)
                                ins = t.matmul(psum_nq[0:1, j * 512:(j + 1) * 512], ones_col[:],
                                               scr_a[:, j * 512:(j + 1) * 512],
                                               start=True, stop=True)
                                if j == 1:
                                    ins.then_inc(q_nqpe, 1)

                        def emit_bc(hh2):
                            t.wait_ge(q_dvc, dvc_after_neg[hh2])
                            if hh2 >= 1:
                                t.wait_ge(q_cev, cev_after_cbe[hh2 - 1])
                            t.matmul(psum_tq[:], ones_rowf[:], cb_n[0:1, hh2 % 2:hh2 % 2 + 1],
                                     start=True, stop=True).then_inc(q_cpe, 1)

                        for h in range(NH):
                            p = h % 2
                            for j in range(2):
                                n = 2 * h + j
                                if n >= 4:
                                    t.wait_ge(q_ev, n - 3)
                                for dc in range(DC):
                                    if j == 0 and dc == 0:
                                        t.wait_ge(q_abs, 2 * h + 1)
                                    if j == 0 and dc == 16:
                                        t.wait_ge(q_abs, 2 * h + 2)
                                    ins = t.matmul(psum_q[n % 4][:],
                                                   wqb[:, p * D + dc * 128: p * D + (dc + 1) * 128],
                                                   x_sb[:, dc * TOK + j * 512: dc * TOK + (j + 1) * 512],
                                                   start=(dc == 0), stop=(dc == DC - 1))
                                ins.then_inc(q_pe, 1)
                            if h >= 1:
                                emit_nq(h - 1)
                            if h >= 2:
                                emit_bc(h - 2)
                        emit_nq(NH - 1)
                        emit_bc(NH - 2)
                        emit_bc(NH - 1)

                    @block.vector
                    def _(v):
                        for h in range(NH + 2):
                            if h < NH:
                                p = h % 2
                                v.wait_ge(q_w[h % 2], 16 * (h // 2 + 1))
                                if h >= 2:
                                    v.wait_ge(q_dv, 2 * (h - 1))
                                v.tensor_reduce(spool[:, p:p + 1], wqf[:, p * D:(p + 1) * D],
                                                axis=AX.X, op=ALU.add,
                                                apply_absolute_value=True).then_inc(q_pl, 1)
                                v.wait_ge(q_pl, h + 1)
                                v.tensor_scalar(scol_tmp[:, p:p + 1], spool[:, p:p + 1], INV4096, 1e-8,
                                                ALU.mult, ALU.max).then_inc(q_dv, 1)
                                v.wait_ge(q_dv, 2 * h + 1)
                                if h >= 2:
                                    v.wait_ge(q_ev, 2 * (h - 1))
                                v.scalar_tensor_tensor(qvec_sb[:, p:p + 1], scol_tmp[:, p:p + 1],
                                                       INV_SQRT_HD, qsc_vec[:, h:h + 1],
                                                       ALU.mult, ALU.mult).then_inc(q_dv, 1)
                            if 1 <= h <= NH:
                                hm = h - 1
                                v.wait_ge(q_nqpe, hm + 1)
                                if hm >= 2:
                                    v.wait_ge(q_cev, cev_after_sqrt[hm - 2])
                                v.reduce_max(nqx_sb[0:1, hm % 2:hm % 2 + 1], psum_nq[:],
                                             axis=AX.X).then_inc(q_dvc, 1)
                            if 2 <= h <= NH + 1:
                                hn = h - 2
                                v.wait_ge(q_cev, cev_after_sqrt[hn])
                                if hn >= 2:
                                    v.wait_ge(q_cpe, hn - 1)
                                v.tensor_scalar_mul(cb_n[0:1, hn % 2:hn % 2 + 1],
                                                    cb_c[0:1, hn % 2:hn % 2 + 1], -1.0).then_inc(q_dvc, 1)

            if debug:
                with nc.Block("DBGQ", no_gpsimd_drain=True) as block:
                    @block.sync
                    def _(s):
                        s.dma_start(dbg_q[:], A[:]).then_inc(dbg_s, 16)
                        s.wait_ge(dbg_s, 16)

            wof = scA.enter_context(nc.sbuf_tensor("wof", [128, 2 * DC * 128], F32, side="right"))

            with ExitStack() as ph:
                PH = ph.enter_context
                rr_sb = PH(nc.sbuf_tensor("rr_sb", [1, 2 * 512], F32, side="right"))
                ptbuf = PH(nc.sbuf_tensor("ptbuf", [128, 3 * 512], BF16, side="right"))
                psum_st = [PH(nc.psum_tensor(f"pst{i}", [128, 512], F32)) for i in range(3)]
                psum_av = [PH(nc.psum_tensor(f"pav{i}", [128, 512], F32)) for i in range(2)]
                psum_r = [PH(nc.psum_tensor(f"pr{i}", [1, 512], F32)) for i in range(2)]
                psum_bc = PH(nc.psum_tensor("pbc", [128, 512], F32))

                with nc.Block("P3", no_gpsimd_drain=True) as block:
                    @block.sync
                    def _(s):
                        for g in range(2, NG):
                            s.wait_ge(a_pe, 128 * (g - 1))
                            s.dma_start(Kg[:, (g % 2) * SEQ:((g % 2) + 1) * SEQ], k_s[g]).then_inc(a_k[g % 2], 16)
                            s.wait_ge(a_pu, 128 * (g - 1))
                            s.dma_start(
                                Vg[:, (g % 2) * SEQ:((g % 2) + 1) * SEQ].rearrange("p (k c) -> p k c", k=16),
                                v_s[:, :, g * 128:(g + 1) * 128].rearrange("k p c -> p k c"),
                            ).then_inc(a_v[g % 2], 16)
                        for oc in range(2):
                            s.wait_ge(a_pe, 128 * (NG - 1))
                            s.dma_start(wof[:, (oc % 2) * D:((oc % 2) + 1) * D],
                                        wo_t[oc]).then_inc(o_w[oc % 2], 16)

                    @block.tensor
                    def _(t):
                        def emit_bc(itb):
                            t.wait_ge(a_rr, itb + 1)
                            if itb >= 1:
                                t.wait_ge(a_bce, itb)
                            t.matmul(psum_bc[:], ones_rowf[:],
                                     rr_sb[0:1, (itb % 2) * 512:((itb % 2) + 1) * 512],
                                     start=True, stop=True).then_inc(a_pb, 1)

                        for g in range(NG):
                            gp = g % 2
                            for hh in range(4):
                                h = g * 4 + hh
                                for qs in range(2):
                                    it = h * 2 + qs
                                    for kc in range(17):
                                        if kc < 16:
                                            n = it * 16 + kc
                                            if kc == 0:
                                                t.wait_ge(a_k[g % 2], 16 * (g // 2 + 1))
                                                t.wait_ge(a_v[g % 2], 16 * (g // 2 + 1))
                                            if n >= 3:
                                                t.wait_ge(a_pt, n - 2)
                                            t.matmul(psum_st[n % 3][:],
                                                     Kg[:, gp * SEQ + kc * 128: gp * SEQ + (kc + 1) * 128],
                                                     q_slot(h)[:, qs * 512:(qs + 1) * 512],
                                                     start=True, stop=True).then_inc(a_pe, 1)
                                        if kc == 2 and it >= 1:
                                            emit_bc(it - 1)
                                        if kc >= 1:
                                            km = kc - 1
                                            nm = it * 16 + km
                                            t.wait_ge(a_pt, nm + 1)
                                            if km == 0 and it >= 2:
                                                t.wait_ge(a_rr, it - 1)
                                            ins_r = t.matmul(psum_r[it % 2][:], ones_col[:],
                                                             ptbuf[:, (nm % 3) * 512:((nm % 3) + 1) * 512],
                                                             start=(km == 0), stop=(km == 15))
                                            if km == 15:
                                                ins_r.then_inc(a_pr, 1)
                                            if km == 0 and it >= 2:
                                                t.wait_ge(a_at, it - 1)
                                            t.matmul(psum_av[it % 2][:],
                                                     Vg[:, gp * SEQ + km * 128: gp * SEQ + (km + 1) * 128],
                                                     ptbuf[:, (nm % 3) * 512:((nm % 3) + 1) * 512],
                                                     start=(km == 0), stop=(km == 15)).then_inc(a_pu, 1)
                        emit_bc(63)

                    @block.scalar
                    def _(sc):
                        def emit_bcevac(itb):
                            sc.wait_ge(a_pb, itb + 1)
                            if itb >= 2:
                                sc.wait_ge(a_at, itb - 1)
                            sc.activation(scr_b[:, (itb % 2) * 512:((itb % 2) + 1) * 512],
                                          psum_bc[:], AF.Copy).then_inc(a_bce, 1)

                        for it in range(64):
                            for kc in range(16):
                                n = it * 16 + kc
                                sc.wait_ge(a_pe, n + 1)
                                if n >= 3:
                                    sc.wait_ge(a_pu, n - 2)
                                sc.activation(ptbuf[:, (n % 3) * 512:((n % 3) + 1) * 512],
                                              psum_st[n % 3][:], AF.Exp,
                                              bias=cbias[:, (it // 2):(it // 2) + 1]).then_inc(a_pt, 1)
                                if kc == 3 and it >= 1:
                                    emit_bcevac(it - 1)
                        emit_bcevac(63)

                    @block.vector
                    def _(v):
                        for it in range(64):
                            h = it // 2
                            qs = it % 2
                            v.wait_ge(a_pr, it + 1)
                            if it >= 2:
                                v.wait_ge(a_pb, it - 1)
                            v.reciprocal(rr_sb[0:1, (it % 2) * 512:((it % 2) + 1) * 512],
                                         psum_r[it % 2][:]).then_inc(a_rr, 1)
                            v.wait_ge(a_bce, it + 1)
                            v.wait_ge(a_pu, 16 * (it + 1))
                            v.tensor_tensor(att_slot(h)[:, qs * 512:(qs + 1) * 512],
                                            psum_av[it % 2][:],
                                            scr_b[:, (it % 2) * 512:((it % 2) + 1) * 512],
                                            ALU.mult).then_inc(a_at, 1)

            with ExitStack() as ph:
                PH = ph.enter_context
                wob = PH(nc.sbuf_tensor("wob", [128, 2 * DC * 128], BF16, side="right"))
                psum_o = [PH(nc.psum_tensor(f"po{i}", [128, 512], F32)) for i in range(4)]

                with nc.Block("P4a", no_gpsimd_drain=True) as block:
                    @block.sync
                    def _(s):
                        for oc in range(2, DC):
                            s.wait_ge(o_abs, 2 * (oc - 1))
                            s.wait_ge(o_pl, oc - 1)
                            s.dma_start(wof[:, (oc % 2) * D:((oc % 2) + 1) * D],
                                        wo_t[oc]).then_inc(o_w[oc % 2], 16)

                    @block.scalar
                    def _(sc):
                        for oc in range(DC):
                            p = oc % 2
                            sc.wait_ge(o_w[oc % 2], 16 * (oc // 2 + 1))
                            if oc >= 2:
                                sc.wait_ge(o_pe, 2 * (oc - 1))
                            sc.activation(wob[:, p * D:p * D + 2048], wof[:, p * D:p * D + 2048],
                                          AF.Sign).then_inc(o_abs, 1)
                            sc.activation(wob[:, p * D + 2048:(p + 1) * D],
                                          wof[:, p * D + 2048:(p + 1) * D],
                                          AF.Sign).then_inc(o_abs, 1)

                    @block.tensor
                    def _(t):
                        for oc in range(DC):
                            p = oc % 2
                            for qs in range(2):
                                n = 2 * oc + qs
                                if n >= 4:
                                    t.wait_ge(o_ev, n - 3)
                                for hc in range(DC):
                                    if qs == 0 and hc == 0:
                                        t.wait_ge(o_abs, 2 * oc + 1)
                                    if qs == 0 and hc == 16:
                                        t.wait_ge(o_abs, 2 * oc + 2)
                                    ins = t.matmul(psum_o[n % 4][:],
                                                   wob[:, p * D + hc * 128: p * D + (hc + 1) * 128],
                                                   att_slot(hc)[:, qs * 512:(qs + 1) * 512],
                                                   start=(hc == 0), stop=(hc == DC - 1))
                                ins.then_inc(o_pe, 1)

                    @block.vector
                    def _(v):
                        for oc in range(DC):
                            p = oc % 2
                            v.wait_ge(o_w[oc % 2], 16 * (oc // 2 + 1))
                            if oc >= 2:
                                v.wait_ge(o_dv, 2 * (oc - 1))
                            v.tensor_reduce(spool[:, p:p + 1], wof[:, p * D:(p + 1) * D],
                                            axis=AX.X, op=ALU.add,
                                            apply_absolute_value=True).then_inc(o_pl, 1)
                            v.wait_ge(o_pl, oc + 1)
                            v.tensor_scalar(scol_tmp[:, p:p + 1], spool[:, p:p + 1], INV4096, 1e-8,
                                            ALU.mult, ALU.max).then_inc(o_dv, 1)
                            v.wait_ge(o_dv, 2 * oc + 1)
                            v.tensor_tensor(ovec_sb[:, p:p + 1], scol_tmp[:, p:p + 1],
                                            osc_vec[:, oc:oc + 1], ALU.mult).then_inc(o_dv, 1)
                            v.wait_ge(o_dv, 2 * (oc + 1))
                            for qs in range(2):
                                n = 2 * oc + qs
                                v.wait_ge(o_pe, n + 1)
                                tgt = x_sb[:, oc * TOK + qs * 512: oc * TOK + (qs + 1) * 512]
                                v.scalar_tensor_tensor(tgt, psum_o[n % 4][:], ovec_sb[:, p:p + 1], tgt,
                                                       ALU.mult, ALU.add).then_inc(o_ev, 1)

        def emit_ln(name, span_lo, span_w, to_out, sq, pe, ev, dv, bpe, bev, nrm, base,
                    store_sem=None, tbi=0, prefetch=None):
            nspan = span_w // 512
            with ExitStack() as ph:
                PH = ph.enter_context
                psum_s1 = PH(nc.psum_tensor(f"ps1{name}", [1, span_w], F32))
                psum_s2 = PH(nc.psum_tensor(f"ps2{name}", [1, span_w], F32))
                psum_ab = PH(nc.psum_tensor(f"pab{name}", [128, 2 * span_w], F32))
                ln_r1 = PH(nc.sbuf_tensor(f"lr1{name}", [1, span_w], F32))
                ln_r2 = PH(nc.sbuf_tensor(f"lr2{name}", [1, span_w], F32))
                ln_mu = ln_r1
                ln_a = PH(nc.sbuf_tensor(f"la{name}", [1, span_w], F32))
                ln_b = PH(nc.sbuf_tensor(f"lb{name}", [1, span_w], F32))
                ln_tmp = PH(nc.sbuf_tensor(f"lt{name}", [1, span_w], F32))
                ln_sd = ln_tmp
                a_bc = PH(nc.sbuf_tensor(f"abc{name}", [128, span_w], BF16))
                b_bc = PH(nc.sbuf_tensor(f"bbc{name}", [128, span_w], BF16))

                NDV = 7 + DC

                with nc.Block(name, no_gpsimd_drain=True) as block:
                    if prefetch is not None:
                        @block.gpsimd
                        def _(g):
                            prefetch(g)

                    @block.scalar
                    def _(sc):
                        B = base * NDV
                        if base >= 1:
                            sc.wait_ge(dv, B)
                        for dc in range(DC):
                            if dc >= 2:
                                sc.wait_ge(pe, base * DC + dc - 1)
                            sc.activation(scr_a[:, (dc % 2) * span_w:((dc % 2) + 1) * span_w],
                                          x_sb[:, dc * TOK + span_lo: dc * TOK + span_lo + span_w],
                                          AF.Square).then_inc(sq, 1)
                        sc.wait_ge(pe, base * DC + DC)
                        sc.activation(ln_r1[0:1, 0:span_w], psum_s1[:], AF.Copy).then_inc(ev, 1)
                        sc.activation(ln_r2[0:1, 0:span_w], psum_s2[:], AF.Copy).then_inc(ev, 1)
                        sc.wait_ge(dv, B + 5)
                        sc.activation(ln_sd[0:1, 0:span_w], ln_r2[0:1, 0:span_w], AF.Sqrt).then_inc(ev, 1)
                        sc.wait_ge(bpe, base * 2 * nspan + 2 * nspan)
                        for j in range(nspan):
                            sc.activation(a_bc[:, j * 512:(j + 1) * 512],
                                          psum_ab[:, j * 512:(j + 1) * 512], AF.Copy).then_inc(bev, 1)
                            sc.activation(b_bc[:, j * 512:(j + 1) * 512],
                                          psum_ab[:, span_w + j * 512: span_w + (j + 1) * 512],
                                          AF.Copy).then_inc(bev, 1)

                    @block.tensor
                    def _(t):
                        B = base * NDV
                        for dc in range(DC):
                            t.wait_ge(sq, base * DC + dc + 1)
                            for j in range(nspan):
                                t.matmul(psum_s1[0:1, j * 512:(j + 1) * 512], ones_col[:],
                                         x_sb[:, dc * TOK + span_lo + j * 512:
                                              dc * TOK + span_lo + (j + 1) * 512],
                                         start=(dc == 0), stop=(dc == DC - 1))
                                ins = t.matmul(psum_s2[0:1, j * 512:(j + 1) * 512], ones_col[:],
                                               scr_a[:, (dc % 2) * span_w + j * 512:
                                                     (dc % 2) * span_w + (j + 1) * 512],
                                               start=(dc == 0), stop=(dc == DC - 1))
                            ins.then_inc(pe, 1)
                        t.wait_ge(dv, B + 7)
                        for j in range(nspan):
                            t.matmul(psum_ab[:, j * 512:(j + 1) * 512], ones_rowf[:],
                                     ln_a[0:1, j * 512:(j + 1) * 512], start=True, stop=True).then_inc(bpe, 1)
                            t.matmul(psum_ab[:, span_w + j * 512: span_w + (j + 1) * 512], ones_rowf[:],
                                     ln_b[0:1, j * 512:(j + 1) * 512], start=True, stop=True).then_inc(bpe, 1)

                    @block.vector
                    def _(v):
                        B = base * NDV
                        v.wait_ge(ev, base * 3 + 2)
                        v.tensor_scalar_mul(ln_mu[0:1, 0:span_w], ln_r1[0:1, 0:span_w], INV4096).then_inc(dv, 1)
                        v.tensor_scalar_mul(ln_r2[0:1, 0:span_w], ln_r2[0:1, 0:span_w], INV4096).then_inc(dv, 1)
                        v.wait_ge(dv, B + 1)
                        v.tensor_tensor(ln_tmp[0:1, 0:span_w], ln_mu[0:1, 0:span_w],
                                        ln_mu[0:1, 0:span_w], ALU.mult).then_inc(dv, 1)
                        v.wait_ge(dv, B + 3)
                        v.tensor_tensor(ln_r2[0:1, 0:span_w], ln_r2[0:1, 0:span_w],
                                        ln_tmp[0:1, 0:span_w], ALU.subtract).then_inc(dv, 1)
                        v.wait_ge(dv, B + 4)
                        v.tensor_scalar_add(ln_r2[0:1, 0:span_w], ln_r2[0:1, 0:span_w], EPS).then_inc(dv, 1)
                        v.wait_ge(ev, base * 3 + 3)
                        v.reciprocal(ln_a[0:1, 0:span_w], ln_sd[0:1, 0:span_w]).then_inc(dv, 1)
                        v.wait_ge(dv, B + 6)
                        v.scalar_tensor_tensor(ln_b[0:1, 0:span_w], ln_mu[0:1, 0:span_w], -1.0,
                                               ln_a[0:1, 0:span_w], ALU.mult, ALU.mult).then_inc(dv, 1)
                        v.wait_ge(bev, base * 2 * nspan + 2 * nspan)
                        for dc in range(DC):
                            src = x_sb[:, dc * TOK + span_lo: dc * TOK + span_lo + span_w]
                            tt = scr_a[:, (dc % 2) * span_w:((dc % 2) + 1) * span_w]
                            if dc >= 2:
                                v.wait_ge(nrm, base * DC + dc - 1)
                            v.tensor_tensor(tt, src, a_bc[:, 0:span_w], ALU.mult).then_inc(dv, 1)
                            v.wait_ge(dv, B + 7 + dc + 1)
                            if to_out:
                                stg = scr_b[:, (dc % 2) * 512:((dc % 2) + 1) * 512]
                                gdc = tbi * DC + dc
                                if gdc >= 2:
                                    v.wait_ge(store_sem[gdc % 2], 16 * (gdc // 2))
                                v.tensor_tensor(stg, tt, b_bc[:, 0:span_w], ALU.add).then_inc(nrm, 1)
                            else:
                                v.tensor_tensor(src, tt, b_bc[:, 0:span_w], ALU.add).then_inc(nrm, 1)

                    if to_out:
                        @block.sync
                        def _(s):
                            for dc in range(DC):
                                s.wait_ge(nrm, base * DC + dc + 1)
                                s.dma_start(out[dc, :, span_lo:span_lo + 512],
                                            scr_b[:, (dc % 2) * 512:((dc % 2) + 1) * 512]
                                            ).then_inc(store_sem[dc % 2], 16)
                            if tbi == 1:
                                s.wait_ge(store_sem[0], 16 * DC)
                                s.wait_ge(store_sem[1], 16 * DC)

        if debug:
            with nc.Block("DBGX1", no_gpsimd_drain=True) as block:
                @block.sync
                def _(s):
                    s.dma_start(dbg_x1[:], x_sb[:]).then_inc(dbg_s, 16)
                    s.wait_ge(dbg_s, 32)

        with ExitStack() as sc5:
            wgb = sc5.enter_context(nc.sbuf_tensor("wgb", [128, 2 * DC * 128], BF16, side="right"))
            wub = sc5.enter_context(nc.sbuf_tensor("wub", [128, 2 * DC * 128], BF16, side="right"))
            wdb = sc5.enter_context(nc.sbuf_tensor("wdb", [128, 4 * 1024], BF16, side="right"))

            def ffn_prefetch(tb):
                def pf(g):
                    for ic in range(2):
                        g.dma_start(wgb[:, (ic % 2) * D:((ic % 2) + 1) * D],
                                    wg_t[ic]).then_inc(f_w[ic % 2], 16)
                        g.dma_start(wub[:, (ic % 2) * D:((ic % 2) + 1) * D],
                                    wu_t[ic]).then_inc(f_wu[ic % 2], 16)
                return pf

            emit_ln("P4b", 0, TOK, False, l1_sq, l1_pe, l1_ev, l1_dv, l1_bpe, l1_bev, l1_n, 0,
                    prefetch=ffn_prefetch(0))

            for tb in range(2):
                lo = tb * 512
                schb = ExitStack()
                hbuf = schb.enter_context(nc.sbuf_tensor(f"hbuf{tb}", [128, IC * 512], BF16))
                with ExitStack() as ph:
                    PH = ph.enter_context
                    gstage = PH(nc.sbuf_tensor(f"gstage{tb}", [128, 2 * 512], F32, side="right"))
                    psum_g = [PH(nc.psum_tensor(f"pg{tb}{i}", [128, 512], F32)) for i in range(2)]
                    psum_u = [PH(nc.psum_tensor(f"pu{tb}{i}", [128, 512], F32)) for i in range(2)]

                    with nc.Block(f"P5a{tb}", no_gpsimd_drain=True) as block:
                        @block.gpsimd
                        def _(g):
                            for ic in range(2, IC):
                                g.wait_ge(f_mm, 2 * (tb * IC + ic - 1))
                                g.dma_start(wgb[:, (ic % 2) * D:((ic % 2) + 1) * D],
                                            wg_t[ic]).then_inc(f_w[ic % 2], 16)
                                g.dma_start(wub[:, (ic % 2) * D:((ic % 2) + 1) * D],
                                            wu_t[ic]).then_inc(f_wu[ic % 2], 16)
                            for npf in range(4):
                                n0 = tb * 8 * 43 + npf
                                og0, icp0 = npf // 43, npf % 43
                                g.dma_start(
                                    wdb[:, (n0 % 4) * 1024:((n0 % 4) + 1) * 1024].rearrange(
                                        "p (i c) -> p i c", i=2),
                                    wd_t[og0, 2 * icp0:2 * icp0 + 2].rearrange("i p c -> p i c"),
                                ).then_inc(d_w[n0 % 4], 16)

                        @block.tensor
                        def _(t):
                            for ic in range(IC):
                                n = tb * IC + ic
                                t.wait_ge(f_w[ic % 2], 16 * (tb * 43 + ic // 2 + 1))
                                if ic >= 2:
                                    t.wait_ge(f_h, tb * IC + ic - 1)
                                p = ic % 2
                                for dc in range(DC):
                                    ins = t.matmul(psum_g[p][:],
                                                   wgb[:, p * D + dc * 128: p * D + (dc + 1) * 128],
                                                   x_sb[:, dc * TOK + lo: dc * TOK + lo + 512],
                                                   start=(dc == 0), stop=(dc == DC - 1))
                                ins.then_inc(f_mm, 1)
                                t.wait_ge(f_wu[ic % 2], 16 * (tb * 43 + ic // 2 + 1))
                                for dc in range(DC):
                                    ins = t.matmul(psum_u[p][:],
                                                   wub[:, p * D + dc * 128: p * D + (dc + 1) * 128],
                                                   x_sb[:, dc * TOK + lo: dc * TOK + lo + 512],
                                                   start=(dc == 0), stop=(dc == DC - 1))
                                ins.then_inc(f_mm, 1)

                        @block.scalar
                        def _(sc):
                            for ic in range(IC):
                                n = tb * IC + ic
                                sc.wait_ge(f_mm, 2 * n + 1)
                                sc.activation(gstage[:, (ic % 2) * 512:((ic % 2) + 1) * 512],
                                              psum_g[ic % 2][:], AF.Silu,
                                              scale=gs_vec[:]).then_inc(f_g, 1)

                        @block.vector
                        def _(v):
                            for ic in range(IC):
                                n = tb * IC + ic
                                v.wait_ge(f_g, n + 1)
                                v.wait_ge(f_mm, 2 * n + 2)
                                v.scalar_tensor_tensor(hbuf[:, ic * 512:(ic + 1) * 512],
                                                       psum_u[ic % 2][:], usds_vec[:],
                                                       gstage[:, (ic % 2) * 512:((ic % 2) + 1) * 512],
                                                       ALU.mult, ALU.mult).then_inc(f_h, 1)

                with ExitStack() as ph:
                    PH = ph.enter_context
                    psum_d = [PH(nc.psum_tensor(f"pd{tb}{i}", [128, 512], F32)) for i in range(8)]

                    with nc.Block(f"P5b{tb}", no_gpsimd_drain=True) as block:
                        @block.gpsimd
                        def _(g):
                            for og in range(8):
                                for icp in range(43):
                                    n = (tb * 8 + og) * 43 + icp
                                    if n - tb * 8 * 43 < 4:
                                        continue
                                    g.wait_ge(d_mm, n - 3)
                                    g.dma_start(
                                        wdb[:, (n % 4) * 1024:((n % 4) + 1) * 1024].rearrange(
                                            "p (i c) -> p i c", i=2),
                                        wd_t[og, 2 * icp:2 * icp + 2].rearrange("i p c -> p i c"),
                                    ).then_inc(d_w[n % 4], 16)

                        @block.tensor
                        def _(t):
                            for og in range(8):
                                gp = og % 2
                                if og >= 2:
                                    t.wait_ge(d_ev, (tb * 8 + og - 1) * 4)
                                for icp in range(43):
                                    n = (tb * 8 + og) * 43 + icp
                                    t.wait_ge(d_w[n % 4], 16 * (n // 4 + 1))
                                    for sub in range(2):
                                        ic = 2 * icp + sub
                                        for oc in range(4):
                                            ins = t.matmul(psum_d[gp * 4 + oc][:],
                                                           wdb[:, (n % 4) * 1024 + sub * 512 + oc * 128:
                                                               (n % 4) * 1024 + sub * 512 + (oc + 1) * 128],
                                                           hbuf[:, ic * 512:(ic + 1) * 512],
                                                           start=(ic == 0), stop=(ic == IC - 1))
                                    ins.then_inc(d_mm, 1)

                        @block.vector
                        def _(v):
                            for og in range(8):
                                gp = og % 2
                                v.wait_ge(d_mm, (tb * 8 + og + 1) * 43)
                                for oc in range(4):
                                    ocf = og * 4 + oc
                                    tgt = x_sb[:, ocf * TOK + lo: ocf * TOK + lo + 512]
                                    v.tensor_tensor(tgt, psum_d[gp * 4 + oc][:], tgt, ALU.add).then_inc(d_ev, 1)

                schb.close()
                emit_ln(f"P5c{tb}", lo, 512, True, l2_sq, l2_pe, l2_ev, l2_dv, l2_bpe, l2_bev,
                        l2_n, tb, store_sem=s_out, tbi=tb,
                        prefetch=(ffn_prefetch(1) if tb == 0 else None))

    return nc


def _prep_weights(inputs):
    f32 = np.float32
    wq = np.asarray(inputs["L0_attn_q_weight"], f32)
    wk = np.asarray(inputs["L0_attn_k_weight"], f32)
    wv = np.asarray(inputs["L0_attn_v_weight"], f32)
    wo = np.asarray(inputs["L0_attn_o_weight"], f32)
    wg = np.asarray(inputs["L0_ffn_gate_w"], np.int32)
    wu = np.asarray(inputs["L0_ffn_up_w"], np.int32)
    wd = np.asarray(inputs["L0_ffn_down_w"], np.int32)

    d = {}
    d["wq_t"] = np.ascontiguousarray(
        wq.reshape(NH, 128, DC, 128).transpose(0, 3, 2, 1)).reshape(NH, 128, DC * 128)
    d["wk_t"] = np.ascontiguousarray(
        wk.reshape(NG, 128, DC, 128).transpose(0, 3, 2, 1)).reshape(NG, 128, DC * 128)
    d["wv_t"] = np.ascontiguousarray(wv.T).reshape(DC, 128, KV)
    d["wo_t"] = np.ascontiguousarray(
        wo.reshape(DC, 128, DC, 128).transpose(0, 3, 2, 1)).reshape(DC, 128, DC * 128)
    d["wg_t"] = np.ascontiguousarray(
        wg.reshape(IC, 128, DC, 128).transpose(0, 3, 2, 1)).reshape(IC, 128, DC * 128)
    d["wu_t"] = np.ascontiguousarray(
        wu.reshape(IC, 128, DC, 128).transpose(0, 3, 2, 1)).reshape(IC, 128, DC * 128)
    d["wd_t"] = np.ascontiguousarray(
        wd.reshape(8, 4, 128, IC, 128).transpose(0, 3, 4, 1, 2)).reshape(8, IC, 128, 512)
    d["qsc_r"] = np.ascontiguousarray(np.asarray(inputs["L0_attn_q_scale"], f32).reshape(1, D))
    d["ksc_r"] = np.ascontiguousarray(np.asarray(inputs["L0_attn_k_scale"], f32).reshape(1, KV))
    d["vsc_r"] = np.ascontiguousarray(np.asarray(inputs["L0_attn_v_scale"], f32).reshape(1, KV))
    d["osc_r"] = np.ascontiguousarray(np.asarray(inputs["L0_attn_o_scale"], f32).reshape(1, D))
    d["gs_d"] = np.asarray(inputs["L0_ffn_gate_s"], f32).reshape(1, 1)
    d["us_d"] = np.asarray(inputs["L0_ffn_up_s"], f32).reshape(1, 1)
    d["ds_d"] = np.asarray(inputs["L0_ffn_down_s"], f32).reshape(1, 1)
    return d


def kernel(**inputs):
    x = np.asarray(inputs["L0_attn_hidden_states"], np.float32)
    shared = _prep_weights(inputs)

    nc = build_nc(debug=DEBUG_DUMPS)

    in_maps = []
    for c in range(8):
        b, hh = c // 2, c % 2
        x_own = x[b, hh * TOK:(hh + 1) * TOK, :]
        x_oth = x[b, (1 - hh) * TOK:(2 - hh) * TOK, :]
        m = dict(shared)
        m["xo_t"] = np.ascontiguousarray(x_own.T).reshape(DC, 128, TOK)
        m["xr_t"] = np.ascontiguousarray(x_oth.T).reshape(DC, 128, TOK)
        in_maps.append(m)

    res = run_bass_kernel_spmd(nc, in_maps, core_ids=list(range(8)))

    y = np.empty((4, SEQ, D), np.float32)
    for c in range(8):
        b, hh = c // 2, c % 2
        o = np.asarray(res.results[c]["out"])
        y[b, hh * TOK:(hh + 1) * TOK, :] = o.reshape(D, TOK).T
    return y


if __name__ == "__main__":
    import importlib.util
    spec = importlib.util.spec_from_file_location("_r", "/root/problem/reference.py")
    _r = importlib.util.module_from_spec(spec)
    spec.loader.exec_module(_r)
    ins = _r.setup_inputs()
    got = kernel(**{k: np.asarray(v) for k, v in ins.items()})
    want = np.asarray(_r.reference(**ins))
    err = got - want
    rv = float(np.mean(err ** 2) / np.mean(want ** 2))
    mx = float(np.abs(got.astype(ml_dtypes.bfloat16).astype(np.float32)
                      - want.astype(ml_dtypes.bfloat16).astype(np.float32)).max())
    print("residual variance ratio:", rv, " max_abs_bf16:", mx)
